# version 01
# baseline (speedup 1.0000x reference)
import contextlib
import math

import numpy as np
import ml_dtypes

import concourse.bass as bass
import concourse.mybir as mybir
from concourse.bass_utils import run_bass_kernel_spmd

F32 = mybir.dt.float32
BF16 = mybir.dt.bfloat16
U8 = mybir.dt.uint8
AF = mybir.ActivationFunctionType
ALU = mybir.AluOpType
AX = mybir.AxisListType

T = 2048
D = 1024
NT = 16
EPS = 1e-5

ARENA_KB = 196


ENGINES = ("tensor", "vector", "scalar", "sync", "gpsimd")


class Buf:

    def __init__(self, name, off, nbytes):
        self.name = name
        self.off = off
        self.nbytes = nbytes
        self.ghosts = []

    def __repr__(self):
        return f"Buf({self.name}@{self.off}:{self.off+self.nbytes})"


class Arena:
    def __init__(self, nbytes):
        self.nbytes = nbytes
        self.free = [(0, nbytes)]
        self.dead = []
        self.live = {}
        self.peak = 0

    def alloc(self, name, nbytes, align=128, side="auto"):
        nbytes = (nbytes + align - 1) // align * align
        if side == "auto":
            side = "top" if nbytes <= 8 * 1024 else "bottom"
        order = (range(len(self.free)) if side == "bottom"
                 else range(len(self.free) - 1, -1, -1))
        for i in order:
            off, size = self.free[i]
            if side == "top":
                a_off = (off + size - nbytes) // align * align
                if a_off < off:
                    continue
            else:
                a_off = (off + align - 1) // align * align
            if a_off + nbytes <= off + size:
                pre = a_off - off
                post = (off + size) - (a_off + nbytes)
                repl = []
                if pre:
                    repl.append((off, pre))
                if post:
                    repl.append((a_off + nbytes, post))
                self.free[i:i + 1] = repl
                buf = Buf(name, a_off, nbytes)
                keep = []
                for (doff, dsize, tok) in self.dead:
                    if doff < a_off + nbytes and a_off < doff + dsize:
                        buf.ghosts.append(tok)
                        keep.append((doff, dsize, tok))
                    else:
                        keep.append((doff, dsize, tok))
                self.dead = keep
                self.live[name] = buf
                for ob in self.live.values():
                    if ob is not buf and ob.off < buf.off + buf.nbytes and buf.off < ob.off + ob.nbytes:
                        raise AssertionError(f"alloc overlap: {buf} vs {ob}")
                used = self.nbytes - sum(s for _, s in self.free)
                self.peak = max(self.peak, used)
                return buf
        raise MemoryError(
            f"arena: cannot alloc {name} ({nbytes}B); free={self.free}")

    def freebuf(self, buf, tokens=None):
        assert self.live.pop(buf.name) is buf
        for tok in (tokens if tokens is not None else [buf.name]):
            self.dead.append((buf.off, buf.nbytes, tok))
        self.free.append((buf.off, buf.nbytes))
        self.free.sort()
        merged = []
        for off, size in self.free:
            if merged and merged[-1][0] + merged[-1][1] == off:
                merged[-1] = (merged[-1][0], merged[-1][1] + size)
            else:
                merged.append((off, size))
        self.free = [tuple(x) for x in merged]


PARANOID = [False]


class Prog:

    def __init__(self):
        self.ops = {e: [] for e in ENGINES}
        self.count = {e: 0 for e in ENGINES}
        self.last_write = {}
        self.readers = {}
        self.guaranteed = {e: {f: 0 for f in ENGINES} for e in ENGINES}
        self.arena = Arena(ARENA_KB * 1024)
        self.ngroups = 0
        self.alloc_log = []
        self.group_insts = {}

    def alloc(self, name, nbytes, align=128, side="auto"):
        buf = self.arena.alloc(name, nbytes, align, side)
        self.alloc_log.append((self.ngroups, "A", name, buf.off, buf.nbytes))
        return buf

    def free(self, buf):
        self.alloc_log.append((self.ngroups, "F", buf.name, buf.off, buf.nbytes))
        pref = buf.name + "."
        toks = set([buf.name])
        for tok in list(self.last_write) + list(self.readers):
            if tok.startswith(pref):
                toks.add(tok)
        self.arena.freebuf(buf, sorted(toks))

    def group(self, engine, emit, reads=(), writes=(), ptr_reads=(), n_dma=0):
        def owner_of(tok):
            live = self.arena.live
            parts = tok.split(".")
            for i in range(len(parts), 0, -1):
                cand = ".".join(parts[:i])
                if cand in live:
                    return live[cand]
            return None

        def toks(xs, consume_ghosts=False):
            out = []
            ghost_owners = []
            for x in xs:
                if isinstance(x, Buf):
                    out.append(x.name)
                    buf = x
                else:
                    out.append(x)
                    buf = owner_of(x) if consume_ghosts else None
                if consume_ghosts and buf is not None and buf.ghosts:
                    out.extend(buf.ghosts)
                    ghost_owners.append(buf)
            for buf in ghost_owners:
                buf.ghosts = []
            return out

        rtoks = toks(reads)
        wtoks = toks(writes, consume_ghosts=True)
        ptoks = toks(ptr_reads)

        needs_w = {}
        needs_r = {}

        def need(d, e, c):
            if c > d.get(e, 0):
                d[e] = c

        if PARANOID[0]:
            for e in ENGINES:
                if e != engine and self.count[e] > 0:
                    need(needs_w, e, self.count[e])

        for tok in rtoks + ptoks:
            lw = self.last_write.get(tok)
            if lw:
                need(needs_w, *lw)
        for tok in wtoks:
            lw = self.last_write.get(tok)
            if lw:
                need(needs_w, *lw)
            for e, c in self.readers.get(tok, {}).items():
                need(needs_r, e, c)

        needs = dict(needs_r)
        for e, c in needs_w.items():
            if c > needs.get(e, 0):
                needs[e] = c
        waits = []
        for e, c in sorted(needs.items()):
            if e == engine:
                raww = needs_w.get(e, 0)
                if engine in ("tensor", "sync", "gpsimd"):
                    continue
                if raww == 0:
                    continue
                c = raww
                if self.guaranteed[engine].get(e, 0) >= c:
                    continue
            elif self.guaranteed[engine].get(e, 0) >= c:
                continue
            waits.append((e, c))
            self.guaranteed[engine][e] = c

        inc = 16 * n_dma if n_dma else 1
        newc = self.count[engine] + inc
        self.count[engine] = newc
        self.ops[engine].append((waits, emit, inc, n_dma))
        self.ngroups += 1

        for tok in wtoks:
            self.last_write[tok] = (engine, newc)
            self.readers.pop(tok, None)
        for tok in rtoks + ptoks:
            self.readers.setdefault(tok, {})[engine] = newc

    def g(self, engine, reads=(), writes=(), ptr_reads=()):
        def deco(fn):
            self.group(engine, fn, reads, writes, ptr_reads)
        return deco

    def dma(self, queue, emit, reads=(), writes=(), n=1):
        self.group(queue, emit, reads, writes, n_dma=n)

    def emit_all(self, nc, sems):
        for engine in ENGINES:
            plan = self.ops[engine]
            eng = getattr(nc, engine)

            def run(eng, plan=plan, engine=engine):
                for waits, emit, inc, n_dma in plan:
                    for (we, wc) in waits:
                        eng.wait_ge(sems[we], wc)
                    if n_dma:
                        insts = emit(eng)
                        if not isinstance(insts, (list, tuple)):
                            insts = [insts]
                        assert len(insts) == n_dma, (engine, len(insts), n_dma)
                        for ins in insts:
                            ins.then_inc(sems[engine], 16)
                    else:
                        last = emit(eng)
                        assert last is not None, f"group on {engine} returned None"
                        last.then_inc(sems[engine], 1)
            yield engine, run



def host_constants():
    n, dh, window = T, 64, 512
    half = np.arange(0, dh, 2, dtype=np.float64)
    inv = 1.0 / 10000.0 ** (half / dh)
    t = np.arange(n, dtype=np.float64)
    fr = t[:, None] * inv[None, :]
    base = (half + 0.4 * dh) / (1.4 * dh)
    power = (t - n // 2) / float(window // 2)
    sc = base[None, :] ** power[:, None]
    cos = np.cos(fr)
    sin = np.sin(fr)
    cos64 = np.concatenate([cos, cos], axis=1)
    sin64 = np.concatenate([sin, sin], axis=1)
    sc64 = np.concatenate([sc, sc], axis=1)
    qa = (cos64 * sc64).T
    qb = (sin64 * sc64).T
    ka = (cos64 / sc64).T
    kb = (sin64 / sc64).T
    rep = lambda a: np.concatenate([a, a], axis=0).astype(np.float32)
    consts = {
        "C_ROTQA": rep(qa), "C_ROTQB": rep(qb),
        "C_ROTKA": rep(ka), "C_ROTKB": rep(kb),
    }
    consts["C_ID"] = np.eye(128, dtype=ml_dtypes.bfloat16)
    M = np.zeros((64, 64), np.float32)
    for dp in range(32):
        M[dp, dp + 32] = -1.0
    for dp in range(32, 64):
        M[dp, dp - 32] = 1.0
    M128 = np.zeros((128, 128), np.float32)
    M128[0:64, 0:64] = M
    M128[64:128, 64:128] = M
    consts["C_ROTM"] = M128.T.astype(ml_dtypes.bfloat16)
    i = np.arange(128)
    own = np.where(i[None, :] >= i[:, None], 0.0, -1e30).astype(np.float32)
    prev = np.where(i[None, :] <= i[:, None], 0.0, -1e30).astype(np.float32)
    consts["C_MASK_OWN"] = own.astype(ml_dtypes.bfloat16)
    consts["C_MASK_PREV"] = prev.astype(ml_dtypes.bfloat16)
    blk2 = np.zeros((2, 128), np.float32)
    blk2[0, 0:64] = 1.0
    blk2[1, 64:128] = 1.0
    consts["C_BLK2"] = blk2
    consts["C_ONES1"] = np.ones((1, 128), dtype=ml_dtypes.bfloat16)
    return consts


CONST_SPECS = {
    "C_ROTQA": ([128, 2048], F32), "C_ROTQB": ([128, 2048], F32),
    "C_ROTKA": ([128, 2048], F32), "C_ROTKB": ([128, 2048], F32),
    "C_ID": ([128, 128], BF16), "C_ROTM": ([128, 128], BF16),
    "C_MASK_OWN": ([128, 128], BF16), "C_MASK_PREV": ([128, 128], BF16),
    "C_BLK2": ([2, 128], F32), "C_ONES1": ([1, 128], BF16),
}

INPUT_SPECS = {
    "L0_attn_Wq": ([1024, 1024], F32), "L0_attn_bq": ([1024], F32),
    "L0_attn_Wk": ([1024, 1024], F32), "L0_attn_bk": ([1024], F32),
    "L0_attn_Wv": ([512, 1024], F32), "L0_attn_bv": ([512], F32),
    "L0_attn_Wqv": ([512, 1024], F32), "L0_attn_bqv": ([512], F32),
    "L0_attn_Ws": ([1, 96], F32), "L0_attn_bs": ([1], F32),
    "L0_attn_Wf": ([1024, 1024], F32), "L0_attn_bf": ([1024], F32),
    "L0_attn_Wr1": ([2048, 1024], F32), "L0_attn_br1": ([2048], F32),
    "L0_attn_Wr2": ([1024, 2048], F32), "L0_attn_br2": ([1024], F32),
    "L0_attn_g1": ([1024], F32), "L0_attn_b1": ([1024], F32),
    "L0_attn_g2": ([1024], F32), "L0_attn_b2": ([1024], F32),
    "L0_ffn_wqkv": ([2, 1024, 1536], F32),
    "L0_ffn_q_scale": ([2, 64], F32), "L0_ffn_k_scale": ([2, 64], F32),
    "L0_ffn_wo": ([2, 512, 1024], F32),
    "L0_ffn_ln_g": ([2, 1024], F32), "L0_ffn_ln_b": ([2, 1024], F32),
    "L0_ffn_w1": ([2, 1024, 5460], F32), "L0_ffn_b1": ([2, 5460], F32),
    "L0_ffn_w2": ([2, 2730, 1024], F32), "L0_ffn_b2": ([2, 1024], F32),
}



class Kern:
    def __init__(self, debug_taps=(), detect_races=True):
        self.nc = bass.Bass("TRN2", target_bir_lowering=False,
                            detect_race_conditions=detect_races)
        nc = self.nc
        self.p = Prog()
        self.es = contextlib.ExitStack()
        self.debug_taps = list(debug_taps)
        self.taps_done = []

        self.x_in = nc.dram_tensor("x", [T, D], F32, kind="ExternalInput").ap()
        self.out = nc.dram_tensor("out", [T, D], F32, kind="ExternalOutput").ap()
        self.inp = {}
        for name, (shape, dt) in INPUT_SPECS.items():
            self.inp[name] = nc.dram_tensor(name, shape, dt, kind="ExternalInput").ap()
        self.cst = {}
        for name, (shape, dt) in CONST_SPECS.items():
            self.cst[name] = nc.dram_tensor(name, shape, dt, kind="ExternalInput").ap()
        self.sp_xa = nc.dram_tensor("sp_xa", [T, D], F32).ap()
        self.sp_xb = nc.dram_tensor("sp_xb", [T, D], F32).ap()

        self.arena_t = self.es.enter_context(
            nc.sbuf_tensor("arena", [128, ARENA_KB * 1024], U8))
        self.ps = [self.es.enter_context(nc.psum_tensor(f"ps{i}", [128, 1024], F32))
                   for i in range(3)]
        self.ps3a = self.es.enter_context(nc.psum_tensor("ps3a", [128, 512], F32))
        self.ps3t = self.es.enter_context(nc.psum_tensor("ps3t", [128, 1024], BF16))
        self.sems = {e: self.es.enter_context(nc.semaphore(f"sem_{e}"))
                     for e in ENGINES}

        self.dbg_cnt = 0

    def f32(self, buf, cols=None, off_el=0):
        ap = self.arena_t[:, buf.off + off_el * 4:
                          buf.off + (off_el + (cols or (buf.nbytes // 4 - off_el))) * 4]
        return ap.bitcast(F32)

    def bf(self, buf, cols=None, off_el=0):
        ap = self.arena_t[:, buf.off + off_el * 2:
                          buf.off + (off_el + (cols or (buf.nbytes // 2 - off_el))) * 2]
        return ap.bitcast(BF16)

    def pstok(self, i, half):
        return f"ps{i}{'ab'[half]}"

    def tap_f32(self, name, get_ap, rows, cols):
        if name not in self.debug_taps:
            return
        o = self.nc.dram_tensor(f"tap_{name}", [rows, cols], F32,
                                kind="ExternalOutput").ap()
        ap, reads = get_ap()
        self.p.dma("sync", lambda eng: eng.dma_start(out=o, in_=ap),
                   reads=reads, n=1)
        self.taps_done.append(name)

    def tap_bf16(self, name, get_ap, rows, cols):
        if name not in self.debug_taps:
            return
        o = self.nc.dram_tensor(f"tap_{name}", [rows, cols], BF16,
                                kind="ExternalOutput").ap()
        ap, reads = get_ap()
        self.p.dma("sync", lambda eng: eng.dma_start(out=o, in_=ap),
                   reads=reads, n=1)
        self.taps_done.append(name)


def ceil_div(a, b):
    return (a + b - 1) // b




def bpack_layout():
    entries = [
        ("bq", 8), ("bk", 8), ("bqv", 4), ("br1", 16),
        ("g1", 8), ("b1", 8), ("g2", 8), ("b2", 8),
        ("lng.0", 8), ("lnb.0", 8), ("lng.1", 8), ("lnb.1", 8),
        ("b1a.0", 22), ("b1g.0", 22), ("b1a.1", 22), ("b1g.1", 22),
        ("qs.0", 1), ("qssw.0", 1), ("ks.0", 1), ("kssw.0", 1),
        ("qs.1", 1), ("qssw.1", 1), ("ks.1", 1), ("kssw.1", 1),
        ("bs", 1), ("c_eps", 1), ("c_eps2", 1),
    ]
    m, c = {}, 0
    for name, n in entries:
        m[name] = (c, n)
        c += n
    return m, c


def bbc_layout():
    entries = [("bv", 512), ("bf", 1024), ("br2", 1024),
               ("b2f.0", 1024), ("b2f.1", 1024), ("wsout", 512)]
    m, c = {}, 0
    for name, n in entries:
        m[name] = (c, n)
        c += n
    return m, c


BPACK_MAP, BPACK_COLS = bpack_layout()
BBC_MAP, BBC_COLS = bbc_layout()


def host_packs(inp):
    def cols(vec, n):
        v = np.zeros((128, n), np.float32)
        flat = np.asarray(vec, np.float32).reshape(-1)
        for c in range(n):
            seg = flat[c * 128:(c + 1) * 128]
            v[:len(seg), c] = seg
        return v

    bp = np.zeros((128, BPACK_COLS), np.float32)

    def put(name, arr):
        c0, n = BPACK_MAP[name]
        bp[:, c0:c0 + n] = arr

    put("bq", cols(inp["L0_attn_bq"], 8))
    put("bk", cols(inp["L0_attn_bk"], 8))
    put("bqv", cols(inp["L0_attn_bqv"], 4))
    put("br1", cols(inp["L0_attn_br1"], 16))
    put("g1", cols(inp["L0_attn_g1"], 8))
    put("b1", cols(inp["L0_attn_b1"], 8))
    put("g2", cols(inp["L0_attn_g2"], 8))
    put("b2", cols(inp["L0_attn_b2"], 8))
    for l in range(2):
        put(f"lng.{l}", cols(inp["L0_ffn_ln_g"][l], 8))
        put(f"lnb.{l}", cols(inp["L0_ffn_ln_b"][l], 8))
        put(f"b1a.{l}", cols(inp["L0_ffn_b1"][l][0:2730], 22))
        put(f"b1g.{l}", cols(inp["L0_ffn_b1"][l][2730:5460], 22))
        qs = np.asarray(inp["L0_ffn_q_scale"][l], np.float32)
        ks = np.asarray(inp["L0_ffn_k_scale"][l], np.float32)
        put(f"qs.{l}", np.concatenate([qs, qs]).reshape(128, 1))
        put(f"qssw.{l}", np.concatenate([qs[32:], qs[:32], qs[32:], qs[:32]]).reshape(128, 1))
        put(f"ks.{l}", np.concatenate([ks, ks]).reshape(128, 1))
        put(f"kssw.{l}", np.concatenate([ks[32:], ks[:32], ks[32:], ks[:32]]).reshape(128, 1))
    put("bs", np.broadcast_to(np.asarray(inp["L0_attn_bs"], np.float32), (128, 1)))
    put("c_eps", np.full((128, 1), EPS, np.float32))
    put("c_eps2", np.full((128, 1), EPS * EPS, np.float32))

    bbc = np.zeros((128, BBC_COLS), np.float32)

    def putb(name, vec):
        c0, n = BBC_MAP[name]
        bbc[:, c0:c0 + n] = np.broadcast_to(np.asarray(vec, np.float32).reshape(1, n),
                                            (128, n))

    putb("bv", inp["L0_attn_bv"])
    putb("bf", inp["L0_attn_bf"])
    putb("br2", inp["L0_attn_br2"])
    putb("b2f.0", inp["L0_ffn_b2"][0])
    putb("b2f.1", inp["L0_ffn_b2"][1])
    ws = np.asarray(inp["L0_attn_Ws"], np.float32).reshape(96)
    putb("wsout", np.concatenate([ws[32:96]] * 8))

    wsblk = np.zeros((128, 4), np.float32)
    for m in range(4):
        wsblk[32 * m:32 * m + 32, m] = ws[0:32]
    return bp, bbc, wsblk.astype(ml_dtypes.bfloat16)


PACK_SPECS = {
    "BPACK": ([128, BPACK_COLS], F32),
    "BBC": ([128, BBC_COLS], F32),
    "BPACKBF": ([128, 4], BF16),
}



class Builder(Kern):
    def __init__(self, debug_taps=(), detect_races=True):
        super().__init__(debug_taps, detect_races)
        for name, (shape, dt) in PACK_SPECS.items():
            self.inp[name] = self.nc.dram_tensor(name, shape, dt,
                                                 kind="ExternalInput").ap()
        self._stat_used = 0
        self._stat_map = {}

    def stat_cols(self, name, n):
        if name in self._stat_map:
            return self._stat_map[name][0]
        c = self._stat_used
        assert c + n <= self.STAT.nbytes // 4, f"STAT overflow at {name}"
        self._stat_map[name] = (c, n)
        self._stat_used += n
        return c

    def sv(self, name, col=0, n=None):
        base, cnt = self._stat_map[name]
        if n is None:
            n = cnt - col
        return self.f32(self.STAT, n, base + col)

    def bcol(self, key, chunk=0, n=1):
        c0, cnt = BPACK_MAP[key]
        return self.f32(self.BPACK_B, n, c0 + chunk)

    def load_bbc(self, key):
        p = self.p
        c0, cnt = BBC_MAP[key]
        self._bbc_n += 1
        buf = p.alloc(f"BBC.{key}.{self._bbc_n}", cnt * 4)
        p.dma("sync", lambda eng, buf=buf, c0=c0, cnt=cnt: eng.dma_start(
            out=self.f32(buf, cnt), in_=self.inp["BBC"][:, c0:c0 + cnt]),
            writes=[buf], n=1)
        return buf

    def bbc_view(self, buf, off=0, n=None):
        if n is None:
            n = buf.nbytes // 4 - off
        return self.f32(buf, n, off)

    def build(self):
        p = self.p
        self.XA = p.alloc("XA", T * D * 4 // 128)
        self.CONST = p.alloc("CONST", 2 * 1024, side="top")
        self.STAT = p.alloc("STAT", 16 * 1024, side="top")
        self.BPACK_B = p.alloc("BPACK", BPACK_COLS * 4, side="top")
        self.WSBLK = p.alloc("WSBLK", 4 * 2)
        self._bbc_n = 0

        self.load_input_x()
        self.load_consts()
        p.dma("sync", lambda eng: [
            eng.dma_start(out=self.f32(self.BPACK_B, BPACK_COLS), in_=self.inp["BPACK"]),
            eng.dma_start(out=self.bf(self.WSBLK, 4), in_=self.inp["BPACKBF"]),
        ], writes=[self.BPACK_B, self.WSBLK, "bpack"], n=2)

        xcur = self.XA
        xb = p.alloc("XB1", T * D * 4 // 128)
        self.ln_plain(xcur, xb, "ln0")
        self.tap("ln0", xb, "tok_f32")
        self.spill(xcur, self.sp_xa, "spA")
        p.free(xcur)
        xb = self.ffn_sublayer(xb, "F1")
        xcur = self.reload_add_half(self.sp_xa, "spA", xb, "mac1")
        self.tap("x1", xcur, "tok_f32")
        if self.stop_after == "A":
            self.store_output(xcur); return
        xcur = self.attn_sublayer(xcur)
        if self.stop_after == "B":
            self.store_output(xcur); return
        xb = p.alloc("XB2", T * D * 4 // 128)
        self.ln_plain(xcur, xb, "ln2")
        self.spill(xcur, self.sp_xa, "spC")
        p.free(xcur)
        xb = self.ffn_sublayer(xb, "F2")
        xcur = self.reload_add_half(self.sp_xa, "spC", xb, "mac2")
        self.store_output(xcur)

    stop_after = ""

    def load_input_x(self):
        p = self.p
        dst = self.f32(self.XA).rearrange("p (t d) -> p t d", t=NT)
        src = self.x_in.rearrange("(t p) d -> p t d", p=128)
        p.dma("sync", lambda eng: eng.dma_start(out=dst, in_=src),
              writes=[self.XA] + [f"XA.{t}" for t in range(NT)], n=1)

    def load_consts(self):
        p = self.p
        off = 0
        self.c_off = {}
        plan = [("C_ID", 128 * 2), ("C_ROTM", 128 * 2), ("C_MASK_OWN", 128 * 2),
                ("C_MASK_PREV", 128 * 2), ("C_ONES1", 128 * 2), ("C_BLK2", 128 * 4)]
        for name, nb in plan:
            self.c_off[name] = off
            off += nb
        assert off <= self.CONST.nbytes

        def emit(eng):
            outs = []
            for name, nb in plan:
                shape, dt = CONST_SPECS[name]
                v = self.arena_t[0:shape[0],
                                 self.CONST.off + self.c_off[name]:
                                 self.CONST.off + self.c_off[name] + nb]
                outs.append(eng.dma_start(out=v.bitcast(dt), in_=self.cst[name]))
            return outs
        p.dma("sync", emit, writes=[self.CONST, "consts"], n=len(plan))

    def cview(self, name, rows=None):
        shape, dt = CONST_SPECS[name]
        nb = shape[1] * (2 if dt == BF16 else 4)
        v = self.arena_t[0:(rows or shape[0]),
                         self.CONST.off + self.c_off[name]:
                         self.CONST.off + self.c_off[name] + nb]
        return v.bitcast(dt)

    def ln_stats_plain(self, src, tag):
        p = self.p
        sq = p.alloc(f"sq.{tag}", 1024 * 4)
        self.stat_cols(f"s.{tag}", 16)
        self.stat_cols(f"q.{tag}", 16)
        for t in range(NT):
            xt = self.f32(src, 1024, t * 1024)

            @p.g("scalar", reads=[f"{src.name}.{t}"], writes=[sq, f"st.q.{tag}"])
            def _(eng, xt=xt, t=t):
                return eng.activation(self.f32(sq, 1024), xt, AF.Square,
                                      accum_out=self.sv(f"q.{tag}", t, 1))

            @p.g("vector", reads=[f"{src.name}.{t}"], writes=[f"st.s.{tag}"])
            def _(eng, xt=xt, t=t):
                return eng.reduce_sum(self.sv(f"s.{tag}", t, 1), xt, axis=AX.X)
        p.free(sq)

    def ln_finalize(self, tag, extra_r2=False, n=16):
        p = self.p
        self.stat_cols(f"m.{tag}", n)
        self.stat_cols(f"v.{tag}", n)
        self.stat_cols(f"r.{tag}", n)
        self.stat_cols(f"negmr.{tag}", n)
        inv = 1.0 / D

        @p.g("vector", reads=[f"st.s.{tag}"], writes=[f"st.m.{tag}"])
        def _(eng):
            return eng.tensor_scalar(self.sv(f"m.{tag}"), self.sv(f"s.{tag}"),
                                     inv, None, ALU.mult)

        @p.g("vector", reads=[f"st.m.{tag}"], writes=[f"st.v.{tag}"])
        def _(eng):
            return eng.tensor_mul(self.sv(f"v.{tag}"), self.sv(f"m.{tag}"),
                                  self.sv(f"m.{tag}"))

        @p.g("vector", reads=[f"st.q.{tag}", f"st.v.{tag}"], writes=[f"st.v.{tag}"])
        def _(eng):
            return eng.scalar_tensor_tensor(self.sv(f"v.{tag}"), self.sv(f"q.{tag}"),
                                            inv, self.sv(f"v.{tag}"),
                                            ALU.mult, ALU.subtract)

        @p.g("scalar", reads=[f"st.v.{tag}", "bpack"], writes=[f"st.r.{tag}"],
             ptr_reads=["bpack"])
        def _(eng):
            return eng.activation(self.sv(f"r.{tag}"), self.sv(f"v.{tag}"),
                                  AF.Ln, bias=self.bcol("c_eps"))

        @p.g("scalar", reads=[f"st.r.{tag}"], writes=[f"st.r.{tag}"])
        def _(eng):
            return eng.activation(self.sv(f"r.{tag}"), self.sv(f"r.{tag}"),
                                  AF.Exp, scale=-0.5)

        @p.g("vector", reads=[f"st.r.{tag}", f"st.m.{tag}"], writes=[f"st.negmr.{tag}"])
        def _(eng):
            return eng.scalar_tensor_tensor(self.sv(f"negmr.{tag}"),
                                            self.sv(f"m.{tag}"), -1.0,
                                            self.sv(f"r.{tag}"), ALU.mult, ALU.mult)
        if extra_r2:
            self.stat_cols(f"r2.{tag}", n)
            self.stat_cols(f"negmr2.{tag}", n)

            @p.g("scalar", reads=[f"st.v.{tag}", "bpack"], writes=[f"st.r2.{tag}"],
                 ptr_reads=["bpack"])
            def _(eng):
                return eng.activation(self.sv(f"r2.{tag}"), self.sv(f"v.{tag}"),
                                      AF.Ln, scale=1.0 + EPS, bias=self.bcol("c_eps2"))

            @p.g("scalar", reads=[f"st.r2.{tag}"], writes=[f"st.r2.{tag}"])
            def _(eng):
                return eng.activation(self.sv(f"r2.{tag}"), self.sv(f"r2.{tag}"),
                                      AF.Exp, scale=-0.5)

            @p.g("vector", reads=[f"st.r2.{tag}", f"st.m.{tag}"],
                 writes=[f"st.negmr2.{tag}"])
            def _(eng):
                return eng.scalar_tensor_tensor(self.sv(f"negmr2.{tag}"),
                                                self.sv(f"m.{tag}"), -1.0,
                                                self.sv(f"r2.{tag}"),
                                                ALU.mult, ALU.mult)

    def ln_plain(self, src, dst, tag):
        p = self.p
        self.ln_stats_plain(src, tag)
        self.ln_finalize(tag)
        for t in range(NT):
            xt = self.f32(src, 1024, t * 1024)
            yt = self.f32(dst, 1024, t * 1024)

            @p.g("scalar", reads=[f"{src.name}.{t}"], writes=[f"{dst.name}.{t}"],
                 ptr_reads=[f"st.r.{tag}", f"st.negmr.{tag}"])
            def _(eng, xt=xt, yt=yt, t=t):
                return eng.activation(yt, xt, AF.Identity,
                                      scale=self.sv(f"r.{tag}", t, 1),
                                      bias=self.sv(f"negmr.{tag}", t, 1))

    def spill(self, buf, dram, tag):
        p = self.p
        src = self.f32(buf).rearrange("p (t d) -> p t d", t=NT)
        dst = dram.rearrange("(t p) d -> p t d", p=128)
        p.dma("sync", lambda eng: eng.dma_start(out=dst, in_=src),
              reads=[buf] + [f"{buf.name}.{t}" for t in range(NT)],
              writes=[f"dram.{tag}"], n=1)

    def reload_add_half(self, dram, dtag, other, tag, scale=0.5):
        p = self.p
        stage = p.alloc(f"rl.{tag}", 2 * 1024 * 4)
        for t in range(NT):
            half = t % 2
            sview = self.f32(stage, 1024, half * 1024)
            src = dram.rearrange("(t p) d -> p t d", p=128)[:, t]
            p.dma("sync",
                  lambda eng, sv_=sview, src=src: eng.dma_start(out=sv_, in_=src),
                  reads=[f"dram.{dtag}", f"rl.{tag}.{half}"],
                  writes=[f"rl.{tag}.w{half}"], n=1)
            ot = self.f32(other, 1024, t * 1024)

            @p.g("vector", reads=[f"rl.{tag}.w{half}", f"{other.name}.{t}"],
                 writes=[f"{other.name}.{t}", f"rl.{tag}.{half}"])
            def _(eng, ot=ot, sv_=sview, scale=scale):
                return eng.scalar_tensor_tensor(ot, ot, scale, sv_, ALU.mult, ALU.add)
        p.free(stage)
        return other

    def store_output(self, buf):
        p = self.p
        src = self.f32(buf).rearrange("p (t d) -> p t d", t=NT)
        dst = self.out.rearrange("(t p) d -> p t d", p=128)
        p.dma("sync", lambda eng: eng.dma_start(out=dst, in_=src),
              reads=[buf] + [f"{buf.name}.{t}" for t in range(NT)],
              writes=["output"], n=1)
        p.ops["sync"].append(([("sync", p.count["sync"])], None, 0, 0))

    def transpose_tokmajor(self, srcbuf, src_cols, dstbuf, n_ftiles, tag,
                           shift_ftiles=(), evac="vector", affine=None,
                           src_tile_cols=None):
        p = self.p
        stc = src_tile_cols if src_tile_cols is not None else src_cols
        for ft in range(n_ftiles):
            shifted = ft in shift_ftiles
            if shifted:
                @p.g("vector", writes=[f"{dstbuf.name}.{ft}"])
                def _(eng, ft=ft):
                    return eng.memset(self.bf(dstbuf, 1, ft * 2048), 0.0)
            for grp in range(4):
                slots = []
                for s in range(4):
                    tc = grp * 4 + s
                    chunk = self.bf(srcbuf, 128, tc * stc + ft * 128)
                    dstp = self.ps3t[:, s * 128:(s + 1) * 128]
                    slots.append((chunk, dstp, tc))

                @p.g("tensor",
                     reads=[f"{srcbuf.name}.{tc}" for _, _, tc in slots] + ["consts"],
                     writes=["ps3t"])
                def _(eng, slots=slots):
                    last = None
                    for chunk, dstp, tc in slots:
                        last = eng.transpose(dstp, chunk, self.cview("C_ID"))
                    return last

                base = ft * 2048 + grp * 512
                if shifted:
                    width = 512 if grp < 3 else 511
                    dst = self.bf(dstbuf, width, base + 1)
                else:
                    width = 512
                    dst = self.bf(dstbuf, width, base)
                src_ps = self.ps3t[:, 0:width]
                if affine is not None:
                    sc, bi = affine

                    @p.g("scalar", reads=["ps3t", "bpack"],
                         writes=[f"{dstbuf.name}.{ft}"],
                         ptr_reads=[])
                    def _(eng, dst=dst, src_ps=src_ps, ft=ft, sc=sc, bi=bi):
                        return eng.activation(dst, src_ps, AF.Identity,
                                              scale=sc(ft), bias=bi(ft))
                else:
                    @p.g(evac, reads=["ps3t"], writes=[f"{dstbuf.name}.{ft}"])
                    def _(eng, dst=dst, src_ps=src_ps):
                        return eng.tensor_copy(dst, src_ps)

    def ffn_sublayer(self, xb, tag):
        for l in range(2):
            xb = self.ffn_layer(xb, l, f"{tag}L{l}")
        return xb

    def ffn_layer(self, xb, l, lt):
        p = self.p
        WQKV = p.alloc(f"WQKV.{lt}", 8 * 1536 * 2)
        src = self.inp["L0_ffn_wqkv"][l].rearrange("(k p) m -> p k m", p=128)
        p.dma("gpsimd", lambda eng, s=src, b=WQKV: eng.dma_start(
            out=self.bf(b).rearrange("p (k m) -> p k m", k=8), in_=s),
            writes=[WQKV], n=1)
        XBF = p.alloc(f"XBF.{lt}", 16 * 1024 * 2)
        for t in range(NT):
            @p.g("scalar", reads=[f"{xb.name}.{t}"], writes=[f"XBF.{lt}.{t}"])
            def _(eng, t=t):
                return eng.copy(self.bf(XBF, 1024, t * 1024),
                                self.f32(xb, 1024, t * 1024))
        HF = p.alloc(f"HF.{lt}", 8 * 2048 * 2)
        self.transpose_tokmajor(XBF, 1024, HF, 8, lt, shift_ftiles=(4, 5, 6, 7))
        p.free(XBF)
        self.tap(f"hf.{lt}", HF, "raw_bf16", rows=128, cols=8 * 2048)

        QT = p.alloc(f"QT.{lt}", 16 * 512 * 2)
        KT = p.alloc(f"KT.{lt}", 16 * 512 * 2)
        VT = p.alloc(f"VT.{lt}", 16 * 512 * 2)
        SQ = p.alloc(f"SQ.{lt}", 1024 * 4)
        nrm0 = self.stat_cols(f"nrm.{lt}", 256)
        wv = self.bf(WQKV).rearrange("p (k m) -> p k m", k=8)
        for t in range(NT):
            pa, pb, pc = (("ps0a", "ps0b", "ps1a") if t % 2 == 0
                          else ("ps1b", "ps2a", "ps2b"))
            pmap = {"ps0a": self.ps[0][:, 0:512], "ps0b": self.ps[0][:, 512:1024],
                    "ps1a": self.ps[1][:, 0:512], "ps1b": self.ps[1][:, 512:1024],
                    "ps2a": self.ps[2][:, 0:512], "ps2b": self.ps[2][:, 512:1024]}
            for seg, (c0, ptok) in enumerate([(0, pa), (512, pb), (1024, pc)]):
                @p.g("tensor", reads=[f"HF.{lt}.{k}" for k in range(8)] + [WQKV],
                     writes=[ptok])
                def _(eng, t=t, c0=c0, ptok=ptok, pmap=pmap):
                    last = None
                    for k in range(8):
                        last = eng.matmul(
                            pmap[ptok],
                            self.bf(HF, 128, k * 2048 + t * 128),
                            wv[:, k, c0:c0 + 512],
                            start=(k == 0), stop=(k == 7))
                    return last
            @p.g("vector", reads=[pc], writes=[f"VT.{lt}.{t}"])
            def _(eng, t=t, pc=pc, pmap=pmap):
                return eng.tensor_copy(self.bf(VT, 512, t * 512), pmap[pc])
            @p.g("scalar", reads=[pa, pb], writes=[SQ])
            def _(eng, pa=pa, pb=pb, pmap=pmap, t=t):
                eng.activation(self.f32(SQ, 512, 0), pmap[pa], AF.Square)
                return eng.activation(self.f32(SQ, 512, 512), pmap[pb], AF.Square)

            @p.g("vector", reads=[SQ], writes=[f"st.nrm.{lt}.{t}"])
            def _(eng, t=t):
                return eng.reduce_sum(
                    self.sv(f"nrm.{lt}", 16 * t, 16),
                    self.f32(SQ, 1024).rearrange("p (h c) -> p h c", h=16), axis=AX.X)

            @p.g("vector", reads=[f"st.nrm.{lt}.{t}"], writes=[f"st.nrm.{lt}.{t}"])
            def _(eng, t=t):
                return eng.tensor_scalar(self.sv(f"nrm.{lt}", 16 * t, 16),
                                         self.sv(f"nrm.{lt}", 16 * t, 16),
                                         1e-24, None, ALU.max)

            @p.g("scalar", reads=[f"st.nrm.{lt}.{t}"], writes=[f"st.nrm.{lt}.{t}"])
            def _(eng, t=t):
                return eng.activation(self.sv(f"nrm.{lt}", 16 * t, 16),
                                      self.sv(f"nrm.{lt}", 16 * t, 16), AF.Ln)

            @p.g("scalar", reads=[f"st.nrm.{lt}.{t}"], writes=[f"st.nrm.{lt}.{t}"])
            def _(eng, t=t):
                return eng.activation(self.sv(f"nrm.{lt}", 16 * t, 16),
                                      self.sv(f"nrm.{lt}", 16 * t, 16),
                                      AF.Exp, scale=-0.5)
            @p.g("vector", reads=[pa, pb, f"st.nrm.{lt}.{t}"],
                 writes=[f"QT.{lt}.{t}", f"KT.{lt}.{t}"],
                 ptr_reads=[f"st.nrm.{lt}.{t}"])
            def _(eng, t=t, pa=pa, pb=pb, pmap=pmap):
                last = None
                for h in range(8):
                    last = eng.tensor_scalar(
                        self.bf(QT, 64, t * 512 + h * 64),
                        pmap[pa][:, h * 64:(h + 1) * 64],
                        self.sv(f"nrm.{lt}", 16 * t + h, 1), None, ALU.mult)
                for h in range(8):
                    last = eng.tensor_scalar(
                        self.bf(KT, 64, t * 512 + h * 64),
                        pmap[pb][:, h * 64:(h + 1) * 64],
                        self.sv(f"nrm.{lt}", 16 * t + 8 + h, 1), None, ALU.mult)
                return last
        self.tap(f"qt.{lt}", QT, "tok_bf16")
        self.tap(f"kt.{lt}", KT, "tok_bf16")
        self.tap(f"vt.{lt}", VT, "tok_bf16")
        p.free(SQ)
        p.free(HF)
        p.free(WQKV)

        QF = p.alloc(f"QF.{lt}", 4 * 2048 * 2)
        KF = p.alloc(f"KF.{lt}", 4 * 2048 * 2)
        self.transpose_tokmajor(QT, 512, QF, 4, lt)
        self.transpose_tokmajor(KT, 512, KF, 4, lt)
        p.free(QT)
        p.free(KT)
        ROT = p.alloc(f"ROT.{lt}", 4 * 2048 * 4)
        rot_off = {"qa": 0, "qb": 2048, "ka": 4096, "kb": 6144}
        p.dma("sync", lambda eng: [
            eng.dma_start(out=self.f32(ROT, 2048, rot_off["qa"]), in_=self.cst["C_ROTQA"]),
            eng.dma_start(out=self.f32(ROT, 2048, rot_off["qb"]), in_=self.cst["C_ROTQB"]),
            eng.dma_start(out=self.f32(ROT, 2048, rot_off["ka"]), in_=self.cst["C_ROTKA"]),
            eng.dma_start(out=self.f32(ROT, 2048, rot_off["kb"]), in_=self.cst["C_ROTKB"]),
        ], writes=[ROT], n=4)
        for nm, col, s2 in [("qa", f"qs.{l}", 8.0), ("qb", f"qssw.{l}", 8.0),
                            ("ka", f"ks.{l}", None), ("kb", f"kssw.{l}", None)]:
            @p.g("vector", reads=[ROT, "bpack"], writes=[ROT],
                 ptr_reads=["bpack"])
            def _(eng, nm=nm, col=col, s2=s2):
                if s2 is None:
                    return eng.tensor_scalar(self.f32(ROT, 2048, rot_off[nm]),
                                             self.f32(ROT, 2048, rot_off[nm]),
                                             self.bcol(col), None, ALU.mult)
                return eng.tensor_scalar(self.f32(ROT, 2048, rot_off[nm]),
                                         self.f32(ROT, 2048, rot_off[nm]),
                                         self.bcol(col), s2, ALU.mult,
                                         ALU.mult)
        TMP1 = p.alloc(f"TMP1.{lt}", 1024 * 4)
        TMP2 = p.alloc(f"TMP2.{lt}", 1024 * 4)
        for (dstb, ta, tb) in [(QF, "qa", "qb"), (KF, "ka", "kb")]:
            for qt in range(4):
                for hf in range(2):
                    base = qt * 2048 + hf * 1024
                    qraw = self.bf(dstb, 1024, base)
                    @p.g("tensor", reads=[f"{dstb.name}.{qt}", "consts"],
                         writes=["ps0a", "ps0b"])
                    def _(eng, qraw=qraw):
                        eng.matmul(self.ps[0][:, 0:512], self.cview("C_ROTM"),
                                   qraw[:, 0:512], start=True, stop=True)
                        return eng.matmul(self.ps[0][:, 512:1024], self.cview("C_ROTM"),
                                          qraw[:, 512:1024], start=True, stop=True)
                    @p.g("vector", reads=[f"{dstb.name}.{qt}", ROT], writes=[TMP1])
                    def _(eng, qraw=qraw, ta=ta, hf=hf):
                        return eng.tensor_mul(self.f32(TMP1, 1024), qraw,
                                              self.f32(ROT, 1024, rot_off[ta] + hf * 1024))
                    @p.g("vector", reads=["ps0a", "ps0b", ROT], writes=[TMP2])
                    def _(eng, tb=tb, hf=hf):
                        return eng.tensor_mul(self.f32(TMP2, 1024),
                                              self.ps[0][:, 0:1024],
                                              self.f32(ROT, 1024, rot_off[tb] + hf * 1024))
                    @p.g("vector", reads=[TMP1, TMP2], writes=[f"{dstb.name}.{qt}"])
                    def _(eng, qraw=qraw):
                        return eng.tensor_add(qraw, self.f32(TMP1, 1024),
                                              self.f32(TMP2, 1024))
        p.free(TMP1)
        p.free(TMP2)
        p.free(ROT)
        self.tap(f"qf.{lt}", QF, "raw_bf16", rows=128, cols=4 * 2048)
        self.tap(f"kf.{lt}", KF, "raw_bf16", rows=128, cols=4 * 2048)
        WO = p.alloc(f"WO.{lt}", 4 * 1024 * 2)
        wosrc = self.inp["L0_ffn_wo"][l].rearrange("(k p) m -> p k m", p=128)
        p.dma("gpsimd", lambda eng, s=wosrc, b=WO: eng.dma_start(
            out=self.bf(b).rearrange("p (k m) -> p k m", k=4), in_=s),
            writes=[WO], n=1)
        self.local_attention(xb, l, lt, QF, KF, VT, WO)
        self.tap(f"xwo.{lt}", xb, "tok_f32")
        p.free(QF)
        p.free(KF)
        p.free(VT)
        p.free(WO)
        return self.ffn_mlp(xb, l, lt)


    def local_attention(self, xb, l, lt, QF, KF, VT, WO):
        p = self.p
        OF = p.alloc(f"OF.{lt}", 4 * 2048 * 2)
        PL = p.alloc(f"PL.{lt}", 640 * 2)
        PTL = p.alloc(f"PTL.{lt}", 20 * 128 * 2)
        self.stat_cols(f"ls.{lt}", 128)
        self.stat_cols(f"lsr.{lt}", 128)
        for h in range(8):
            qtile = h // 2
            pb = 64 * (h % 2)
            for w in range(4):
                for ti in range(4):
                    tau = 4 * w + ti
                    jb0 = max(0, tau - 4)
                    njb = tau - jb0 + 1
                    ncols = njb * 128
                    idx = h * 16 + w * 4 + ti
                    ps_toks = ["ps0a"] + (["ps0b"] if ncols > 512 else [])
                    lq = self.bf(QF, 128, qtile * 2048 + 128 * tau)
                    lk0 = qtile * 2048 + 128 * jb0

                    @p.g("tensor",
                         reads=[f"QF.{lt}.{qtile}", f"KF.{lt}.{qtile}", "consts"],
                         writes=ps_toks)
                    def _(eng, lq=lq, lk0=lk0, ncols=ncols, njb=njb, pb=pb):
                        last = None
                        c = 0
                        while c < ncols:
                            n = min(512, ncols - c)
                            last = eng.matmul(
                                self.ps[0][:, c:c + n],
                                lq[pb:pb + 64, :],
                                self.bf(KF, n, lk0 + c)[pb:pb + 64, :],
                                start=True, stop=False)
                            c += n
                        last = eng.matmul(self.ps[0][:, ncols - 128:ncols],
                                          self.cview("C_ID"),
                                          self.cview("C_MASK_PREV"),
                                          start=False, stop=True)
                        if njb == 5:
                            last = eng.matmul(self.ps[0][:, 0:128],
                                              self.cview("C_ID"),
                                              self.cview("C_MASK_OWN"),
                                              start=False, stop=True)
                        return last

                    @p.g("scalar", reads=ps_toks, writes=[PL, f"st.ls.{lt}.{idx}"])
                    def _(eng, ncols=ncols, idx=idx):
                        return eng.activation(self.bf(PL, ncols), self.ps[0][:, 0:ncols],
                                              AF.Exp,
                                              accum_out=self.sv(f"ls.{lt}", idx, 1))

                    @p.g("vector", reads=[f"st.ls.{lt}.{idx}"],
                         writes=[f"st.lsr.{lt}.{idx}"])
                    def _(eng, idx=idx):
                        return eng.reciprocal(self.sv(f"lsr.{lt}", idx, 1),
                                              self.sv(f"ls.{lt}", idx, 1))

                    @p.g("vector", reads=[PL, f"st.lsr.{lt}.{idx}"], writes=[PL],
                         ptr_reads=[f"st.lsr.{lt}.{idx}"])
                    def _(eng, ncols=ncols, idx=idx):
                        return eng.tensor_scalar(self.bf(PL, ncols), self.bf(PL, ncols),
                                                 self.sv(f"lsr.{lt}", idx, 1),
                                                 None, ALU.mult)

                    @p.g("tensor", reads=[PL, "consts"], writes=["ps3t"])
                    def _(eng, njb=njb):
                        last = None
                        for jb in range(njb):
                            last = eng.transpose(self.ps3t[:, jb * 128:(jb + 1) * 128],
                                                 self.bf(PL, 128, jb * 128),
                                                 self.cview("C_ID"))
                        return last

                    @p.g("vector", reads=["ps3t"], writes=[PTL])
                    def _(eng, ti=ti, njb=njb, ncols=ncols):
                        return eng.tensor_copy(self.bf(PTL, ncols, ti * 5 * 128),
                                               self.ps3t[:, 0:ncols])
                jb_lo = max(0, 4 * w - 4)
                mm_args = []
                for jb_glob in range(jb_lo, 4 * w + 4):
                    for ti in range(4):
                        tau = 4 * w + ti
                        jb0 = max(0, tau - 4)
                        if not (jb0 <= jb_glob <= tau):
                            continue
                        jbrel = jb_glob - jb0
                        mm_args.append((jb_glob, ti, jbrel))

                @p.g("tensor",
                     reads=[PTL] + [f"VT.{lt}.{jb}" for jb in range(jb_lo, 4 * w + 4)],
                     writes=["ps3a"])
                def _(eng, h=h, w=w, mm_args=mm_args):
                    last = None
                    for n, (jb_glob, ti, jbrel) in enumerate(mm_args):
                        lhs = self.bf(VT, 64, jb_glob * 512 + 64 * h)
                        rhs = self.bf(PTL, 128, (ti * 5 + jbrel) * 128)
                        last = eng.matmul(self.ps3a[0:64, 128 * ti:128 * ti + 128],
                                          lhs, rhs, start=(n == 0),
                                          stop=(n == len(mm_args) - 1))
                    return last

                @p.g("vector", reads=["ps3a"], writes=[f"OF.{lt}.{h // 2}"])
                def _(eng, h=h, w=w, qtile=qtile, pb=pb):
                    dst = self.arena_t[pb:pb + 64,
                                       OF.off + (qtile * 2048 + 512 * w) * 2:
                                       OF.off + (qtile * 2048 + 512 * w + 512) * 2]
                    return eng.tensor_copy(dst.bitcast(BF16), self.ps3a[0:64, :])
        p.free(PL)
        p.free(PTL)
        self.tap(f"of.{lt}", OF, "raw_bf16", rows=128, cols=4 * 2048)
        wo = self.bf(WO).rearrange("p (k m) -> p k m", k=4)
        for t in range(NT):
            pst = self.ps[1] if t % 2 == 0 else self.ps[2]
            ptoks = [f"ps{1 if t % 2 == 0 else 2}a", f"ps{1 if t % 2 == 0 else 2}b"]

            @p.g("tensor", reads=[f"OF.{lt}.{k}" for k in range(4)] + [WO],
                 writes=ptoks)
            def _(eng, t=t, pst=pst):
                last = None
                for c in range(2):
                    for k in range(4):
                        last = eng.matmul(pst[:, c * 512:(c + 1) * 512],
                                          self.bf(OF, 128, k * 2048 + 128 * t),
                                          wo[:, k, c * 512:(c + 1) * 512],
                                          start=(k == 0), stop=(k == 3))
                return last

            @p.g("vector", reads=ptoks + [f"{xb.name}.{t}"],
                 writes=[f"{xb.name}.{t}"])
            def _(eng, t=t, pst=pst):
                return eng.tensor_add(self.f32(xb, 1024, t * 1024),
                                      self.f32(xb, 1024, t * 1024), pst[:, 0:1024])
        p.free(OF)

    def ffn_mlp(self, xb, l, lt):
        p = self.p
        sq = p.alloc(f"sq2.{lt}", 1024 * 4)
        c_slo = self.stat_cols(f"slo.{lt}", 16)
        c_shi = self.stat_cols(f"shi.{lt}", 16)
        c_qhi = self.stat_cols(f"qhi.{lt}", 16)
        c_qlo = self.stat_cols(f"qlo.{lt}", 16)
        c_shis = self.stat_cols(f"shis.{lt}", 16)
        c_qhis = self.stat_cols(f"qhis.{lt}", 16)
        for t in range(NT):
            xt = self.f32(xb, 1024, t * 1024)

            @p.g("scalar", reads=[f"{xb.name}.{t}"], writes=[sq])
            def _(eng, xt=xt):
                return eng.activation(self.f32(sq, 1024), xt, AF.Square)

            @p.g("vector", reads=[sq], writes=[f"st.qh.{lt}"])
            def _(eng, t=t):
                eng.reduce_sum(self.sv(f"qlo.{lt}", t, 1),
                               self.f32(sq, 512, 0), axis=AX.X)
                return eng.reduce_sum(self.sv(f"qhi.{lt}", t, 1),
                                      self.f32(sq, 512, 512), axis=AX.X)

            @p.g("vector", reads=[f"{xb.name}.{t}"], writes=[f"st.sh.{lt}"])
            def _(eng, xt=xt, t=t):
                eng.reduce_sum(self.sv(f"slo.{lt}", t, 1), xt[:, 0:512], axis=AX.X)
                return eng.reduce_sum(self.sv(f"shi.{lt}", t, 1),
                                      xt[:, 512:1024], axis=AX.X)
        p.free(sq)
        hi0 = self._stat_map[f"shi.{lt}"][0]
        his0 = self._stat_map[f"shis.{lt}"][0]
        stat_bytes = lambda c: self.STAT.off + c * 4

        def emit_shift(eng):
            a = eng.dma_start(
                out=self.arena_t[1:128, stat_bytes(his0):stat_bytes(his0 + 32)].bitcast(F32),
                in_=self.arena_t[0:127, stat_bytes(hi0):stat_bytes(hi0 + 32)].bitcast(F32))
            b = eng.dma_start(
                out=self.arena_t[0:1, stat_bytes(his0):stat_bytes(his0 + 32)].bitcast(F32)
                .rearrange("p (a c) -> p a c", a=2)[:, :, 1:16],
                in_=self.arena_t[127:128, stat_bytes(hi0):stat_bytes(hi0 + 32)].bitcast(F32)
                .rearrange("p (a c) -> p a c", a=2)[:, :, 0:15])
            return [a, b]
        p.dma("sync", emit_shift, reads=[f"st.sh.{lt}", f"st.qh.{lt}"],
              writes=[f"st.shs.{lt}"], n=2)

        @p.g("vector", reads=[f"st.shs.{lt}"], writes=[f"st.shs.{lt}"])
        def _(eng):
            return eng.memset(
                self.arena_t[0:1, stat_bytes(his0):stat_bytes(his0 + 32)].bitcast(F32)
                .rearrange("p (a c) -> p a c", a=2)[:, :, 0:1], 0.0)

        self.stat_cols(f"s.{lt}", 16)
        self.stat_cols(f"q.{lt}", 16)

        @p.g("vector", reads=[f"st.sh.{lt}", f"st.shs.{lt}"],
             writes=[f"st.s.{lt}", f"st.q.{lt}"])
        def _(eng):
            eng.tensor_add(self.sv(f"s.{lt}"), self.sv(f"slo.{lt}"),
                           self.sv(f"shis.{lt}"))
            return eng.tensor_add(self.sv(f"q.{lt}"), self.sv(f"qlo.{lt}"),
                                  self.sv(f"qhis.{lt}"))
        self.ln_finalize(lt)
        c_r = self._stat_map[f"r.{lt}"][0]
        c_negmr = self._stat_map[f"negmr.{lt}"][0]
        assert c_negmr == c_r + 16
        c_rs = self.stat_cols(f"rs.{lt}", 16)
        c_negmrs = self.stat_cols(f"negmrs.{lt}", 16)
        assert c_negmrs == c_rs + 16

        def emit_shift2(eng):
            a = eng.dma_start(
                out=self.arena_t[0:127, stat_bytes(c_rs):stat_bytes(c_rs + 32)].bitcast(F32),
                in_=self.arena_t[1:128, stat_bytes(c_r):stat_bytes(c_r + 32)].bitcast(F32))
            b = eng.dma_start(
                out=self.arena_t[127:128, stat_bytes(c_rs):stat_bytes(c_rs + 32)].bitcast(F32)
                .rearrange("p (a c) -> p a c", a=2)[:, :, 0:15],
                in_=self.arena_t[0:1, stat_bytes(c_r):stat_bytes(c_r + 32)].bitcast(F32)
                .rearrange("p (a c) -> p a c", a=2)[:, :, 1:16])
            c1 = eng.dma_start(
                out=self.arena_t[127:128, stat_bytes(c_rs + 15):stat_bytes(c_rs + 16)].bitcast(F32),
                in_=self.inp["BPACK"][127:128, BPACK_MAP["c_eps"][0]:BPACK_MAP["c_eps"][0] + 1])
            c2 = eng.dma_start(
                out=self.arena_t[127:128, stat_bytes(c_negmrs + 15):stat_bytes(c_negmrs + 16)].bitcast(F32),
                in_=self.inp["BPACK"][127:128, BPACK_MAP["c_eps"][0]:BPACK_MAP["c_eps"][0] + 1])
            return [a, b, c1, c2]
        p.dma("sync", emit_shift2, reads=[f"st.r.{lt}", f"st.negmr.{lt}"],
              writes=[f"st.rs.{lt}"], n=4)

        XS = p.alloc(f"XS.{lt}", 16 * 1024 * 2)
        for t in range(NT):
            xt = self.f32(xb, 1024, t * 1024)

            @p.g("scalar", reads=[f"{xb.name}.{t}", f"st.rs.{lt}"],
                 writes=[f"XS.{lt}.{t}"],
                 ptr_reads=[f"st.r.{lt}", f"st.negmr.{lt}", f"st.rs.{lt}"])
            def _(eng, xt=xt, t=t):
                eng.activation(self.bf(XS, 512, t * 1024), xt[:, 0:512], AF.Identity,
                               scale=self.sv(f"r.{lt}", t, 1),
                               bias=self.sv(f"negmr.{lt}", t, 1))
                return eng.activation(self.bf(XS, 512, t * 1024 + 512),
                                      xt[:, 512:1024], AF.Identity,
                                      scale=self.sv(f"rs.{lt}", t, 1),
                                      bias=self.sv(f"negmrs.{lt}", t, 1))
        HF2 = p.alloc(f"HF2.{lt}", 8 * 2048 * 2)
        self.transpose_tokmajor(
            XS, 1024, HF2, 8, lt, shift_ftiles=(4, 5, 6, 7),
            affine=(lambda ft: self.bcol(f"lng.{l}", ft),
                    lambda ft: self.bcol(f"lnb.{l}", ft)))
        p.free(XS)
        self.tap(f"hf2.{lt}", HF2, "raw_bf16", rows=128, cols=8 * 2048)
        self.spill(xb, self.sp_xb, f"mlp.{lt}")
        p.free(xb)

        H2 = p.alloc(f"H2.{lt}", 22 * 2048 * 2)
        GEL = p.alloc(f"GEL.{lt}", 1024 * 2)
        w1 = self.inp["L0_ffn_w1"][l]
        slab_cols = {}

        def load_w1_half(half):
            a0, a1 = (0, 1408) if half == 0 else (1408, 2730)
            g0, g1 = (2730 + a0, 2730 + a1)
            wa = p.alloc(f"W1A{half}.{lt}", 8 * (a1 - a0) * 2)
            wg = p.alloc(f"W1G{half}.{lt}", 8 * (a1 - a0) * 2)
            p.dma("gpsimd", lambda eng, wa=wa, a0=a0, a1=a1: eng.dma_start(
                out=self.bf(wa, 8 * (a1 - a0)).rearrange("p (k m) -> p k m", k=8),
                in_=w1[:, a0:a1].rearrange("(k p) m -> p k m", p=128)),
                writes=[wa], n=1)
            p.dma("gpsimd", lambda eng, wg=wg, g0=g0, g1=g1: eng.dma_start(
                out=self.bf(wg, 8 * (g1 - g0)).rearrange("p (k m) -> p k m", k=8),
                in_=w1[:, g0:g1].rearrange("(k p) m -> p k m", p=128)),
                writes=[wg], n=1)
            slab_cols[half] = (wa, wg, a0, a1)
        load_w1_half(0)
        for mp in range(22):
            half = 0 if mp < 11 else 1
            wa, wg, a0, a1 = slab_cols[half]
            c0 = mp * 128 - a0
            mw = min(128, 2730 - mp * 128)
            wav = self.bf(wa, 8 * (a1 - a0)).rearrange("p (k m) -> p k m", k=8)
            wgv = self.bf(wg, 8 * (a1 - a0)).rearrange("p (k m) -> p k m", k=8)
            for th in range(2):
                pa = self.ps[0] if (mp + th) % 2 == 0 else self.ps[1]
                pg = self.ps[1] if (mp + th) % 2 == 0 else self.ps[0]
                patok = ["ps0a", "ps0b"] if (mp + th) % 2 == 0 else ["ps1a", "ps1b"]
                pgtok = ["ps1a", "ps1b"] if (mp + th) % 2 == 0 else ["ps0a", "ps0b"]

                @p.g("tensor", reads=[wa, wg] + [f"HF2.{lt}.{k}" for k in range(8)],
                     writes=patok + pgtok)
                def _(eng, wav=wav, wgv=wgv, c0=c0, mw=mw, th=th, pa=pa, pg=pg):
                    last = None
                    for dst, wsl in ((pa, wav), (pg, wgv)):
                        for c in range(2):
                            tc0 = th * 1024 + c * 512
                            for k in range(8):
                                last = eng.matmul(
                                    dst[0:mw, c * 512:(c + 1) * 512],
                                    wsl[:, k, c0:c0 + mw],
                                    self.bf(HF2, 512, k * 2048 + tc0),
                                    start=(k == 0), stop=(k == 7))
                    return last

                @p.g("scalar", reads=pgtok + ["bpack"], writes=[GEL])
                def _(eng, pg=pg, mp=mp, mw=mw):
                    return eng.activation(self.bf(GEL, 1024)[0:mw, :],
                                          pg[0:mw, 0:1024], AF.Gelu,
                                          bias=self.bcol(f"b1g.{l}", mp)[0:mw, :])

                @p.g("vector", reads=patok + [GEL, "bpack"],
                     writes=[f"H2.{lt}.{mp}"])
                def _(eng, pa=pa, mp=mp, mw=mw, th=th):
                    return eng.scalar_tensor_tensor(
                        self.bf(H2, 1024, mp * 2048 + th * 1024)[0:mw, :],
                        pa[0:mw, 0:1024], self.bcol(f"b1a.{l}", mp)[0:mw, :],
                        self.bf(GEL, 1024)[0:mw, :], ALU.add, ALU.mult)
            if mp == 10:
                p.free(slab_cols[0][0])
                p.free(slab_cols[0][1])
                load_w1_half(1)
        p.free(slab_cols[1][0])
        p.free(slab_cols[1][1])
        p.free(GEL)
        p.free(HF2)
        self.tap(f"h2.{lt}", H2, "raw_bf16", rows=128, cols=22 * 2048)

        xb2 = p.alloc(f"X2.{lt}", T * D * 4 // 128)
        dst = self.f32(xb2).rearrange("p (t d) -> p t d", t=NT)
        srcd = self.sp_xb.rearrange("(t p) d -> p t d", p=128)
        p.dma("sync", lambda eng: eng.dma_start(out=dst, in_=srcd),
              reads=[f"dram.mlp.{lt}"],
              writes=[xb2] + [f"{xb2.name}.{t}" for t in range(NT)], n=1)
        TMPW = p.alloc(f"TMPW.{lt}", 512 * 4)
        B2B = self.load_bbc(f"b2f.{l}")
        w2 = self.inp["L0_ffn_w2"][l]
        WQN = 4
        wcols = 1024 // WQN
        for half in range(WQN):
            m0 = half * wcols
            ws = p.alloc(f"W2H{half}.{lt}", 22 * wcols * 2)

            def emit_w2(eng, ws=ws, m0=m0):
                a = eng.dma_start(
                    out=self.bf(ws).rearrange("p (k m) -> p k m", k=22)[:, 0:21, :],
                    in_=w2[0:2688, m0:m0 + wcols].rearrange("(k p) m -> p k m", p=128))
                b = eng.dma_start(
                    out=self.bf(ws).rearrange("p (k m) -> p k m", k=22)[0:42, 21:22, :],
                    in_=w2[2688:2730, m0:m0 + wcols].rearrange("(k p) m -> p k m", p=42))
                return [a, b]
            p.dma("gpsimd", emit_w2, writes=[ws], n=2)
            wsv = self.bf(ws).rearrange("p (k m) -> p k m", k=22)
            for t in range(NT):
                pst = "ps2a" if t % 2 == 0 else "ps2b"
                psv = (self.ps[2][:, 0:wcols] if t % 2 == 0
                       else self.ps[2][:, 512:512 + wcols])

                @p.g("tensor", reads=[ws] + [f"H2.{lt}.{k}" for k in range(22)],
                     writes=[pst])
                def _(eng, t=t, psv=psv, wsv=wsv):
                    last = None
                    for k in range(22):
                        kw = min(128, 2730 - k * 128)
                        last = eng.matmul(
                            psv,
                            self.bf(H2, 128, k * 2048 + t * 128)[0:kw, :],
                            wsv[0:kw, k, :],
                            start=(k == 0), stop=(k == 21))
                    return last

                @p.g("vector", reads=[pst, B2B], writes=[TMPW])
                def _(eng, psv=psv, half=half):
                    return eng.tensor_add(self.f32(TMPW, wcols), psv,
                                          self.bbc_view(B2B, half * wcols, wcols))

                @p.g("vector", reads=[TMPW, f"{xb2.name}.{t}"],
                     writes=[f"{xb2.name}.{t}"])
                def _(eng, t=t, m0=m0):
                    xv = self.f32(xb2, wcols, t * 1024 + m0)
                    return eng.tensor_add(xv, xv, self.f32(TMPW, wcols))
            p.free(ws)
        p.free(TMPW)
        p.free(B2B)
        p.free(H2)
        self.tap(f"xmlp.{lt}", xb2, "tok_f32")
        return xb2


    def tap(self, name, buf, kind, rows=128, cols=None):
        if name not in self.debug_taps:
            return
        p = self.p
        if kind == "tok_f32":
            o = self.nc.dram_tensor(f"tap_{name}", [T, D], F32,
                                    kind="ExternalOutput").ap()
            srcv = self.f32(buf).rearrange("p (t d) -> p t d", t=NT)
            dstv = o.rearrange("(t p) d -> p t d", p=128)
            rd = [buf] + [f"{buf.name}.{t}" for t in range(NT)]
        elif kind == "tok_bf16":
            o = self.nc.dram_tensor(f"tap_{name}", [T, buf.nbytes // 2 // NT], BF16,
                                    kind="ExternalOutput").ap()
            srcv = self.bf(buf).rearrange("p (t d) -> p t d", t=NT)
            dstv = o.rearrange("(t p) d -> p t d", p=128)
            rd = [buf] + [f"{buf.name}.{t}" for t in range(NT)]
        elif kind == "raw_bf16":
            cols = cols or buf.nbytes // 2
            o = self.nc.dram_tensor(f"tap_{name}", [rows, cols], BF16,
                                    kind="ExternalOutput").ap()
            srcv = self.bf(buf, cols)[0:rows, :]
            dstv = o
            rd = [buf] + [f"{buf.name}.{t}" for t in range(24)]
        else:
            cols = cols or buf.nbytes // 4
            o = self.nc.dram_tensor(f"tap_{name}", [rows, cols], F32,
                                    kind="ExternalOutput").ap()
            srcv = self.f32(buf, cols)[0:rows, :]
            dstv = o
            rd = [buf] + [f"{buf.name}.{t}" for t in range(24)]
        p.dma("sync", lambda eng: eng.dma_start(out=dstv, in_=srcv),
              reads=rd, writes=[f"tap.{name}"], n=1)
        self.taps_done.append(name)

    def transp_weight(self, wname, oc, ic, tag, oc_group=None):
        p = self.p
        O, I = oc * 128, ic * 128
        og = oc_group or oc
        wt = p.alloc(f"WT.{tag}", ic * O * 2)
        for g0 in range(0, oc, og):
            n_oc = min(og, oc - g0)
            slab = p.alloc(f"WN.{tag}.{g0}", n_oc * I * 2)
            src = self.inp[wname][g0 * 128:(g0 + n_oc) * 128].rearrange(
                "(k p) m -> p k m", p=128)
            p.dma("gpsimd", lambda eng, slab=slab, src=src, n_oc=n_oc: eng.dma_start(
                out=self.bf(slab).rearrange("p (k m) -> p k m", k=n_oc), in_=src),
                writes=[slab], n=1)
            for icc in range(ic):
                for g in range(ceil_div(n_oc, 8)):
                    n = min(8, n_oc - g * 8)

                    @p.g("tensor", reads=[slab, "consts"], writes=["ps3t"])
                    def _(eng, icc=icc, g=g, n=n, slab=slab):
                        last = None
                        for s in range(n):
                            last = eng.transpose(
                                self.ps3t[:, s * 128:(s + 1) * 128],
                                self.bf(slab, 128, (g * 8 + s) * I + icc * 128),
                                self.cview("C_ID"))
                        return last

                    @p.g("vector", reads=["ps3t"], writes=[wt])
                    def _(eng, icc=icc, g=g, n=n, g0=g0):
                        return eng.tensor_copy(
                            self.bf(wt, n * 128, icc * O + (g0 + g * 8) * 128),
                            self.ps3t[:, 0:n * 128])
            p.free(slab)
        return wt

    def attn_sublayer(self, xa):
        p = self.p
        tag = "AT"
        self.ln_stats_plain(xa, tag)
        self.ln_finalize(tag, extra_r2=True)
        yb = p.alloc("Y.AT", T * D * 4 // 128)
        for t in range(NT):
            @p.g("scalar", reads=[f"{xa.name}.{t}"], writes=[f"Y.AT.{t}"],
                 ptr_reads=[f"st.r.{tag}", f"st.negmr.{tag}"])
            def _(eng, t=t):
                return eng.activation(self.f32(yb, 1024, t * 1024),
                                      self.f32(xa, 1024, t * 1024), AF.Identity,
                                      scale=self.sv(f"r.{tag}", t, 1),
                                      bias=self.sv(f"negmr.{tag}", t, 1))
        self.spill(yb, self.sp_xb, "res")
        p.free(yb)
        XN = p.alloc("XN.AT", 16 * 1024 * 2)
        for t in range(NT):
            @p.g("scalar", reads=[f"{xa.name}.{t}", f"st.r2.{tag}"],
                 writes=[f"XN.AT.{t}"],
                 ptr_reads=[f"st.r2.{tag}", f"st.negmr2.{tag}"])
            def _(eng, t=t):
                return eng.activation(self.bf(XN, 1024, t * 1024),
                                      self.f32(xa, 1024, t * 1024), AF.Identity,
                                      scale=self.sv(f"r2.{tag}", t, 1),
                                      bias=self.sv(f"negmr2.{tag}", t, 1))
        self.spill(xa, self.sp_xa, "x1")
        p.free(xa)
        XNF = p.alloc("XNF.AT", 8 * 2048 * 2)
        self.transpose_tokmajor(
            XN, 1024, XNF, 8, tag,
            affine=(lambda ft: self.bcol("g1", ft), lambda ft: self.bcol("b1", ft)))
        p.free(XN)
        self.tap("xnf", XNF, "raw_bf16", rows=128, cols=8 * 2048)
        WQT = self.transp_weight("L0_attn_Wq", 8, 8, "WQT")
        WKT = self.transp_weight("L0_attn_Wk", 8, 8, "WKT")
        WVT = self.transp_weight("L0_attn_Wv", 4, 8, "WVT")
        WQVT = self.transp_weight("L0_attn_Wqv", 4, 8, "WQVT")
        QFg = p.alloc("QFg", 8 * 2048 * 2)
        KFg = p.alloc("KFg", 8 * 2048 * 2)
        xnf = lambda k, c, n=512: self.bf(XNF, n, k * 2048 + c)
        xnf_tok = [f"XNF.AT.{k}" for k in range(8)]
        for m in range(8):
            for half in range(2):
                ps_q = self.ps[0] if half == 0 else self.ps[1]
                qtok = ["ps0a", "ps0b"] if half == 0 else ["ps1a", "ps1b"]

                @p.g("tensor", reads=[WQT] + xnf_tok, writes=qtok)
                def _(eng, m=m, half=half, ps_q=ps_q):
                    last = None
                    for c in range(2):
                        for k in range(8):
                            last = eng.matmul(
                                ps_q[:, c * 512:(c + 1) * 512],
                                self.bf(WQT, 128, k * 1024 + m * 128),
                                xnf(k, half * 1024 + c * 512),
                                start=(k == 0), stop=(k == 7))
                    return last

                @p.g("vector", reads=qtok + ["bpack"], writes=[f"QFg.{m}"])
                def _(eng, m=m, half=half, ps_q=ps_q):
                    return eng.tensor_scalar(
                        self.bf(QFg, 1024, m * 2048 + half * 1024),
                        ps_q[:, 0:1024], self.bcol("bq", m), 0.125,
                        ALU.add, ALU.mult)

                ps_k = self.ps[1] if half == 0 else self.ps[0]
                ktok = ["ps1a", "ps1b"] if half == 0 else ["ps0a", "ps0b"]

                @p.g("tensor", reads=[WKT] + xnf_tok, writes=ktok)
                def _(eng, m=m, half=half, ps_k=ps_k):
                    last = None
                    for c in range(2):
                        for k in range(8):
                            last = eng.matmul(
                                ps_k[:, c * 512:(c + 1) * 512],
                                self.bf(WKT, 128, k * 1024 + m * 128),
                                xnf(k, half * 1024 + c * 512),
                                start=(k == 0), stop=(k == 7))
                    return last

                @p.g("scalar", reads=ktok + ["bpack"], writes=[f"KFg.{m}"])
                def _(eng, m=m, half=half, ps_k=ps_k):
                    return eng.activation(
                        self.bf(KFg, 1024, m * 2048 + half * 1024),
                        ps_k[:, 0:1024], AF.Identity, bias=self.bcol("bk", m))
        p.free(WQT)
        p.free(WKT)
        VTg = p.alloc("VTg", 16 * 512 * 2)
        BVB = self.load_bbc("bv")
        for t in range(NT):
            pv = self.ps[2][:, 0:512] if t % 2 == 0 else self.ps[2][:, 512:1024]
            ptok = "ps2a" if t % 2 == 0 else "ps2b"

            @p.g("tensor", reads=[WVT] + xnf_tok, writes=[ptok])
            def _(eng, t=t, pv=pv):
                last = None
                for k in range(8):
                    last = eng.matmul(pv, xnf(k, t * 128, 128),
                                      self.bf(WVT, 512, k * 512),
                                      start=(k == 0), stop=(k == 7))
                return last

            @p.g("vector", reads=[ptok, BVB], writes=[f"VTg.{t}"])
            def _(eng, t=t, pv=pv):
                return eng.tensor_add(self.bf(VTg, 512, t * 512), pv,
                                      self.bbc_view(BVB))
        p.free(WVT)
        p.free(BVB)
        QVF = p.alloc("QVF", 4 * 2048 * 2)
        for m in range(4):
            for half in range(2):
                pq = self.ps[0] if half == 0 else self.ps[1]
                qtok = ["ps0a", "ps0b"] if half == 0 else ["ps1a", "ps1b"]

                @p.g("tensor", reads=[WQVT] + xnf_tok, writes=qtok)
                def _(eng, m=m, half=half, pq=pq):
                    last = None
                    for c in range(2):
                        for k in range(8):
                            last = eng.matmul(
                                pq[:, c * 512:(c + 1) * 512],
                                self.bf(WQVT, 128, k * 512 + m * 128),
                                xnf(k, half * 1024 + c * 512),
                                start=(k == 0), stop=(k == 7))
                    return last

                @p.g("vector", reads=qtok + ["bpack"], writes=[f"QVF.{m}"])
                def _(eng, m=m, half=half, pq=pq):
                    return eng.tensor_scalar(
                        self.bf(QVF, 1024, m * 2048 + half * 1024),
                        pq[:, 0:1024], self.bcol("bqv", m),
                        1.0 / math.sqrt(32.0), ALU.add, ALU.mult)
        p.free(WQVT)
        p.free(XNF)
        VWS = p.alloc("VWS", 16 * 8 * 2)
        TMPV = p.alloc("TMPV", 512 * 4)
        VWS8 = p.alloc("VWS8", 8 * 4)
        WSB = self.load_bbc("wsout")
        for jt in range(NT):
            @p.g("vector", reads=[f"VTg.{jt}", WSB], writes=[TMPV])
            def _(eng, jt=jt):
                return eng.tensor_mul(self.f32(TMPV, 512),
                                      self.bf(VTg, 512, jt * 512),
                                      self.bbc_view(WSB))

            @p.g("vector", reads=[TMPV], writes=["VWS8.x"])
            def _(eng, jt=jt):
                return eng.reduce_sum(
                    self.f32(TMPV, 8, 504) if False else self.f32(VWS8, 8),
                    self.f32(TMPV, 512).rearrange("p (h c) -> p h c", h=8),
                    axis=AX.X)

            @p.g("vector", reads=["VWS8.x"], writes=[f"VWS.{jt}"])
            def _(eng, jt=jt):
                return eng.tensor_copy(self.bf(VWS, 8, jt * 8), self.f32(VWS8, 8))
        p.free(TMPV)
        p.free(VWS8)
        p.free(WSB)
        QVDT = p.alloc("QVDT", 256 * 4)
        QVDB = p.alloc("QVDB", 2048 * 2)
        for qq in range(4):
            for cc in range(4):
                @p.g("tensor", reads=[f"QVF.{qq}", self.WSBLK], writes=["ps3a"])
                def _(eng, qq=qq, cc=cc):
                    return eng.matmul(self.ps3a[0:4, :], self.bf(self.WSBLK, 4),
                                      self.bf(QVF, 512, qq * 2048 + cc * 512),
                                      start=True, stop=True)

                @p.g("vector", reads=["ps3a"], writes=[QVDB])
                def _(eng, cc=cc):
                    return eng.tensor_copy(self.bf(QVDB, 512, cc * 512)[0:4, :],
                                           self.ps3a[0:4, :])
            @p.g("tensor", reads=[QVDB, "consts"], writes=["ps0a"])
            def _(eng, qq=qq):
                last = None
                for piece in range(16):
                    last = eng.matmul(self.ps[0][:, piece * 4:piece * 4 + 4],
                                      self.bf(QVDB, 128, piece * 128)[0:4, :],
                                      self.cview("C_ID")[0:4, 0:4],
                                      start=(piece == 0), stop=(piece == 15))
                return last

            @p.g("vector", reads=["ps0a", "bpack"], writes=[QVDT])
            def _(eng, qq=qq):
                dst = self.f32(QVDT, 256).rearrange(
                    "p (t h) -> p t h", t=16)[:, :, 4 * qq:4 * qq + 4]
                src = self.ps[0][:, 0:64].rearrange("p (t h) -> p t h", t=16)
                return eng.tensor_scalar(dst, src, self.bcol("bs", 0),
                                         None, ALU.add)
        p.free(QVDB)
        p.free(QVF)
        self.tap("qfg", QFg, "raw_bf16", rows=128, cols=8 * 2048)
        self.tap("kfg", KFg, "raw_bf16", rows=128, cols=8 * 2048)
        self.tap("vtg", VTg, "tok_bf16")
        self.tap("qvdt", QVDT, "raw_f32", rows=128, cols=256)
        self.tap("vws", VWS, "raw_bf16", rows=128, cols=128)
        return self.attn_main(VTg, QFg, KFg, VWS, QVDT)

    def attn_main(self, VTg, QFg, KFg, VWS, QVDT):
        p = self.p
        tag = "AT"
        XMIX = p.alloc("XMIX", 16 * 1024 * 2)
        PB = [p.alloc("PB0", 2048 * 2), p.alloc("PB1", 2048 * 2)]
        PT = [p.alloc("PT0", 2048 * 2), p.alloc("PT1", 2048 * 2)]
        MIX = p.alloc("MIXs", 512 * 4)
        gw0 = self.stat_cols("gw.AT", 128)
        for hp in range(8):
            for it in range(16):
                slot = (hp * 16 + it) % 4
                for jh in range(2):
                    @p.g("tensor", reads=[f"QFg.{hp}", f"KFg.{hp}"],
                         writes=["ps0a", "ps0b", "ps1a", "ps1b"])
                    def _(eng, hp=hp, it=it, jh=jh):
                        last = None
                        for head in range(2):
                            pbase = 64 * head
                            dst = self.ps[head]
                            for c in range(2):
                                last = eng.matmul(
                                    dst[:, c * 512:(c + 1) * 512],
                                    self.bf(QFg, 128, hp * 2048 + it * 128)[
                                        pbase:pbase + 64, :],
                                    self.bf(KFg, 512,
                                            hp * 2048 + jh * 1024 + c * 512)[
                                        pbase:pbase + 64, :],
                                    start=True, stop=True,
                                    tile_position=(pbase, 0))
                        return last
                    for head in range(2):
                        stok = ["ps0a", "ps0b"] if head == 0 else ["ps1a", "ps1b"]
                        scol = gw0 + slot * 32 + head * 16

                        @p.g("scalar", reads=stok,
                             writes=[PB[head], f"st.gw.{slot}.{head}.s{jh}"])
                        def _(eng, head=head, jh=jh, scol=scol):
                            sv = self.f32(self.STAT, 1, scol + jh)
                            return eng.activation(
                                self.bf(PB[head], 1024, jh * 1024),
                                self.ps[head][:, 0:1024], AF.Exp, accum_out=sv)
                for head in range(2):
                    slot_toks = [f"st.gw.{slot}.{head}.s0", f"st.gw.{slot}.{head}.s1"]
                    scol = gw0 + slot * 32 + head * 16
                    for g in range(2):
                        @p.g("tensor", reads=[PB[head], "consts"], writes=["ps3t"])
                        def _(eng, head=head, g=g):
                            last = None
                            for s in range(8):
                                last = eng.transpose(
                                    self.ps3t[:, s * 128:(s + 1) * 128],
                                    self.bf(PB[head], 128, (g * 8 + s) * 128),
                                    self.cview("C_ID"))
                            return last

                        @p.g("vector", reads=["ps3t"], writes=[PT[head]])
                        def _(eng, head=head, g=g):
                            return eng.tensor_copy(
                                self.bf(PT[head], 1024, g * 1024),
                                self.ps3t[:, 0:1024])
                    pvtok = "ps2a" if head == 0 else "ps2b"
                    pv = self.ps[2][:, 0:512] if head == 0 else self.ps[2][:, 512:1024]

                    @p.g("tensor",
                         reads=[PT[head], VWS] + [f"VTg.{j}" for j in range(16)],
                         writes=[pvtok, "ps3a"])
                    def _(eng, head=head, pv=pv):
                        last = None
                        for jt in range(16):
                            lhs = self.bf(PT[head], 128, jt * 128)
                            last = eng.matmul(pv, lhs, self.bf(VTg, 512, jt * 512),
                                              start=(jt == 0), stop=(jt == 15))
                            last = eng.matmul(
                                self.ps3a[:, head * 16:head * 16 + 8],
                                lhs, self.bf(VWS, 8, jt * 8),
                                start=(jt == 0), stop=(jt == 15))
                        return last
                    @p.g("vector", reads=slot_toks, writes=[f"st.gw.{slot}.{head}.t"])
                    def _(eng, scol=scol):
                        sv = self.f32(self.STAT, 4, scol)
                        return eng.tensor_add(sv[:, 2:3], sv[:, 0:1], sv[:, 1:2])

                    @p.g("vector", reads=[f"st.gw.{slot}.{head}.t"],
                         writes=[f"st.gw.{slot}.{head}.rs"])
                    def _(eng, scol=scol):
                        sv = self.f32(self.STAT, 4, scol)
                        return eng.reciprocal(sv[:, 3:4], sv[:, 2:3])

                    @p.g("vector",
                         reads=["ps3a", QVDT, f"st.gw.{slot}.{head}.rs"],
                         writes=[f"st.gw.{slot}.{head}.cl"],
                         ptr_reads=[f"st.gw.{slot}.{head}.rs", QVDT])
                    def _(eng, head=head, scol=scol, it=it, hp=hp):
                        h = 2 * hp + head
                        return eng.tensor_scalar(
                            self.f32(self.STAT, 8, scol + 4),
                            self.ps3a[:, head * 16:head * 16 + 8],
                            self.f32(self.STAT, 1, scol + 3),
                            self.f32(QVDT, 1, it * 16 + h),
                            ALU.mult, ALU.add)

                    @p.g("scalar", reads=[f"st.gw.{slot}.{head}.cl"],
                         writes=[f"st.gw.{slot}.{head}.cl", f"st.gw.{slot}.{head}.cs"])
                    def _(eng, scol=scol):
                        return eng.activation(
                            self.f32(self.STAT, 8, scol + 4),
                            self.f32(self.STAT, 8, scol + 4), AF.Exp,
                            accum_out=self.f32(self.STAT, 1, scol + 12))

                    @p.g("vector",
                         reads=[f"st.gw.{slot}.{head}.cs", f"st.gw.{slot}.{head}.t"],
                         writes=[f"st.gw.{slot}.{head}.f0"])
                    def _(eng, scol=scol):
                        sv = self.f32(self.STAT, 16, scol)
                        return eng.tensor_mul(sv[:, 13:14], sv[:, 12:13], sv[:, 2:3])

                    @p.g("vector", reads=[f"st.gw.{slot}.{head}.f0"],
                         writes=[f"st.gw.{slot}.{head}.f"])
                    def _(eng, scol=scol):
                        sv = self.f32(self.STAT, 16, scol)
                        return eng.reciprocal(sv[:, 14:15], sv[:, 13:14])

                    @p.g("vector",
                         reads=[pvtok, f"st.gw.{slot}.{head}.cl"],
                         writes=[f"{MIX.name}.a"],
                         ptr_reads=[f"st.gw.{slot}.{head}.cl"])
                    def _(eng, head=head, scol=scol, pv=pv):
                        last = None
                        for r in range(8):
                            last = eng.tensor_scalar(
                                self.f32(MIX, 64, r * 64), pv[:, r * 64:(r + 1) * 64],
                                self.f32(self.STAT, 1, scol + 4 + r), None, ALU.mult)
                        return last

                    @p.g("vector", reads=[f"{MIX.name}.a"], writes=[f"{MIX.name}.b"])
                    def _(eng):
                        last = None
                        for r in range(4):
                            last = eng.tensor_add(self.f32(MIX, 64, r * 64),
                                                  self.f32(MIX, 64, r * 64),
                                                  self.f32(MIX, 64, (r + 4) * 64))
                        return last

                    @p.g("vector", reads=[f"{MIX.name}.b"], writes=[f"{MIX.name}.c"])
                    def _(eng):
                        last = None
                        for r in range(2):
                            last = eng.tensor_add(self.f32(MIX, 64, r * 64),
                                                  self.f32(MIX, 64, r * 64),
                                                  self.f32(MIX, 64, (r + 2) * 64))
                        return last

                    @p.g("vector", reads=[f"{MIX.name}.c"], writes=[f"{MIX.name}.d"])
                    def _(eng):
                        return eng.tensor_add(self.f32(MIX, 64, 0),
                                              self.f32(MIX, 64, 0),
                                              self.f32(MIX, 64, 64))

                    @p.g("vector",
                         reads=[f"{MIX.name}.d", f"st.gw.{slot}.{head}.f"],
                         writes=[f"XMIX.{it}"],
                         ptr_reads=[f"st.gw.{slot}.{head}.f"])
                    def _(eng, head=head, scol=scol, it=it, hp=hp):
                        h = 2 * hp + head
                        return eng.tensor_scalar(
                            self.bf(XMIX, 64, it * 1024 + h * 64),
                            self.f32(MIX, 64, 0),
                            self.f32(self.STAT, 1, scol + 14), None, ALU.mult)
        self.tap("xmix", XMIX, "tok_bf16")
        p.free(PB[0]); p.free(PB[1]); p.free(PT[0]); p.free(PT[1]); p.free(MIX)
        p.free(VTg); p.free(QFg); p.free(KFg); p.free(VWS); p.free(QVDT)
        XMF = p.alloc("XMF", 8 * 2048 * 2)
        self.transpose_tokmajor(XMIX, 1024, XMF, 8, "XMF")
        p.free(XMIX)
        WFT = self.transp_weight("L0_attn_Wf", 8, 8, "WFT")
        xb2 = p.alloc("RES2", T * D * 4 // 128)
        dst = self.f32(xb2).rearrange("p (t d) -> p t d", t=NT)
        p.dma("sync", lambda eng: eng.dma_start(
            out=dst, in_=self.sp_xb.rearrange("(t p) d -> p t d", p=128)),
            reads=["dram.res"],
            writes=[xb2] + [f"{xb2.name}.{t}" for t in range(NT)], n=1)
        TMPF = p.alloc("TMPF", 1024 * 4)
        BFB = self.load_bbc("bf")
        for t in range(NT):
            pw = self.ps[0] if t % 2 == 0 else self.ps[1]
            ptoks = ["ps0a", "ps0b"] if t % 2 == 0 else ["ps1a", "ps1b"]

            @p.g("tensor", reads=[WFT] + [f"XMF.{k}" for k in range(8)],
                 writes=ptoks)
            def _(eng, t=t, pw=pw):
                last = None
                for c in range(2):
                    for k in range(8):
                        last = eng.matmul(
                            pw[:, c * 512:(c + 1) * 512],
                            self.bf(XMF, 128, k * 2048 + t * 128),
                            self.bf(WFT, 512, k * 1024 + c * 512),
                            start=(k == 0), stop=(k == 7))
                return last

            @p.g("vector", reads=ptoks + [BFB], writes=[TMPF])
            def _(eng, pw=pw):
                return eng.tensor_add(self.f32(TMPF, 1024), pw[:, 0:1024],
                                      self.bbc_view(BFB))

            @p.g("vector", reads=[TMPF, f"{xb2.name}.{t}"],
                 writes=[f"{xb2.name}.{t}"])
            def _(eng, t=t):
                xv = self.f32(xb2, 1024, t * 1024)
                return eng.tensor_add(xv, xv, self.f32(TMPF, 1024))
        p.free(TMPF)
        p.free(BFB)
        p.free(XMF)
        p.free(WFT)
        self.tap("res2", xb2, "tok_f32")
        return self.attn_relu_ffn(xb2)

    def attn_relu_ffn(self, res2):
        p = self.p
        tag = "G2"
        self.ln_stats_plain(res2, tag)
        self.ln_finalize(tag)
        ON = p.alloc("ON.G2", 16 * 1024 * 2)
        for t in range(NT):
            @p.g("scalar", reads=[f"{res2.name}.{t}", f"st.r.{tag}"],
                 writes=[f"ON.G2.{t}"],
                 ptr_reads=[f"st.r.{tag}", f"st.negmr.{tag}"])
            def _(eng, t=t):
                return eng.activation(self.bf(ON, 1024, t * 1024),
                                      self.f32(res2, 1024, t * 1024), AF.Identity,
                                      scale=self.sv(f"r.{tag}", t, 1),
                                      bias=self.sv(f"negmr.{tag}", t, 1))
        ONF = p.alloc("ONF", 8 * 2048 * 2)
        self.transpose_tokmajor(
            ON, 1024, ONF, 8, tag,
            affine=(lambda ft: self.bcol("g2", ft), lambda ft: self.bcol("b2", ft)))
        p.free(ON)
        xc = self.reload_add_half(self.sp_xa, "x1", res2, "x2p", scale=1.0)
        WR1T = self.transp_weight("L0_attn_Wr1", 16, 8, "WR1T", oc_group=4)
        WR2T = self.transp_weight("L0_attn_Wr2", 8, 16, "WR2T", oc_group=2)
        TMPR = p.alloc("TMPR", 1024 * 4)
        BR2B = self.load_bbc("br2")
        MIDs = [p.alloc(f"MID.{i}", 16 * 128 * 2) for i in range(2)]
        for t in range(NT):
            MID = MIDs[t % 2]
            for mg in range(2):
                pm = self.ps[mg]
                ptoks = ["ps0a", "ps0b"] if mg == 0 else ["ps1a", "ps1b"]

                @p.g("tensor", reads=[WR1T] + [f"ONF.{k}" for k in range(8)],
                     writes=ptoks)
                def _(eng, t=t, mg=mg, pm=pm):
                    last = None
                    for ms in range(8):
                        m = mg * 8 + ms
                        for k in range(8):
                            last = eng.matmul(
                                pm[:, ms * 128:ms * 128 + 128],
                                self.bf(WR1T, 128, k * 2048 + m * 128),
                                self.bf(ONF, 128, k * 2048 + t * 128),
                                start=(k == 0 and ms % 4 == 0),
                                stop=(k == 7 and ms % 4 == 3))
                    return last

                @p.g("scalar", reads=ptoks + ["bpack"],
                     writes=[f"MID.{t % 2}.{mg}"])
                def _(eng, t=t, mg=mg, pm=pm, MID=MID):
                    last = None
                    for ms in range(8):
                        m = mg * 8 + ms
                        last = eng.activation(
                            self.bf(MID, 128, m * 128),
                            pm[:, ms * 128:ms * 128 + 128],
                            AF.Relu, bias=self.bcol("br1", m))
                    return last
            pw = self.ps[2]
            wtoks = ["ps2a", "ps2b"]

            @p.g("tensor",
                 reads=[WR2T, f"MID.{t % 2}.0", f"MID.{t % 2}.1"], writes=wtoks)
            def _(eng, t=t, pw=pw, MID=MID):
                last = None
                for c in range(2):
                    for k in range(16):
                        last = eng.matmul(
                            pw[:, c * 512:(c + 1) * 512],
                            self.bf(MID, 128, k * 128),
                            self.bf(WR2T, 512, k * 1024 + c * 512),
                            start=(k == 0), stop=(k == 15))
                return last

            @p.g("vector", reads=wtoks + [BR2B], writes=[TMPR])
            def _(eng, pw=pw):
                return eng.tensor_add(self.f32(TMPR, 1024), pw[:, 0:1024],
                                      self.bbc_view(BR2B))

            @p.g("vector", reads=[TMPR, f"{xc.name}.{t}"],
                 writes=[f"{xc.name}.{t}"])
            def _(eng, t=t):
                xv = self.f32(xc, 1024, t * 1024)
                return eng.tensor_add(xv, xv, self.f32(TMPR, 1024))
        self.tap("x2", xc, "tok_f32")
        p.free(MIDs[0])
        p.free(MIDs[1])
        p.free(TMPR)
        p.free(BR2B)
        p.free(WR1T)
        p.free(WR2T)
        p.free(ONF)
        return xc

    def emit_block(self):
        nc, p, sems = self.nc, self.p, self.sems

        def mk(engine):
            plan = p.ops[engine]

            def run(eng):
                for gi, (waits, emit, inc, n_dma) in enumerate(plan):
                    for (we, wc) in waits:
                        eng.wait_ge(sems[we], wc)
                    if emit is None:
                        continue
                    if n_dma:
                        insts = emit(eng)
                        if not isinstance(insts, (list, tuple)):
                            insts = [insts]
                        assert len(insts) == n_dma, (engine, len(insts), n_dma)
                        for ins in insts:
                            ins.then_inc(sems[engine], 16)
                        p.group_insts[insts[-1].ins.name] = (engine, gi)
                    else:
                        last = emit(eng)
                        assert last is not None, f"group on {engine} returned None"
                        last.then_inc(sems[engine], 1)
                        p.group_insts[last.ins.name] = (engine, gi)
            return run

        with nc.Block() as block:
            block.tensor(mk("tensor"))
            block.vector(mk("vector"))
            block.scalar(mk("scalar"))
            block.sync(mk("sync"))
            block.gpsimd(mk("gpsimd"))
        self.es.close()



def build_kernel(debug_taps=(), stop_after="", detect_races=True, paranoid=False):
    PARANOID[0] = paranoid
    b = Builder(debug_taps, detect_races)
    b.stop_after = stop_after
    b.build()
    b.emit_block()
    return b


def kernel(**inputs):
    x = np.ascontiguousarray(np.asarray(inputs["L0_attn_x"], dtype=np.float32))
    assert x.shape == (8, T, D)
    consts = host_constants()
    bp, bbc, wsblk = host_packs(inputs)
    b = build_kernel()
    common = {}
    for name in INPUT_SPECS:
        common[name] = np.ascontiguousarray(np.asarray(inputs[name], np.float32))
    for name, arr in consts.items():
        common[name] = np.ascontiguousarray(arr)
    common["BPACK"] = np.ascontiguousarray(bp)
    common["BBC"] = np.ascontiguousarray(bbc)
    common["BPACKBF"] = np.ascontiguousarray(wsblk)
    in_maps = [dict(common, x=np.ascontiguousarray(x[c])) for c in range(8)]
    res = run_bass_kernel_spmd(b.nc, in_maps, core_ids=list(range(8)))
    return np.stack([np.asarray(res.results[c]["out"], np.float32)
                     for c in range(8)], axis=0)


# version 04
# speedup vs baseline: 1.0742x; 1.0742x over previous
import contextlib
import math

import numpy as np
import ml_dtypes

import concourse.bass as bass
import concourse.mybir as mybir
from concourse.bass_utils import run_bass_kernel_spmd

F32 = mybir.dt.float32
BF16 = mybir.dt.bfloat16
U8 = mybir.dt.uint8
AF = mybir.ActivationFunctionType
ALU = mybir.AluOpType
AX = mybir.AxisListType

T = 2048
D = 1024
NT = 16
EPS = 1e-5

ARENA_KB = 207


ENGINES = ("tensor", "vector", "scalar", "sync", "gpsimd")


class Buf:

    def __init__(self, name, off, nbytes):
        self.name = name
        self.off = off
        self.nbytes = nbytes
        self.ghosts = []

    def __repr__(self):
        return f"Buf({self.name}@{self.off}:{self.off+self.nbytes})"


class Arena:
    def __init__(self, nbytes):
        self.nbytes = nbytes
        self.free = [(0, nbytes)]
        self.dead = []
        self.live = {}
        self.peak = 0

    def alloc(self, name, nbytes, align=128, side="auto"):
        nbytes = (nbytes + align - 1) // align * align
        if side == "auto":
            side = "top" if nbytes <= 8 * 1024 else "bottom"
        order = (range(len(self.free)) if side == "bottom"
                 else range(len(self.free) - 1, -1, -1))
        for i in order:
            off, size = self.free[i]
            if side == "top":
                a_off = (off + size - nbytes) // align * align
                if a_off < off:
                    continue
            else:
                a_off = (off + align - 1) // align * align
            if a_off + nbytes <= off + size:
                pre = a_off - off
                post = (off + size) - (a_off + nbytes)
                repl = []
                if pre:
                    repl.append((off, pre))
                if post:
                    repl.append((a_off + nbytes, post))
                self.free[i:i + 1] = repl
                buf = Buf(name, a_off, nbytes)
                keep = []
                for (doff, dsize, tok) in self.dead:
                    if doff < a_off + nbytes and a_off < doff + dsize:
                        buf.ghosts.append(tok)
                        keep.append((doff, dsize, tok))
                    else:
                        keep.append((doff, dsize, tok))
                self.dead = keep
                self.live[name] = buf
                for ob in self.live.values():
                    if ob is not buf and ob.off < buf.off + buf.nbytes and buf.off < ob.off + ob.nbytes:
                        raise AssertionError(f"alloc overlap: {buf} vs {ob}")
                used = self.nbytes - sum(s for _, s in self.free)
                self.peak = max(self.peak, used)
                return buf
        raise MemoryError(
            f"arena: cannot alloc {name} ({nbytes}B); free={self.free}")

    def freebuf(self, buf, tokens=None):
        assert self.live.pop(buf.name) is buf
        for tok in (tokens if tokens is not None else [buf.name]):
            self.dead.append((buf.off, buf.nbytes, tok))
        self.free.append((buf.off, buf.nbytes))
        self.free.sort()
        merged = []
        for off, size in self.free:
            if merged and merged[-1][0] + merged[-1][1] == off:
                merged[-1] = (merged[-1][0], merged[-1][1] + size)
            else:
                merged.append((off, size))
        self.free = [tuple(x) for x in merged]


PARANOID = [False]


class Prog:

    def __init__(self):
        self.ops = {e: [] for e in ENGINES}
        self.count = {e: 0 for e in ENGINES}
        self.last_write = {}
        self.readers = {}
        self.guaranteed = {e: {f: 0 for f in ENGINES} for e in ENGINES}
        self.arena = Arena(ARENA_KB * 1024)
        self.ngroups = 0
        self.alloc_log = []
        self.group_insts = {}

    def alloc(self, name, nbytes, align=128, side="auto"):
        buf = self.arena.alloc(name, nbytes, align, side)
        self.alloc_log.append((self.ngroups, "A", name, buf.off, buf.nbytes))
        return buf

    def free(self, buf):
        self.alloc_log.append((self.ngroups, "F", buf.name, buf.off, buf.nbytes))
        pref = buf.name + "."
        toks = set([buf.name])
        for tok in list(self.last_write) + list(self.readers):
            if tok.startswith(pref):
                toks.add(tok)
        self.arena.freebuf(buf, sorted(toks))

    def group(self, engine, emit, reads=(), writes=(), ptr_reads=(), n_dma=0):
        def owner_of(tok):
            live = self.arena.live
            parts = tok.split(".")
            for i in range(len(parts), 0, -1):
                cand = ".".join(parts[:i])
                if cand in live:
                    return live[cand]
            return None

        def toks(xs, consume_ghosts=False):
            out = []
            ghost_owners = []
            for x in xs:
                if isinstance(x, Buf):
                    out.append(x.name)
                    buf = x
                else:
                    out.append(x)
                    buf = owner_of(x) if consume_ghosts else None
                if consume_ghosts and buf is not None and buf.ghosts:
                    out.extend(buf.ghosts)
                    ghost_owners.append(buf)
            for buf in ghost_owners:
                buf.ghosts = []
            return out

        rtoks = toks(reads)
        wtoks = toks(writes, consume_ghosts=True)
        ptoks = toks(ptr_reads)

        needs_w = {}
        needs_r = {}

        def need(d, e, c):
            if c > d.get(e, 0):
                d[e] = c

        if PARANOID[0]:
            for e in ENGINES:
                if e != engine and self.count[e] > 0:
                    need(needs_w, e, self.count[e])

        for tok in rtoks + ptoks:
            lw = self.last_write.get(tok)
            if lw:
                need(needs_w, *lw)
        for tok in wtoks:
            lw = self.last_write.get(tok)
            if lw:
                need(needs_w, *lw)
            for e, c in self.readers.get(tok, {}).items():
                need(needs_r, e, c)

        needs = dict(needs_r)
        for e, c in needs_w.items():
            if c > needs.get(e, 0):
                needs[e] = c
        waits = []
        for e, c in sorted(needs.items()):
            if e == engine:
                raww = needs_w.get(e, 0)
                if engine in ("tensor", "sync", "gpsimd"):
                    continue
                if raww == 0:
                    continue
                c = raww
                if self.guaranteed[engine].get(e, 0) >= c:
                    continue
            elif self.guaranteed[engine].get(e, 0) >= c:
                continue
            waits.append((e, c))
            self.guaranteed[engine][e] = c

        inc = 16 * n_dma if n_dma else 1
        newc = self.count[engine] + inc
        self.count[engine] = newc
        self.ops[engine].append((waits, emit, inc, n_dma))
        self.ngroups += 1

        for tok in wtoks:
            self.last_write[tok] = (engine, newc)
            self.readers.pop(tok, None)
        for tok in rtoks + ptoks:
            self.readers.setdefault(tok, {})[engine] = newc

    def g(self, engine, reads=(), writes=(), ptr_reads=()):
        def deco(fn):
            self.group(engine, fn, reads, writes, ptr_reads)
        return deco

    def dma(self, queue, emit, reads=(), writes=(), n=1):
        self.group(queue, emit, reads, writes, n_dma=n)

    def emit_all(self, nc, sems):
        for engine in ENGINES:
            plan = self.ops[engine]
            eng = getattr(nc, engine)

            def run(eng, plan=plan, engine=engine):
                for waits, emit, inc, n_dma in plan:
                    for (we, wc) in waits:
                        eng.wait_ge(sems[we], wc)
                    if n_dma:
                        insts = emit(eng)
                        if not isinstance(insts, (list, tuple)):
                            insts = [insts]
                        assert len(insts) == n_dma, (engine, len(insts), n_dma)
                        for ins in insts:
                            ins.then_inc(sems[engine], 16)
                    else:
                        last = emit(eng)
                        assert last is not None, f"group on {engine} returned None"
                        last.then_inc(sems[engine], 1)
            yield engine, run



def host_constants():
    n, dh, window = T, 64, 512
    half = np.arange(0, dh, 2, dtype=np.float64)
    inv = 1.0 / 10000.0 ** (half / dh)
    t = np.arange(n, dtype=np.float64)
    fr = t[:, None] * inv[None, :]
    base = (half + 0.4 * dh) / (1.4 * dh)
    power = (t - n // 2) / float(window // 2)
    sc = base[None, :] ** power[:, None]
    cos = np.cos(fr)
    sin = np.sin(fr)
    cos64 = np.concatenate([cos, cos], axis=1)
    sin64 = np.concatenate([sin, sin], axis=1)
    sc64 = np.concatenate([sc, sc], axis=1)
    qa = (cos64 * sc64).T
    qb = (sin64 * sc64).T
    ka = (cos64 / sc64).T
    kb = (sin64 / sc64).T
    rep = lambda a: np.concatenate([a, a], axis=0).astype(np.float32)
    consts = {
        "C_ROTQA": rep(qa), "C_ROTQB": rep(qb),
        "C_ROTKA": rep(ka), "C_ROTKB": rep(kb),
    }
    consts["C_ID"] = np.eye(128, dtype=ml_dtypes.bfloat16)
    M = np.zeros((64, 64), np.float32)
    for dp in range(32):
        M[dp, dp + 32] = -1.0
    for dp in range(32, 64):
        M[dp, dp - 32] = 1.0
    M128 = np.zeros((128, 128), np.float32)
    M128[0:64, 0:64] = M
    M128[64:128, 64:128] = M
    consts["C_ROTM"] = M128.T.astype(ml_dtypes.bfloat16)
    i = np.arange(128)
    own = np.where(i[None, :] >= i[:, None], 0.0, -1e30).astype(np.float32)
    prev = np.where(i[None, :] <= i[:, None], 0.0, -1e30).astype(np.float32)
    consts["C_MASK_OWN"] = own.astype(ml_dtypes.bfloat16)
    consts["C_MASK_PREV"] = prev.astype(ml_dtypes.bfloat16)
    blk2 = np.zeros((2, 128), np.float32)
    blk2[0, 0:64] = 1.0
    blk2[1, 64:128] = 1.0
    consts["C_BLK2"] = blk2
    consts["C_ONES1"] = np.ones((1, 128), dtype=ml_dtypes.bfloat16)
    ind4 = np.zeros((128, 32), np.float32)
    for m in range(4):
        ind4[m, m * 8:m * 8 + 8] = 1.0
    consts["C_IND4"] = ind4.astype(ml_dtypes.bfloat16)
    ind8 = np.zeros((128, 544), np.float32)
    for c in range(4):
        for pp in range(128):
            ind8[2 * c + pp // 64, c * 128 + pp] = 1.0
    ind8[0:8, 512] = 1.0
    consts["C_IND8"] = ind8.astype(ml_dtypes.bfloat16)
    return consts


CONST_SPECS = {
    "C_ROTQA": ([128, 2048], F32), "C_ROTQB": ([128, 2048], F32),
    "C_ROTKA": ([128, 2048], F32), "C_ROTKB": ([128, 2048], F32),
    "C_ID": ([128, 128], BF16), "C_ROTM": ([128, 128], BF16),
    "C_MASK_OWN": ([128, 128], BF16), "C_MASK_PREV": ([128, 128], BF16),
    "C_BLK2": ([2, 128], F32), "C_ONES1": ([1, 128], BF16),
    "C_IND4": ([128, 32], BF16), "C_IND8": ([128, 544], BF16),
}

INPUT_SPECS = {
    "L0_attn_Wq": ([1024, 1024], F32), "L0_attn_bq": ([1024], F32),
    "L0_attn_Wk": ([1024, 1024], F32), "L0_attn_bk": ([1024], F32),
    "L0_attn_Wv": ([512, 1024], F32), "L0_attn_bv": ([512], F32),
    "L0_attn_Wqv": ([512, 1024], F32), "L0_attn_bqv": ([512], F32),
    "L0_attn_Ws": ([1, 96], F32), "L0_attn_bs": ([1], F32),
    "L0_attn_Wf": ([1024, 1024], F32), "L0_attn_bf": ([1024], F32),
    "L0_attn_Wr1": ([2048, 1024], F32), "L0_attn_br1": ([2048], F32),
    "L0_attn_Wr2": ([1024, 2048], F32), "L0_attn_br2": ([1024], F32),
    "L0_attn_g1": ([1024], F32), "L0_attn_b1": ([1024], F32),
    "L0_attn_g2": ([1024], F32), "L0_attn_b2": ([1024], F32),
    "L0_ffn_wqkv": ([2, 1024, 1536], F32),
    "L0_ffn_q_scale": ([2, 64], F32), "L0_ffn_k_scale": ([2, 64], F32),
    "L0_ffn_wo": ([2, 512, 1024], F32),
    "L0_ffn_ln_g": ([2, 1024], F32), "L0_ffn_ln_b": ([2, 1024], F32),
    "L0_ffn_w1": ([2, 1024, 5460], F32), "L0_ffn_b1": ([2, 5460], F32),
    "L0_ffn_w2": ([2, 2730, 1024], F32), "L0_ffn_b2": ([2, 1024], F32),
}



class Kern:
    def __init__(self, debug_taps=(), detect_races=True):
        self.nc = bass.Bass("TRN2", target_bir_lowering=False,
                            detect_race_conditions=detect_races)
        nc = self.nc
        self.p = Prog()
        self.es = contextlib.ExitStack()
        self.debug_taps = list(debug_taps)
        self.taps_done = []

        self.x_in = nc.dram_tensor("x", [T, D], F32, kind="ExternalInput").ap()
        self.out = nc.dram_tensor("out", [T, D], F32, kind="ExternalOutput").ap()
        self.inp = {}
        for name, (shape, dt) in INPUT_SPECS.items():
            self.inp[name] = nc.dram_tensor(name, shape, dt, kind="ExternalInput").ap()
        self.cst = {}
        for name, (shape, dt) in CONST_SPECS.items():
            self.cst[name] = nc.dram_tensor(name, shape, dt, kind="ExternalInput").ap()
        self.sp_xa = nc.dram_tensor("sp_xa", [T, D], F32).ap()
        self.sp_xb = nc.dram_tensor("sp_xb", [T, D], F32).ap()

        self.arena_t = self.es.enter_context(
            nc.sbuf_tensor("arena", [128, ARENA_KB * 1024], U8))
        self.ps = [self.es.enter_context(nc.psum_tensor(f"ps{i}", [128, 1024], F32))
                   for i in range(3)]
        self.ps3a = self.es.enter_context(nc.psum_tensor("ps3a", [128, 512], F32))
        self.ps3t = self.es.enter_context(nc.psum_tensor("ps3t", [128, 1024], BF16))
        self.sems = {e: self.es.enter_context(nc.semaphore(f"sem_{e}"))
                     for e in ENGINES}

        self.dbg_cnt = 0

    def f32(self, buf, cols=None, off_el=0):
        ap = self.arena_t[:, buf.off + off_el * 4:
                          buf.off + (off_el + (cols or (buf.nbytes // 4 - off_el))) * 4]
        return ap.bitcast(F32)

    def bf(self, buf, cols=None, off_el=0):
        ap = self.arena_t[:, buf.off + off_el * 2:
                          buf.off + (off_el + (cols or (buf.nbytes // 2 - off_el))) * 2]
        return ap.bitcast(BF16)

    def pstok(self, i, half):
        return f"ps{i}{'ab'[half]}"

    def tap_f32(self, name, get_ap, rows, cols):
        if name not in self.debug_taps:
            return
        o = self.nc.dram_tensor(f"tap_{name}", [rows, cols], F32,
                                kind="ExternalOutput").ap()
        ap, reads = get_ap()
        self.p.dma("sync", lambda eng: eng.dma_start(out=o, in_=ap),
                   reads=reads, n=1)
        self.taps_done.append(name)

    def tap_bf16(self, name, get_ap, rows, cols):
        if name not in self.debug_taps:
            return
        o = self.nc.dram_tensor(f"tap_{name}", [rows, cols], BF16,
                                kind="ExternalOutput").ap()
        ap, reads = get_ap()
        self.p.dma("sync", lambda eng: eng.dma_start(out=o, in_=ap),
                   reads=reads, n=1)
        self.taps_done.append(name)


def ceil_div(a, b):
    return (a + b - 1) // b




def bpack_layout():
    entries = [
        ("bq", 8), ("bk", 8), ("bqv", 4), ("br1", 16),
        ("g1", 8), ("b1", 8), ("g2", 8), ("b2", 8),
        ("lng.0", 8), ("lnb.0", 8), ("lng.1", 8), ("lnb.1", 8),
        ("b1a.0", 22), ("b1g.0", 22), ("b1a.1", 22), ("b1g.1", 22),
        ("qs.0", 1), ("qssw.0", 1), ("ks.0", 1), ("kssw.0", 1),
        ("qs.1", 1), ("qssw.1", 1), ("ks.1", 1), ("kssw.1", 1),
        ("bs", 1), ("c_eps", 1), ("c_eps2", 1),
    ]
    m, c = {}, 0
    for name, n in entries:
        m[name] = (c, n)
        c += n
    return m, c


def bbc_layout():
    entries = [("bv", 512), ("bf", 1024), ("br2", 1024),
               ("b2f.0", 1024), ("b2f.1", 1024), ("wsout", 512)]
    m, c = {}, 0
    for name, n in entries:
        m[name] = (c, n)
        c += n
    return m, c


BPACK_MAP, BPACK_COLS = bpack_layout()
BBC_MAP, BBC_COLS = bbc_layout()


def host_packs(inp):
    def cols(vec, n):
        v = np.zeros((128, n), np.float32)
        flat = np.asarray(vec, np.float32).reshape(-1)
        for c in range(n):
            seg = flat[c * 128:(c + 1) * 128]
            v[:len(seg), c] = seg
        return v

    bp = np.zeros((128, BPACK_COLS), np.float32)

    def put(name, arr):
        c0, n = BPACK_MAP[name]
        bp[:, c0:c0 + n] = arr

    put("bq", cols(inp["L0_attn_bq"], 8))
    put("bk", cols(inp["L0_attn_bk"], 8))
    put("bqv", cols(inp["L0_attn_bqv"], 4))
    put("br1", cols(inp["L0_attn_br1"], 16))
    put("g1", cols(inp["L0_attn_g1"], 8))
    put("b1", cols(inp["L0_attn_b1"], 8))
    put("g2", cols(inp["L0_attn_g2"], 8))
    put("b2", cols(inp["L0_attn_b2"], 8))
    for l in range(2):
        put(f"lng.{l}", cols(inp["L0_ffn_ln_g"][l], 8))
        put(f"lnb.{l}", cols(inp["L0_ffn_ln_b"][l], 8))
        put(f"b1a.{l}", cols(inp["L0_ffn_b1"][l][0:2730], 22))
        put(f"b1g.{l}", cols(inp["L0_ffn_b1"][l][2730:5460], 22))
        qs = np.asarray(inp["L0_ffn_q_scale"][l], np.float32)
        ks = np.asarray(inp["L0_ffn_k_scale"][l], np.float32)
        put(f"qs.{l}", np.concatenate([qs, qs]).reshape(128, 1))
        put(f"qssw.{l}", np.concatenate([qs[32:], qs[:32], qs[32:], qs[:32]]).reshape(128, 1))
        put(f"ks.{l}", np.concatenate([ks, ks]).reshape(128, 1))
        put(f"kssw.{l}", np.concatenate([ks[32:], ks[:32], ks[32:], ks[:32]]).reshape(128, 1))
    put("bs", np.broadcast_to(np.asarray(inp["L0_attn_bs"], np.float32), (128, 1)))
    put("c_eps", np.full((128, 1), EPS, np.float32))
    put("c_eps2", np.full((128, 1), EPS * EPS, np.float32))

    bbc = np.zeros((128, BBC_COLS), np.float32)

    def putb(name, vec):
        c0, n = BBC_MAP[name]
        bbc[:, c0:c0 + n] = np.broadcast_to(np.asarray(vec, np.float32).reshape(1, n),
                                            (128, n))

    putb("bv", inp["L0_attn_bv"])
    putb("bf", inp["L0_attn_bf"])
    putb("br2", inp["L0_attn_br2"])
    putb("b2f.0", inp["L0_ffn_b2"][0])
    putb("b2f.1", inp["L0_ffn_b2"][1])
    ws = np.asarray(inp["L0_attn_Ws"], np.float32).reshape(96)
    putb("wsout", np.concatenate([ws[32:96]] * 8))

    wsblk = np.zeros((128, 4), np.float32)
    for m in range(4):
        wsblk[32 * m:32 * m + 32, m] = ws[0:32]
    return bp, bbc, wsblk.astype(ml_dtypes.bfloat16)


PACK_SPECS = {
    "BPACK": ([128, BPACK_COLS], F32),
    "BBC": ([128, BBC_COLS], F32),
    "BPACKBF": ([128, 4], BF16),
}



class Builder(Kern):
    def __init__(self, debug_taps=(), detect_races=True):
        super().__init__(debug_taps, detect_races)
        for name, (shape, dt) in PACK_SPECS.items():
            self.inp[name] = self.nc.dram_tensor(name, shape, dt,
                                                 kind="ExternalInput").ap()
        self._stat_used = 0
        self._stat_map = {}

    def stat_cols(self, name, n):
        if name in self._stat_map:
            return self._stat_map[name][0]
        c = self._stat_used
        assert c + n <= self.STAT.nbytes // 4, f"STAT overflow at {name}"
        self._stat_map[name] = (c, n)
        self._stat_used += n
        return c

    def sv(self, name, col=0, n=None):
        base, cnt = self._stat_map[name]
        if n is None:
            n = cnt - col
        return self.f32(self.STAT, n, base + col)

    def bcol(self, key, chunk=0, n=1):
        c0, cnt = BPACK_MAP[key]
        return self.f32(self.BPACK_B, n, c0 + chunk)

    def load_bbc(self, key):
        p = self.p
        c0, cnt = BBC_MAP[key]
        self._bbc_n += 1
        buf = p.alloc(f"BBC.{key}.{self._bbc_n}", cnt * 4)
        p.dma("sync", lambda eng, buf=buf, c0=c0, cnt=cnt: eng.dma_start(
            out=self.f32(buf, cnt), in_=self.inp["BBC"][:, c0:c0 + cnt]),
            writes=[buf], n=1)
        return buf

    def bbc_view(self, buf, off=0, n=None):
        if n is None:
            n = buf.nbytes // 4 - off
        return self.f32(buf, n, off)

    def build(self):
        p = self.p
        self.XA = p.alloc("XA", T * D * 4 // 128)
        self.CONST = p.alloc("CONST", 3 * 1024, side="top")
        self.STAT = p.alloc("STAT", 14 * 1024, side="top")
        self.BPACK_B = p.alloc("BPACK", BPACK_COLS * 4, side="top")
        self.WSBLK = p.alloc("WSBLK", 4 * 2)
        self._bbc_n = 0

        self.load_input_x()
        self.load_consts()
        p.dma("sync", lambda eng: [
            eng.dma_start(out=self.f32(self.BPACK_B, BPACK_COLS), in_=self.inp["BPACK"]),
            eng.dma_start(out=self.bf(self.WSBLK, 4), in_=self.inp["BPACKBF"]),
        ], writes=[self.BPACK_B, self.WSBLK, "bpack"], n=2)

        xcur = self.XA
        xb = p.alloc("XB1", T * D * 4 // 128)
        self.ln_plain(xcur, xb, "ln0")
        self.tap("ln0", xb, "tok_f32")
        self.spill(xcur, self.sp_xa, "spA")
        p.free(xcur)
        xb = self.ffn_sublayer(xb, "F1")
        xcur = self.reload_add_half(self.sp_xa, "spA", xb, "mac1")
        self.tap("x1", xcur, "tok_f32")
        if self.stop_after == "A":
            self.store_output(xcur); return
        xcur = self.attn_sublayer(xcur)
        if self.stop_after == "B":
            self.store_output(xcur); return
        xb = p.alloc("XB2", T * D * 4 // 128)
        self.ln_plain(xcur, xb, "ln2")
        self.spill(xcur, self.sp_xa, "spC")
        p.free(xcur)
        xb = self.ffn_sublayer(xb, "F2")
        xcur = self.reload_add_half(self.sp_xa, "spC", xb, "mac2")
        self.store_output(xcur)

    stop_after = ""

    def load_input_x(self):
        p = self.p
        dst = self.f32(self.XA).rearrange("p (t d) -> p t d", t=NT)
        src = self.x_in.rearrange("(t p) d -> p t d", p=128)
        p.dma("sync", lambda eng: eng.dma_start(out=dst, in_=src),
              writes=[self.XA] + [f"XA.{t}" for t in range(NT)], n=1)

    def load_consts(self):
        p = self.p
        off = 0
        self.c_off = {}
        plan = [("C_ID", 128 * 2), ("C_ROTM", 128 * 2), ("C_MASK_OWN", 128 * 2),
                ("C_MASK_PREV", 128 * 2), ("C_ONES1", 128 * 2), ("C_BLK2", 128 * 4),
                ("C_IND4", 32 * 2), ("C_IND8", 544 * 2)]
        for name, nb in plan:
            self.c_off[name] = off
            off += nb
        assert off <= self.CONST.nbytes

        def emit(eng):
            outs = []
            for name, nb in plan:
                shape, dt = CONST_SPECS[name]
                v = self.arena_t[0:shape[0],
                                 self.CONST.off + self.c_off[name]:
                                 self.CONST.off + self.c_off[name] + nb]
                outs.append(eng.dma_start(out=v.bitcast(dt), in_=self.cst[name]))
            return outs
        p.dma("sync", emit, writes=[self.CONST, "consts"], n=len(plan))

    def cview(self, name, rows=None):
        shape, dt = CONST_SPECS[name]
        nb = shape[1] * (2 if dt == BF16 else 4)
        v = self.arena_t[0:(rows or shape[0]),
                         self.CONST.off + self.c_off[name]:
                         self.CONST.off + self.c_off[name] + nb]
        return v.bitcast(dt)

    def ln_stats_plain(self, src, tag):
        p = self.p
        sq = p.alloc(f"sq.{tag}", 1024 * 4)
        self.stat_cols(f"s.{tag}", 16)
        self.stat_cols(f"q.{tag}", 16)
        for t in range(NT):
            xt = self.f32(src, 1024, t * 1024)

            @p.g("scalar", reads=[f"{src.name}.{t}"], writes=[sq, f"st.q.{tag}"])
            def _(eng, xt=xt, t=t):
                return eng.activation(self.f32(sq, 1024), xt, AF.Square,
                                      accum_out=self.sv(f"q.{tag}", t, 1))

            @p.g("vector", reads=[f"{src.name}.{t}"], writes=[f"st.s.{tag}"])
            def _(eng, xt=xt, t=t):
                return eng.reduce_sum(self.sv(f"s.{tag}", t, 1), xt, axis=AX.X)
        p.free(sq)

    def ln_finalize(self, tag, extra_r2=False, n=16):
        p = self.p
        self.stat_cols(f"m.{tag}", n)
        self.stat_cols(f"v.{tag}", n)
        self.stat_cols(f"r.{tag}", n)
        self.stat_cols(f"negmr.{tag}", n)
        inv = 1.0 / D

        @p.g("vector", reads=[f"st.s.{tag}"], writes=[f"st.m.{tag}"])
        def _(eng):
            return eng.tensor_scalar(self.sv(f"m.{tag}"), self.sv(f"s.{tag}"),
                                     inv, None, ALU.mult)

        @p.g("vector", reads=[f"st.m.{tag}"], writes=[f"st.v.{tag}"])
        def _(eng):
            return eng.tensor_mul(self.sv(f"v.{tag}"), self.sv(f"m.{tag}"),
                                  self.sv(f"m.{tag}"))

        @p.g("vector", reads=[f"st.q.{tag}", f"st.v.{tag}"], writes=[f"st.v.{tag}"])
        def _(eng):
            return eng.scalar_tensor_tensor(self.sv(f"v.{tag}"), self.sv(f"q.{tag}"),
                                            inv, self.sv(f"v.{tag}"),
                                            ALU.mult, ALU.subtract)

        @p.g("scalar", reads=[f"st.v.{tag}", "bpack"], writes=[f"st.r.{tag}"],
             ptr_reads=["bpack"])
        def _(eng):
            return eng.activation(self.sv(f"r.{tag}"), self.sv(f"v.{tag}"),
                                  AF.Ln, bias=self.bcol("c_eps"))

        @p.g("scalar", reads=[f"st.r.{tag}"], writes=[f"st.r.{tag}"])
        def _(eng):
            return eng.activation(self.sv(f"r.{tag}"), self.sv(f"r.{tag}"),
                                  AF.Exp, scale=-0.5)

        @p.g("vector", reads=[f"st.r.{tag}", f"st.m.{tag}"], writes=[f"st.negmr.{tag}"])
        def _(eng):
            return eng.scalar_tensor_tensor(self.sv(f"negmr.{tag}"),
                                            self.sv(f"m.{tag}"), -1.0,
                                            self.sv(f"r.{tag}"), ALU.mult, ALU.mult)
        if extra_r2:
            self.stat_cols(f"r2.{tag}", n)
            self.stat_cols(f"negmr2.{tag}", n)

            @p.g("scalar", reads=[f"st.v.{tag}", "bpack"], writes=[f"st.r2.{tag}"],
                 ptr_reads=["bpack"])
            def _(eng):
                return eng.activation(self.sv(f"r2.{tag}"), self.sv(f"v.{tag}"),
                                      AF.Ln, scale=1.0 + EPS, bias=self.bcol("c_eps2"))

            @p.g("scalar", reads=[f"st.r2.{tag}"], writes=[f"st.r2.{tag}"])
            def _(eng):
                return eng.activation(self.sv(f"r2.{tag}"), self.sv(f"r2.{tag}"),
                                      AF.Exp, scale=-0.5)

            @p.g("vector", reads=[f"st.r2.{tag}", f"st.m.{tag}"],
                 writes=[f"st.negmr2.{tag}"])
            def _(eng):
                return eng.scalar_tensor_tensor(self.sv(f"negmr2.{tag}"),
                                                self.sv(f"m.{tag}"), -1.0,
                                                self.sv(f"r2.{tag}"),
                                                ALU.mult, ALU.mult)

    def ln_plain(self, src, dst, tag):
        p = self.p
        self.ln_stats_plain(src, tag)
        self.ln_finalize(tag)
        for t in range(NT):
            xt = self.f32(src, 1024, t * 1024)
            yt = self.f32(dst, 1024, t * 1024)

            @p.g("scalar", reads=[f"{src.name}.{t}"], writes=[f"{dst.name}.{t}"],
                 ptr_reads=[f"st.r.{tag}", f"st.negmr.{tag}"])
            def _(eng, xt=xt, yt=yt, t=t):
                return eng.activation(yt, xt, AF.Identity,
                                      scale=self.sv(f"r.{tag}", t, 1),
                                      bias=self.sv(f"negmr.{tag}", t, 1))

    def spill(self, buf, dram, tag):
        p = self.p
        src = self.f32(buf).rearrange("p (t d) -> p t d", t=NT)
        dst = dram.rearrange("(t p) d -> p t d", p=128)
        p.dma("sync", lambda eng: eng.dma_start(out=dst, in_=src),
              reads=[buf] + [f"{buf.name}.{t}" for t in range(NT)],
              writes=[f"dram.{tag}"], n=1)

    def reload_add_half(self, dram, dtag, other, tag, scale=0.5):
        p = self.p
        stage = p.alloc(f"rl.{tag}", 2 * 1024 * 4)
        for t in range(NT):
            half = t % 2
            sview = self.f32(stage, 1024, half * 1024)
            src = dram.rearrange("(t p) d -> p t d", p=128)[:, t]
            p.dma("sync",
                  lambda eng, sv_=sview, src=src: eng.dma_start(out=sv_, in_=src),
                  reads=[f"dram.{dtag}", f"rl.{tag}.{half}"],
                  writes=[f"rl.{tag}.w{half}"], n=1)
            ot = self.f32(other, 1024, t * 1024)

            @p.g("vector", reads=[f"rl.{tag}.w{half}", f"{other.name}.{t}"],
                 writes=[f"{other.name}.{t}", f"rl.{tag}.{half}"])
            def _(eng, ot=ot, sv_=sview, scale=scale):
                return eng.scalar_tensor_tensor(ot, ot, scale, sv_, ALU.mult, ALU.add)
        p.free(stage)
        return other

    def store_output(self, buf):
        p = self.p
        src = self.f32(buf).rearrange("p (t d) -> p t d", t=NT)
        dst = self.out.rearrange("(t p) d -> p t d", p=128)
        p.dma("sync", lambda eng: eng.dma_start(out=dst, in_=src),
              reads=[buf] + [f"{buf.name}.{t}" for t in range(NT)],
              writes=["output"], n=1)
        p.ops["sync"].append(([("sync", p.count["sync"])], None, 0, 0))

    def transpose_tokmajor(self, srcbuf, src_cols, dstbuf, n_ftiles, tag,
                           shift_ftiles=(), evac="vector", affine=None,
                           src_tile_cols=None):
        p = self.p
        stc = src_tile_cols if src_tile_cols is not None else src_cols
        for ft in range(n_ftiles):
            shifted = ft in shift_ftiles
            if shifted:
                @p.g("vector", writes=[f"{dstbuf.name}.{ft}"])
                def _(eng, ft=ft):
                    return eng.memset(self.bf(dstbuf, 1, ft * 2048), 0.0)
            for grp in range(4):
                slots = []
                for s in range(4):
                    tc = grp * 4 + s
                    chunk = self.bf(srcbuf, 128, tc * stc + ft * 128)
                    dstp = self.ps3t[:, s * 128:(s + 1) * 128]
                    slots.append((chunk, dstp, tc))

                @p.g("tensor",
                     reads=[f"{srcbuf.name}.{tc}" for _, _, tc in slots] + ["consts"],
                     writes=["ps3t"])
                def _(eng, slots=slots):
                    last = None
                    for chunk, dstp, tc in slots:
                        last = eng.transpose(dstp, chunk, self.cview("C_ID"))
                    return last

                base = ft * 2048 + grp * 512
                if shifted:
                    width = 512 if grp < 3 else 511
                    dst = self.bf(dstbuf, width, base + 1)
                else:
                    width = 512
                    dst = self.bf(dstbuf, width, base)
                src_ps = self.ps3t[:, 0:width]
                if affine is not None:
                    sc, bi = affine

                    @p.g("scalar", reads=["ps3t", "bpack"],
                         writes=[f"{dstbuf.name}.{ft}"],
                         ptr_reads=[])
                    def _(eng, dst=dst, src_ps=src_ps, ft=ft, sc=sc, bi=bi):
                        return eng.activation(dst, src_ps, AF.Identity,
                                              scale=sc(ft), bias=bi(ft))
                else:
                    @p.g(evac, reads=["ps3t"], writes=[f"{dstbuf.name}.{ft}"])
                    def _(eng, dst=dst, src_ps=src_ps):
                        return eng.tensor_copy(dst, src_ps)

    def ffn_sublayer(self, xb, tag):
        for l in range(2):
            xb = self.ffn_layer(xb, l, f"{tag}L{l}")
        return xb

    def ffn_layer(self, xb, l, lt):
        p = self.p
        WQKV = p.alloc(f"WQKV.{lt}", 8 * 1536 * 2)
        src = self.inp["L0_ffn_wqkv"][l].rearrange("(k p) m -> p k m", p=128)
        p.dma("gpsimd", lambda eng, s=src, b=WQKV: eng.dma_start(
            out=self.bf(b).rearrange("p (k m) -> p k m", k=8), in_=s),
            writes=[WQKV], n=1)
        XBF = p.alloc(f"XBF.{lt}", 16 * 1024 * 2)
        for t in range(NT):
            @p.g("scalar", reads=[f"{xb.name}.{t}"], writes=[f"XBF.{lt}.{t}"])
            def _(eng, t=t):
                return eng.copy(self.bf(XBF, 1024, t * 1024),
                                self.f32(xb, 1024, t * 1024))
        HF = p.alloc(f"HF.{lt}", 8 * 2048 * 2)
        self.transpose_tokmajor(XBF, 1024, HF, 8, lt, shift_ftiles=(4, 5, 6, 7))
        p.free(XBF)
        self.tap(f"hf.{lt}", HF, "raw_bf16", rows=128, cols=8 * 2048)

        QT = p.alloc(f"QT.{lt}", 16 * 512 * 2)
        KT = p.alloc(f"KT.{lt}", 16 * 512 * 2)
        VT = p.alloc(f"VT.{lt}", 16 * 512 * 2)
        SQ = p.alloc(f"SQ.{lt}", 1024 * 4)
        nrm0 = self.stat_cols(f"nrm.{lt}", 256)
        wv = self.bf(WQKV).rearrange("p (k m) -> p k m", k=8)
        for t in range(NT):
            pa, pb, pc = (("ps0a", "ps0b", "ps1a") if t % 2 == 0
                          else ("ps1b", "ps2a", "ps2b"))
            pmap = {"ps0a": self.ps[0][:, 0:512], "ps0b": self.ps[0][:, 512:1024],
                    "ps1a": self.ps[1][:, 0:512], "ps1b": self.ps[1][:, 512:1024],
                    "ps2a": self.ps[2][:, 0:512], "ps2b": self.ps[2][:, 512:1024]}
            for seg, (c0, ptok) in enumerate([(0, pa), (512, pb), (1024, pc)]):
                @p.g("tensor", reads=[f"HF.{lt}.{k}" for k in range(8)] + [WQKV],
                     writes=[ptok])
                def _(eng, t=t, c0=c0, ptok=ptok, pmap=pmap):
                    last = None
                    for k in range(8):
                        last = eng.matmul(
                            pmap[ptok],
                            self.bf(HF, 128, k * 2048 + t * 128),
                            wv[:, k, c0:c0 + 512],
                            start=(k == 0), stop=(k == 7))
                    return last
            @p.g("vector", reads=[pc], writes=[f"VT.{lt}.{t}"])
            def _(eng, t=t, pc=pc, pmap=pmap):
                return eng.tensor_copy(self.bf(VT, 512, t * 512), pmap[pc])
            @p.g("scalar", reads=[pa, pb], writes=[SQ])
            def _(eng, pa=pa, pb=pb, pmap=pmap, t=t):
                eng.activation(self.f32(SQ, 512, 0), pmap[pa], AF.Square)
                return eng.activation(self.f32(SQ, 512, 512), pmap[pb], AF.Square)

            @p.g("vector", reads=[SQ], writes=[f"st.nrm.{lt}.{t}"])
            def _(eng, t=t):
                return eng.reduce_sum(
                    self.sv(f"nrm.{lt}", 16 * t, 16),
                    self.f32(SQ, 1024).rearrange("p (h c) -> p h c", h=16), axis=AX.X)

            @p.g("vector", reads=[f"st.nrm.{lt}.{t}"], writes=[f"st.nrm.{lt}.{t}"])
            def _(eng, t=t):
                return eng.tensor_scalar(self.sv(f"nrm.{lt}", 16 * t, 16),
                                         self.sv(f"nrm.{lt}", 16 * t, 16),
                                         1e-24, None, ALU.max)

            @p.g("scalar", reads=[f"st.nrm.{lt}.{t}"], writes=[f"st.nrm.{lt}.{t}"])
            def _(eng, t=t):
                return eng.activation(self.sv(f"nrm.{lt}", 16 * t, 16),
                                      self.sv(f"nrm.{lt}", 16 * t, 16), AF.Ln)

            @p.g("scalar", reads=[f"st.nrm.{lt}.{t}"], writes=[f"st.nrm.{lt}.{t}"])
            def _(eng, t=t):
                return eng.activation(self.sv(f"nrm.{lt}", 16 * t, 16),
                                      self.sv(f"nrm.{lt}", 16 * t, 16),
                                      AF.Exp, scale=-0.5)
            @p.g("vector", reads=[pa, pb, f"st.nrm.{lt}.{t}"],
                 writes=[f"QT.{lt}.{t}", f"KT.{lt}.{t}"],
                 ptr_reads=[f"st.nrm.{lt}.{t}"])
            def _(eng, t=t, pa=pa, pb=pb, pmap=pmap):
                last = None
                for h in range(8):
                    last = eng.tensor_scalar(
                        self.bf(QT, 64, t * 512 + h * 64),
                        pmap[pa][:, h * 64:(h + 1) * 64],
                        self.sv(f"nrm.{lt}", 16 * t + h, 1), None, ALU.mult)
                for h in range(8):
                    last = eng.tensor_scalar(
                        self.bf(KT, 64, t * 512 + h * 64),
                        pmap[pb][:, h * 64:(h + 1) * 64],
                        self.sv(f"nrm.{lt}", 16 * t + 8 + h, 1), None, ALU.mult)
                return last
        self.tap(f"qt.{lt}", QT, "tok_bf16")
        self.tap(f"kt.{lt}", KT, "tok_bf16")
        self.tap(f"vt.{lt}", VT, "tok_bf16")
        p.free(SQ)
        p.free(HF)
        p.free(WQKV)

        QF = p.alloc(f"QF.{lt}", 4 * 2048 * 2)
        KF = p.alloc(f"KF.{lt}", 4 * 2048 * 2)
        self.transpose_tokmajor(QT, 512, QF, 4, lt)
        self.transpose_tokmajor(KT, 512, KF, 4, lt)
        p.free(QT)
        p.free(KT)
        ROT = p.alloc(f"ROT.{lt}", 4 * 2048 * 4)
        rot_off = {"qa": 0, "qb": 2048, "ka": 4096, "kb": 6144}
        p.dma("sync", lambda eng: [
            eng.dma_start(out=self.f32(ROT, 2048, rot_off["qa"]), in_=self.cst["C_ROTQA"]),
            eng.dma_start(out=self.f32(ROT, 2048, rot_off["qb"]), in_=self.cst["C_ROTQB"]),
            eng.dma_start(out=self.f32(ROT, 2048, rot_off["ka"]), in_=self.cst["C_ROTKA"]),
            eng.dma_start(out=self.f32(ROT, 2048, rot_off["kb"]), in_=self.cst["C_ROTKB"]),
        ], writes=[ROT], n=4)
        for nm, col, s2 in [("qa", f"qs.{l}", 8.0), ("qb", f"qssw.{l}", 8.0),
                            ("ka", f"ks.{l}", None), ("kb", f"kssw.{l}", None)]:
            @p.g("vector", reads=[ROT, "bpack"], writes=[ROT],
                 ptr_reads=["bpack"])
            def _(eng, nm=nm, col=col, s2=s2):
                if s2 is None:
                    return eng.tensor_scalar(self.f32(ROT, 2048, rot_off[nm]),
                                             self.f32(ROT, 2048, rot_off[nm]),
                                             self.bcol(col), None, ALU.mult)
                return eng.tensor_scalar(self.f32(ROT, 2048, rot_off[nm]),
                                         self.f32(ROT, 2048, rot_off[nm]),
                                         self.bcol(col), s2, ALU.mult,
                                         ALU.mult)
        TMP1 = p.alloc(f"TMP1.{lt}", 1024 * 4)
        TMP2 = p.alloc(f"TMP2.{lt}", 1024 * 4)
        for (dstb, ta, tb) in [(QF, "qa", "qb"), (KF, "ka", "kb")]:
            for qt in range(4):
                for hf in range(2):
                    base = qt * 2048 + hf * 1024
                    qraw = self.bf(dstb, 1024, base)
                    @p.g("tensor", reads=[f"{dstb.name}.{qt}", "consts"],
                         writes=["ps0a", "ps0b"])
                    def _(eng, qraw=qraw):
                        eng.matmul(self.ps[0][:, 0:512], self.cview("C_ROTM"),
                                   qraw[:, 0:512], start=True, stop=True)
                        return eng.matmul(self.ps[0][:, 512:1024], self.cview("C_ROTM"),
                                          qraw[:, 512:1024], start=True, stop=True)
                    @p.g("vector", reads=[f"{dstb.name}.{qt}", ROT], writes=[TMP1])
                    def _(eng, qraw=qraw, ta=ta, hf=hf):
                        return eng.tensor_mul(self.f32(TMP1, 1024), qraw,
                                              self.f32(ROT, 1024, rot_off[ta] + hf * 1024))
                    @p.g("vector", reads=["ps0a", "ps0b", ROT], writes=[TMP2])
                    def _(eng, tb=tb, hf=hf):
                        return eng.tensor_mul(self.f32(TMP2, 1024),
                                              self.ps[0][:, 0:1024],
                                              self.f32(ROT, 1024, rot_off[tb] + hf * 1024))
                    @p.g("vector", reads=[TMP1, TMP2], writes=[f"{dstb.name}.{qt}"])
                    def _(eng, qraw=qraw):
                        return eng.tensor_add(qraw, self.f32(TMP1, 1024),
                                              self.f32(TMP2, 1024))
        p.free(TMP1)
        p.free(TMP2)
        p.free(ROT)
        self.tap(f"qf.{lt}", QF, "raw_bf16", rows=128, cols=4 * 2048)
        self.tap(f"kf.{lt}", KF, "raw_bf16", rows=128, cols=4 * 2048)
        WO = p.alloc(f"WO.{lt}", 4 * 1024 * 2)
        wosrc = self.inp["L0_ffn_wo"][l].rearrange("(k p) m -> p k m", p=128)
        p.dma("gpsimd", lambda eng, s=wosrc, b=WO: eng.dma_start(
            out=self.bf(b).rearrange("p (k m) -> p k m", k=4), in_=s),
            writes=[WO], n=1)
        self.local_attention(xb, l, lt, QF, KF, VT, WO)
        self.tap(f"xwo.{lt}", xb, "tok_f32")
        p.free(QF)
        p.free(KF)
        p.free(VT)
        p.free(WO)
        return self.ffn_mlp(xb, l, lt)


    def local_attention(self, xb, l, lt, QF, KF, VT, WO):
        p = self.p
        OF = p.alloc(f"OF.{lt}", 4 * 2048 * 2)
        PL = p.alloc(f"PL.{lt}", 640 * 2)
        PTL = p.alloc(f"PTL.{lt}", 20 * 128 * 2)
        self.stat_cols(f"ls.{lt}", 128)
        self.stat_cols(f"lsr.{lt}", 128)
        for h in range(8):
            qtile = h // 2
            pb = 64 * (h % 2)
            for w in range(4):
                for ti in range(4):
                    tau = 4 * w + ti
                    jb0 = max(0, tau - 4)
                    njb = tau - jb0 + 1
                    ncols = njb * 128
                    idx = h * 16 + w * 4 + ti
                    ps_toks = ["ps0a"] + (["ps0b"] if ncols > 512 else [])
                    lq = self.bf(QF, 128, qtile * 2048 + 128 * tau)
                    lk0 = qtile * 2048 + 128 * jb0

                    @p.g("tensor",
                         reads=[f"QF.{lt}.{qtile}", f"KF.{lt}.{qtile}", "consts"],
                         writes=ps_toks)
                    def _(eng, lq=lq, lk0=lk0, ncols=ncols, njb=njb, pb=pb):
                        last = None
                        c = 0
                        while c < ncols:
                            n = min(512, ncols - c)
                            last = eng.matmul(
                                self.ps[0][:, c:c + n],
                                lq[pb:pb + 64, :],
                                self.bf(KF, n, lk0 + c)[pb:pb + 64, :],
                                start=True, stop=False)
                            c += n
                        last = eng.matmul(self.ps[0][:, ncols - 128:ncols],
                                          self.cview("C_ID"),
                                          self.cview("C_MASK_PREV"),
                                          start=False, stop=True)
                        if njb == 5:
                            last = eng.matmul(self.ps[0][:, 0:128],
                                              self.cview("C_ID"),
                                              self.cview("C_MASK_OWN"),
                                              start=False, stop=True)
                        return last

                    @p.g("scalar", reads=ps_toks, writes=[PL, f"st.ls.{lt}.{idx}"])
                    def _(eng, ncols=ncols, idx=idx):
                        return eng.activation(self.bf(PL, ncols), self.ps[0][:, 0:ncols],
                                              AF.Exp,
                                              accum_out=self.sv(f"ls.{lt}", idx, 1))

                    @p.g("vector", reads=[f"st.ls.{lt}.{idx}"],
                         writes=[f"st.lsr.{lt}.{idx}"])
                    def _(eng, idx=idx):
                        return eng.reciprocal(self.sv(f"lsr.{lt}", idx, 1),
                                              self.sv(f"ls.{lt}", idx, 1))

                    @p.g("vector", reads=[PL, f"st.lsr.{lt}.{idx}"], writes=[PL],
                         ptr_reads=[f"st.lsr.{lt}.{idx}"])
                    def _(eng, ncols=ncols, idx=idx):
                        return eng.tensor_scalar(self.bf(PL, ncols), self.bf(PL, ncols),
                                                 self.sv(f"lsr.{lt}", idx, 1),
                                                 None, ALU.mult)

                    @p.g("tensor", reads=[PL, "consts"], writes=["ps3t"])
                    def _(eng, njb=njb):
                        last = None
                        for jb in range(njb):
                            last = eng.transpose(self.ps3t[:, jb * 128:(jb + 1) * 128],
                                                 self.bf(PL, 128, jb * 128),
                                                 self.cview("C_ID"))
                        return last

                    @p.g("vector", reads=["ps3t"], writes=[PTL])
                    def _(eng, ti=ti, njb=njb, ncols=ncols):
                        return eng.tensor_copy(self.bf(PTL, ncols, ti * 5 * 128),
                                               self.ps3t[:, 0:ncols])
                jb_lo = max(0, 4 * w - 4)
                mm_args = []
                for jb_glob in range(jb_lo, 4 * w + 4):
                    for ti in range(4):
                        tau = 4 * w + ti
                        jb0 = max(0, tau - 4)
                        if not (jb0 <= jb_glob <= tau):
                            continue
                        jbrel = jb_glob - jb0
                        mm_args.append((jb_glob, ti, jbrel))

                @p.g("tensor",
                     reads=[PTL] + [f"VT.{lt}.{jb}" for jb in range(jb_lo, 4 * w + 4)],
                     writes=["ps3a"])
                def _(eng, h=h, w=w, mm_args=mm_args):
                    last = None
                    for n, (jb_glob, ti, jbrel) in enumerate(mm_args):
                        lhs = self.bf(VT, 64, jb_glob * 512 + 64 * h)
                        rhs = self.bf(PTL, 128, (ti * 5 + jbrel) * 128)
                        last = eng.matmul(self.ps3a[0:64, 128 * ti:128 * ti + 128],
                                          lhs, rhs, start=(n == 0),
                                          stop=(n == len(mm_args) - 1))
                    return last

                @p.g("vector", reads=["ps3a"], writes=[f"OF.{lt}.{h // 2}"])
                def _(eng, h=h, w=w, qtile=qtile, pb=pb):
                    dst = self.arena_t[pb:pb + 64,
                                       OF.off + (qtile * 2048 + 512 * w) * 2:
                                       OF.off + (qtile * 2048 + 512 * w + 512) * 2]
                    return eng.tensor_copy(dst.bitcast(BF16), self.ps3a[0:64, :])
        p.free(PL)
        p.free(PTL)
        self.tap(f"of.{lt}", OF, "raw_bf16", rows=128, cols=4 * 2048)
        wo = self.bf(WO).rearrange("p (k m) -> p k m", k=4)
        for t in range(NT):
            pst = self.ps[1] if t % 2 == 0 else self.ps[2]
            ptoks = [f"ps{1 if t % 2 == 0 else 2}a", f"ps{1 if t % 2 == 0 else 2}b"]

            @p.g("tensor", reads=[f"OF.{lt}.{k}" for k in range(4)] + [WO],
                 writes=ptoks)
            def _(eng, t=t, pst=pst):
                last = None
                for c in range(2):
                    for k in range(4):
                        last = eng.matmul(pst[:, c * 512:(c + 1) * 512],
                                          self.bf(OF, 128, k * 2048 + 128 * t),
                                          wo[:, k, c * 512:(c + 1) * 512],
                                          start=(k == 0), stop=(k == 3))
                return last

            @p.g("vector", reads=ptoks + [f"{xb.name}.{t}"],
                 writes=[f"{xb.name}.{t}"])
            def _(eng, t=t, pst=pst):
                return eng.tensor_add(self.f32(xb, 1024, t * 1024),
                                      self.f32(xb, 1024, t * 1024), pst[:, 0:1024])
        p.free(OF)

    def ffn_mlp(self, xb, l, lt):
        p = self.p
        sq = p.alloc(f"sq2.{lt}", 1024 * 4)
        c_slo = self.stat_cols(f"slo.{lt}", 16)
        c_shi = self.stat_cols(f"shi.{lt}", 16)
        c_qhi = self.stat_cols(f"qhi.{lt}", 16)
        c_qlo = self.stat_cols(f"qlo.{lt}", 16)
        c_shis = self.stat_cols(f"shis.{lt}", 16)
        c_qhis = self.stat_cols(f"qhis.{lt}", 16)
        for t in range(NT):
            xt = self.f32(xb, 1024, t * 1024)

            @p.g("scalar", reads=[f"{xb.name}.{t}"], writes=[sq])
            def _(eng, xt=xt):
                return eng.activation(self.f32(sq, 1024), xt, AF.Square)

            @p.g("vector", reads=[sq], writes=[f"st.qh.{lt}"])
            def _(eng, t=t):
                eng.reduce_sum(self.sv(f"qlo.{lt}", t, 1),
                               self.f32(sq, 512, 0), axis=AX.X)
                return eng.reduce_sum(self.sv(f"qhi.{lt}", t, 1),
                                      self.f32(sq, 512, 512), axis=AX.X)

            @p.g("vector", reads=[f"{xb.name}.{t}"], writes=[f"st.sh.{lt}"])
            def _(eng, xt=xt, t=t):
                eng.reduce_sum(self.sv(f"slo.{lt}", t, 1), xt[:, 0:512], axis=AX.X)
                return eng.reduce_sum(self.sv(f"shi.{lt}", t, 1),
                                      xt[:, 512:1024], axis=AX.X)
        p.free(sq)
        hi0 = self._stat_map[f"shi.{lt}"][0]
        his0 = self._stat_map[f"shis.{lt}"][0]
        stat_bytes = lambda c: self.STAT.off + c * 4

        def emit_shift(eng):
            a = eng.dma_start(
                out=self.arena_t[1:128, stat_bytes(his0):stat_bytes(his0 + 32)].bitcast(F32),
                in_=self.arena_t[0:127, stat_bytes(hi0):stat_bytes(hi0 + 32)].bitcast(F32))
            b = eng.dma_start(
                out=self.arena_t[0:1, stat_bytes(his0):stat_bytes(his0 + 32)].bitcast(F32)
                .rearrange("p (a c) -> p a c", a=2)[:, :, 1:16],
                in_=self.arena_t[127:128, stat_bytes(hi0):stat_bytes(hi0 + 32)].bitcast(F32)
                .rearrange("p (a c) -> p a c", a=2)[:, :, 0:15])
            return [a, b]
        p.dma("sync", emit_shift, reads=[f"st.sh.{lt}", f"st.qh.{lt}"],
              writes=[f"st.shs.{lt}"], n=2)

        @p.g("vector", reads=[f"st.shs.{lt}"], writes=[f"st.shs.{lt}"])
        def _(eng):
            return eng.memset(
                self.arena_t[0:1, stat_bytes(his0):stat_bytes(his0 + 32)].bitcast(F32)
                .rearrange("p (a c) -> p a c", a=2)[:, :, 0:1], 0.0)

        self.stat_cols(f"s.{lt}", 16)
        self.stat_cols(f"q.{lt}", 16)

        @p.g("vector", reads=[f"st.sh.{lt}", f"st.shs.{lt}"],
             writes=[f"st.s.{lt}", f"st.q.{lt}"])
        def _(eng):
            eng.tensor_add(self.sv(f"s.{lt}"), self.sv(f"slo.{lt}"),
                           self.sv(f"shis.{lt}"))
            return eng.tensor_add(self.sv(f"q.{lt}"), self.sv(f"qlo.{lt}"),
                                  self.sv(f"qhis.{lt}"))
        self.ln_finalize(lt)
        c_r = self._stat_map[f"r.{lt}"][0]
        c_negmr = self._stat_map[f"negmr.{lt}"][0]
        assert c_negmr == c_r + 16
        c_rs = self.stat_cols(f"rs.{lt}", 16)
        c_negmrs = self.stat_cols(f"negmrs.{lt}", 16)
        assert c_negmrs == c_rs + 16

        def emit_shift2(eng):
            a = eng.dma_start(
                out=self.arena_t[0:127, stat_bytes(c_rs):stat_bytes(c_rs + 32)].bitcast(F32),
                in_=self.arena_t[1:128, stat_bytes(c_r):stat_bytes(c_r + 32)].bitcast(F32))
            b = eng.dma_start(
                out=self.arena_t[127:128, stat_bytes(c_rs):stat_bytes(c_rs + 32)].bitcast(F32)
                .rearrange("p (a c) -> p a c", a=2)[:, :, 0:15],
                in_=self.arena_t[0:1, stat_bytes(c_r):stat_bytes(c_r + 32)].bitcast(F32)
                .rearrange("p (a c) -> p a c", a=2)[:, :, 1:16])
            c1 = eng.dma_start(
                out=self.arena_t[127:128, stat_bytes(c_rs + 15):stat_bytes(c_rs + 16)].bitcast(F32),
                in_=self.inp["BPACK"][127:128, BPACK_MAP["c_eps"][0]:BPACK_MAP["c_eps"][0] + 1])
            c2 = eng.dma_start(
                out=self.arena_t[127:128, stat_bytes(c_negmrs + 15):stat_bytes(c_negmrs + 16)].bitcast(F32),
                in_=self.inp["BPACK"][127:128, BPACK_MAP["c_eps"][0]:BPACK_MAP["c_eps"][0] + 1])
            return [a, b, c1, c2]
        p.dma("sync", emit_shift2, reads=[f"st.r.{lt}", f"st.negmr.{lt}"],
              writes=[f"st.rs.{lt}"], n=4)

        XS = p.alloc(f"XS.{lt}", 16 * 1024 * 2)
        for t in range(NT):
            xt = self.f32(xb, 1024, t * 1024)

            @p.g("scalar", reads=[f"{xb.name}.{t}", f"st.rs.{lt}"],
                 writes=[f"XS.{lt}.{t}"],
                 ptr_reads=[f"st.r.{lt}", f"st.negmr.{lt}", f"st.rs.{lt}"])
            def _(eng, xt=xt, t=t):
                eng.activation(self.bf(XS, 512, t * 1024), xt[:, 0:512], AF.Identity,
                               scale=self.sv(f"r.{lt}", t, 1),
                               bias=self.sv(f"negmr.{lt}", t, 1))
                return eng.activation(self.bf(XS, 512, t * 1024 + 512),
                                      xt[:, 512:1024], AF.Identity,
                                      scale=self.sv(f"rs.{lt}", t, 1),
                                      bias=self.sv(f"negmrs.{lt}", t, 1))
        HF2 = p.alloc(f"HF2.{lt}", 8 * 2048 * 2)
        self.transpose_tokmajor(
            XS, 1024, HF2, 8, lt, shift_ftiles=(4, 5, 6, 7),
            affine=(lambda ft: self.bcol(f"lng.{l}", ft),
                    lambda ft: self.bcol(f"lnb.{l}", ft)))
        p.free(XS)
        self.tap(f"hf2.{lt}", HF2, "raw_bf16", rows=128, cols=8 * 2048)
        self.spill(xb, self.sp_xb, f"mlp.{lt}")
        p.free(xb)

        H2 = p.alloc(f"H2.{lt}", 22 * 2048 * 2)
        GEL = p.alloc(f"GEL.{lt}", 1024 * 2)
        w1 = self.inp["L0_ffn_w1"][l]
        slab_cols = {}

        def load_w1_half(half):
            a0, a1 = (0, 1408) if half == 0 else (1408, 2730)
            g0, g1 = (2730 + a0, 2730 + a1)
            wa = p.alloc(f"W1A{half}.{lt}", 8 * (a1 - a0) * 2)
            wg = p.alloc(f"W1G{half}.{lt}", 8 * (a1 - a0) * 2)
            p.dma("gpsimd", lambda eng, wa=wa, a0=a0, a1=a1: eng.dma_start(
                out=self.bf(wa, 8 * (a1 - a0)).rearrange("p (k m) -> p k m", k=8),
                in_=w1[:, a0:a1].rearrange("(k p) m -> p k m", p=128)),
                writes=[wa], n=1)
            p.dma("gpsimd", lambda eng, wg=wg, g0=g0, g1=g1: eng.dma_start(
                out=self.bf(wg, 8 * (g1 - g0)).rearrange("p (k m) -> p k m", k=8),
                in_=w1[:, g0:g1].rearrange("(k p) m -> p k m", p=128)),
                writes=[wg], n=1)
            slab_cols[half] = (wa, wg, a0, a1)
        load_w1_half(0)
        for mp in range(22):
            half = 0 if mp < 11 else 1
            wa, wg, a0, a1 = slab_cols[half]
            c0 = mp * 128 - a0
            mw = min(128, 2730 - mp * 128)
            wav = self.bf(wa, 8 * (a1 - a0)).rearrange("p (k m) -> p k m", k=8)
            wgv = self.bf(wg, 8 * (a1 - a0)).rearrange("p (k m) -> p k m", k=8)
            for th in range(2):
                pa = self.ps[0] if (mp + th) % 2 == 0 else self.ps[1]
                pg = self.ps[1] if (mp + th) % 2 == 0 else self.ps[0]
                patok = ["ps0a", "ps0b"] if (mp + th) % 2 == 0 else ["ps1a", "ps1b"]
                pgtok = ["ps1a", "ps1b"] if (mp + th) % 2 == 0 else ["ps0a", "ps0b"]

                @p.g("tensor", reads=[wa, wg] + [f"HF2.{lt}.{k}" for k in range(8)],
                     writes=patok + pgtok)
                def _(eng, wav=wav, wgv=wgv, c0=c0, mw=mw, th=th, pa=pa, pg=pg):
                    last = None
                    for dst, wsl in ((pa, wav), (pg, wgv)):
                        for c in range(2):
                            tc0 = th * 1024 + c * 512
                            for k in range(8):
                                last = eng.matmul(
                                    dst[0:mw, c * 512:(c + 1) * 512],
                                    wsl[:, k, c0:c0 + mw],
                                    self.bf(HF2, 512, k * 2048 + tc0),
                                    start=(k == 0), stop=(k == 7))
                    return last

                @p.g("scalar", reads=pgtok + ["bpack"], writes=[GEL])
                def _(eng, pg=pg, mp=mp, mw=mw):
                    return eng.activation(self.bf(GEL, 1024)[0:mw, :],
                                          pg[0:mw, 0:1024], AF.Gelu,
                                          bias=self.bcol(f"b1g.{l}", mp)[0:mw, :])

                @p.g("vector", reads=patok + [GEL, "bpack"],
                     writes=[f"H2.{lt}.{mp}"])
                def _(eng, pa=pa, mp=mp, mw=mw, th=th):
                    return eng.scalar_tensor_tensor(
                        self.bf(H2, 1024, mp * 2048 + th * 1024)[0:mw, :],
                        pa[0:mw, 0:1024], self.bcol(f"b1a.{l}", mp)[0:mw, :],
                        self.bf(GEL, 1024)[0:mw, :], ALU.add, ALU.mult)
            if mp == 10:
                p.free(slab_cols[0][0])
                p.free(slab_cols[0][1])
                load_w1_half(1)
        p.free(slab_cols[1][0])
        p.free(slab_cols[1][1])
        p.free(GEL)
        p.free(HF2)
        self.tap(f"h2.{lt}", H2, "raw_bf16", rows=128, cols=22 * 2048)

        xb2 = p.alloc(f"X2.{lt}", T * D * 4 // 128)
        dst = self.f32(xb2).rearrange("p (t d) -> p t d", t=NT)
        srcd = self.sp_xb.rearrange("(t p) d -> p t d", p=128)
        p.dma("sync", lambda eng: eng.dma_start(out=dst, in_=srcd),
              reads=[f"dram.mlp.{lt}"],
              writes=[xb2] + [f"{xb2.name}.{t}" for t in range(NT)], n=1)
        TMPW = p.alloc(f"TMPW.{lt}", 512 * 4)
        B2B = self.load_bbc(f"b2f.{l}")
        w2 = self.inp["L0_ffn_w2"][l]
        WQN = 4
        wcols = 1024 // WQN
        for half in range(WQN):
            m0 = half * wcols
            ws = p.alloc(f"W2H{half}.{lt}", 22 * wcols * 2)

            def emit_w2(eng, ws=ws, m0=m0):
                a = eng.dma_start(
                    out=self.bf(ws).rearrange("p (k m) -> p k m", k=22)[:, 0:21, :],
                    in_=w2[0:2688, m0:m0 + wcols].rearrange("(k p) m -> p k m", p=128))
                b = eng.dma_start(
                    out=self.bf(ws).rearrange("p (k m) -> p k m", k=22)[0:42, 21:22, :],
                    in_=w2[2688:2730, m0:m0 + wcols].rearrange("(k p) m -> p k m", p=42))
                return [a, b]
            p.dma("gpsimd", emit_w2, writes=[ws], n=2)
            wsv = self.bf(ws).rearrange("p (k m) -> p k m", k=22)
            for t in range(NT):
                pst = "ps2a" if t % 2 == 0 else "ps2b"
                psv = (self.ps[2][:, 0:wcols] if t % 2 == 0
                       else self.ps[2][:, 512:512 + wcols])

                @p.g("tensor", reads=[ws] + [f"H2.{lt}.{k}" for k in range(22)],
                     writes=[pst])
                def _(eng, t=t, psv=psv, wsv=wsv):
                    last = None
                    for k in range(22):
                        kw = min(128, 2730 - k * 128)
                        last = eng.matmul(
                            psv,
                            self.bf(H2, 128, k * 2048 + t * 128)[0:kw, :],
                            wsv[0:kw, k, :],
                            start=(k == 0), stop=(k == 21))
                    return last

                @p.g("vector", reads=[pst, B2B], writes=[TMPW])
                def _(eng, psv=psv, half=half):
                    return eng.tensor_add(self.f32(TMPW, wcols), psv,
                                          self.bbc_view(B2B, half * wcols, wcols))

                @p.g("vector", reads=[TMPW, f"{xb2.name}.{t}"],
                     writes=[f"{xb2.name}.{t}"])
                def _(eng, t=t, m0=m0):
                    xv = self.f32(xb2, wcols, t * 1024 + m0)
                    return eng.tensor_add(xv, xv, self.f32(TMPW, wcols))
            p.free(ws)
        p.free(TMPW)
        p.free(B2B)
        p.free(H2)
        self.tap(f"xmlp.{lt}", xb2, "tok_f32")
        return xb2


    def tap(self, name, buf, kind, rows=128, cols=None):
        if name not in self.debug_taps:
            return
        p = self.p
        if kind == "tok_f32":
            o = self.nc.dram_tensor(f"tap_{name}", [T, D], F32,
                                    kind="ExternalOutput").ap()
            srcv = self.f32(buf).rearrange("p (t d) -> p t d", t=NT)
            dstv = o.rearrange("(t p) d -> p t d", p=128)
            rd = [buf] + [f"{buf.name}.{t}" for t in range(NT)]
        elif kind == "tok_bf16":
            o = self.nc.dram_tensor(f"tap_{name}", [T, buf.nbytes // 2 // NT], BF16,
                                    kind="ExternalOutput").ap()
            srcv = self.bf(buf).rearrange("p (t d) -> p t d", t=NT)
            dstv = o.rearrange("(t p) d -> p t d", p=128)
            rd = [buf] + [f"{buf.name}.{t}" for t in range(NT)]
        elif kind == "raw_bf16":
            cols = cols or buf.nbytes // 2
            o = self.nc.dram_tensor(f"tap_{name}", [rows, cols], BF16,
                                    kind="ExternalOutput").ap()
            srcv = self.bf(buf, cols)[0:rows, :]
            dstv = o
            rd = [buf] + [f"{buf.name}.{t}" for t in range(24)]
        else:
            cols = cols or buf.nbytes // 4
            o = self.nc.dram_tensor(f"tap_{name}", [rows, cols], F32,
                                    kind="ExternalOutput").ap()
            srcv = self.f32(buf, cols)[0:rows, :]
            dstv = o
            rd = [buf] + [f"{buf.name}.{t}" for t in range(24)]
        p.dma("sync", lambda eng: eng.dma_start(out=dstv, in_=srcv),
              reads=rd, writes=[f"tap.{name}"], n=1)
        self.taps_done.append(name)

    def transp_weight(self, wname, oc, ic, tag, oc_group=None):
        p = self.p
        O, I = oc * 128, ic * 128
        og = oc_group or oc
        wt = p.alloc(f"WT.{tag}", ic * O * 2)
        for g0 in range(0, oc, og):
            n_oc = min(og, oc - g0)
            slab = p.alloc(f"WN.{tag}.{g0}", n_oc * I * 2)
            src = self.inp[wname][g0 * 128:(g0 + n_oc) * 128].rearrange(
                "(k p) m -> p k m", p=128)
            p.dma("gpsimd", lambda eng, slab=slab, src=src, n_oc=n_oc: eng.dma_start(
                out=self.bf(slab).rearrange("p (k m) -> p k m", k=n_oc), in_=src),
                writes=[slab], n=1)
            for icc in range(ic):
                for g in range(ceil_div(n_oc, 8)):
                    n = min(8, n_oc - g * 8)

                    @p.g("tensor", reads=[slab, "consts"], writes=["ps3t"])
                    def _(eng, icc=icc, g=g, n=n, slab=slab):
                        last = None
                        for s in range(n):
                            last = eng.transpose(
                                self.ps3t[:, s * 128:(s + 1) * 128],
                                self.bf(slab, 128, (g * 8 + s) * I + icc * 128),
                                self.cview("C_ID"))
                        return last

                    @p.g("vector", reads=["ps3t"], writes=[wt])
                    def _(eng, icc=icc, g=g, n=n, g0=g0):
                        return eng.tensor_copy(
                            self.bf(wt, n * 128, icc * O + (g0 + g * 8) * 128),
                            self.ps3t[:, 0:n * 128])
            p.free(slab)
        return wt

    def attn_sublayer(self, xa):
        p = self.p
        tag = "AT"
        self.ln_stats_plain(xa, tag)
        self.ln_finalize(tag, extra_r2=True)
        yb = p.alloc("Y.AT", T * D * 4 // 128)
        for t in range(NT):
            @p.g("scalar", reads=[f"{xa.name}.{t}"], writes=[f"Y.AT.{t}"],
                 ptr_reads=[f"st.r.{tag}", f"st.negmr.{tag}"])
            def _(eng, t=t):
                return eng.activation(self.f32(yb, 1024, t * 1024),
                                      self.f32(xa, 1024, t * 1024), AF.Identity,
                                      scale=self.sv(f"r.{tag}", t, 1),
                                      bias=self.sv(f"negmr.{tag}", t, 1))
        self.spill(yb, self.sp_xb, "res")
        p.free(yb)
        XN = p.alloc("XN.AT", 16 * 1024 * 2)
        for t in range(NT):
            @p.g("scalar", reads=[f"{xa.name}.{t}", f"st.r2.{tag}"],
                 writes=[f"XN.AT.{t}"],
                 ptr_reads=[f"st.r2.{tag}", f"st.negmr2.{tag}"])
            def _(eng, t=t):
                return eng.activation(self.bf(XN, 1024, t * 1024),
                                      self.f32(xa, 1024, t * 1024), AF.Identity,
                                      scale=self.sv(f"r2.{tag}", t, 1),
                                      bias=self.sv(f"negmr2.{tag}", t, 1))
        self.spill(xa, self.sp_xa, "x1")
        p.free(xa)
        XNF = p.alloc("XNF.AT", 8 * 2048 * 2)
        self.transpose_tokmajor(
            XN, 1024, XNF, 8, tag,
            affine=(lambda ft: self.bcol("g1", ft), lambda ft: self.bcol("b1", ft)))
        p.free(XN)
        self.tap("xnf", XNF, "raw_bf16", rows=128, cols=8 * 2048)
        WQT = self.transp_weight("L0_attn_Wq", 8, 8, "WQT")
        WKT = self.transp_weight("L0_attn_Wk", 8, 8, "WKT")
        WVT = self.transp_weight("L0_attn_Wv", 4, 8, "WVT")
        WQVT = self.transp_weight("L0_attn_Wqv", 4, 8, "WQVT")
        QFg = p.alloc("QFg", 8 * 2048 * 2)
        KFg = p.alloc("KFg", 8 * 2048 * 2)
        xnf = lambda k, c, n=512: self.bf(XNF, n, k * 2048 + c)
        xnf_tok = [f"XNF.AT.{k}" for k in range(8)]
        for m in range(8):
            for half in range(2):
                ps_q = self.ps[0] if half == 0 else self.ps[1]
                qtok = ["ps0a", "ps0b"] if half == 0 else ["ps1a", "ps1b"]

                @p.g("tensor", reads=[WQT] + xnf_tok, writes=qtok)
                def _(eng, m=m, half=half, ps_q=ps_q):
                    last = None
                    for c in range(2):
                        for k in range(8):
                            last = eng.matmul(
                                ps_q[:, c * 512:(c + 1) * 512],
                                self.bf(WQT, 128, k * 1024 + m * 128),
                                xnf(k, half * 1024 + c * 512),
                                start=(k == 0), stop=(k == 7))
                    return last

                @p.g("vector", reads=qtok + ["bpack"], writes=[f"QFg.{m}"])
                def _(eng, m=m, half=half, ps_q=ps_q):
                    return eng.tensor_scalar(
                        self.bf(QFg, 1024, m * 2048 + half * 1024),
                        ps_q[:, 0:1024], self.bcol("bq", m), 0.125,
                        ALU.add, ALU.mult)

                ps_k = self.ps[1] if half == 0 else self.ps[0]
                ktok = ["ps1a", "ps1b"] if half == 0 else ["ps0a", "ps0b"]

                @p.g("tensor", reads=[WKT] + xnf_tok, writes=ktok)
                def _(eng, m=m, half=half, ps_k=ps_k):
                    last = None
                    for c in range(2):
                        for k in range(8):
                            last = eng.matmul(
                                ps_k[:, c * 512:(c + 1) * 512],
                                self.bf(WKT, 128, k * 1024 + m * 128),
                                xnf(k, half * 1024 + c * 512),
                                start=(k == 0), stop=(k == 7))
                    return last

                @p.g("scalar", reads=ktok + ["bpack"], writes=[f"KFg.{m}"])
                def _(eng, m=m, half=half, ps_k=ps_k):
                    return eng.activation(
                        self.bf(KFg, 1024, m * 2048 + half * 1024),
                        ps_k[:, 0:1024], AF.Identity, bias=self.bcol("bk", m))
        p.free(WQT)
        p.free(WKT)
        VTg = p.alloc("VTg", 16 * 512 * 2)
        BVB = self.load_bbc("bv")
        for t in range(NT):
            pv = self.ps[2][:, 0:512] if t % 2 == 0 else self.ps[2][:, 512:1024]
            ptok = "ps2a" if t % 2 == 0 else "ps2b"

            @p.g("tensor", reads=[WVT] + xnf_tok, writes=[ptok])
            def _(eng, t=t, pv=pv):
                last = None
                for k in range(8):
                    last = eng.matmul(pv, xnf(k, t * 128, 128),
                                      self.bf(WVT, 512, k * 512),
                                      start=(k == 0), stop=(k == 7))
                return last

            @p.g("vector", reads=[ptok, BVB], writes=[f"VTg.{t}"])
            def _(eng, t=t, pv=pv):
                return eng.tensor_add(self.bf(VTg, 512, t * 512), pv,
                                      self.bbc_view(BVB))
        p.free(WVT)
        p.free(BVB)
        QVF = p.alloc("QVF", 4 * 2048 * 2)
        for m in range(4):
            for half in range(2):
                pq = self.ps[0] if half == 0 else self.ps[1]
                qtok = ["ps0a", "ps0b"] if half == 0 else ["ps1a", "ps1b"]

                @p.g("tensor", reads=[WQVT] + xnf_tok, writes=qtok)
                def _(eng, m=m, half=half, pq=pq):
                    last = None
                    for c in range(2):
                        for k in range(8):
                            last = eng.matmul(
                                pq[:, c * 512:(c + 1) * 512],
                                self.bf(WQVT, 128, k * 512 + m * 128),
                                xnf(k, half * 1024 + c * 512),
                                start=(k == 0), stop=(k == 7))
                    return last

                @p.g("vector", reads=qtok + ["bpack"], writes=[f"QVF.{m}"])
                def _(eng, m=m, half=half, pq=pq):
                    return eng.tensor_scalar(
                        self.bf(QVF, 1024, m * 2048 + half * 1024),
                        pq[:, 0:1024], self.bcol("bqv", m),
                        1.0 / math.sqrt(32.0), ALU.add, ALU.mult)
        p.free(WQVT)
        p.free(XNF)
        VWS9 = p.alloc("VWS9", 16 * 9 * 2)
        TMPV = p.alloc("TMPV", 512 * 4)
        VWS8 = p.alloc("VWS8", 8 * 4)
        WSB = self.load_bbc("wsout")

        @p.g("vector", writes=["VWS9.ones"])
        def _(eng):
            return eng.memset(
                self.bf(VWS9, 144).rearrange("p (t n) -> p t n", t=16)[:, :, 8:9],
                1.0)
        for jt in range(NT):
            @p.g("vector", reads=[f"VTg.{jt}", WSB], writes=[TMPV])
            def _(eng, jt=jt):
                return eng.tensor_mul(self.f32(TMPV, 512),
                                      self.bf(VTg, 512, jt * 512),
                                      self.bbc_view(WSB))

            @p.g("vector", reads=[TMPV], writes=["VWS8.x"])
            def _(eng, jt=jt):
                return eng.reduce_sum(
                    self.f32(VWS8, 8),
                    self.f32(TMPV, 512).rearrange("p (h c) -> p h c", h=8),
                    axis=AX.X)

            @p.g("vector", reads=["VWS8.x"], writes=[f"VWS9.{jt}"])
            def _(eng, jt=jt):
                return eng.tensor_copy(self.bf(VWS9, 8, jt * 9), self.f32(VWS8, 8))
        p.free(TMPV)
        p.free(VWS8)
        p.free(WSB)
        QVD = p.alloc("QVD", 4 * 2048 * 2)
        for qq in range(4):
            for ib in range(4):
                @p.g("tensor", reads=[f"QVF.{qq}", self.WSBLK], writes=["ps3a"])
                def _(eng, qq=qq, ib=ib):
                    return eng.matmul(self.ps3a[0:4, :], self.bf(self.WSBLK, 4),
                                      self.bf(QVF, 512, qq * 2048 + ib * 512),
                                      start=True, stop=True)

                @p.g("scalar", reads=["ps3a"], writes=[f"QVD.{qq}.{ib}"])
                def _(eng, qq=qq, ib=ib):
                    return eng.copy(
                        self.bf(QVD, 512, qq * 2048 + ib * 512)[0:4, :],
                        self.ps3a[0:4, :])
        p.free(QVF)
        self.tap("qfg", QFg, "raw_bf16", rows=128, cols=8 * 2048)
        self.tap("kfg", KFg, "raw_bf16", rows=128, cols=8 * 2048)
        self.tap("vtg", VTg, "tok_bf16")
        return self.attn_main(VTg, QFg, KFg, VWS9, QVD)

    def attn_main(self, VTg, QFg, KFg, VWS9, QVD):
        p = self.p
        XMF = p.alloc("XMF", 8 * 2048 * 2)
        PTB = [p.alloc("PTB0", 16 * 512 * 2), p.alloc("PTB1", 16 * 512 * 2)]
        USB = p.alloc("USB", 4 * 512 * 4)
        C9 = p.alloc("C9", 512 * 4)
        RECNF = p.alloc("RECNF", 512 * 4)
        RECNB = p.alloc("RECNB", 512 * 2)
        ZA = p.alloc("ZA", 512 * 4)
        E8 = p.alloc("E8", 512 * 2)
        RECENB = p.alloc("RECENB", 512 * 2)
        M8 = p.alloc("M8", 512 * 2)
        BCC = p.alloc("BCC", 2 * 512 * 4)
        TACC = p.alloc("TACC", 512 * 4)
        TC = p.alloc("TC", 512 * 4)
        ps3tf = self.ps3t[:, 0:1024].bitcast(F32)
        utoks = ["ps0a", "ps0b", "ps1a", "ps1b"]
        ups = [self.ps[0][:, 0:512], self.ps[0][:, 512:1024],
               self.ps[1][:, 0:512], self.ps[1][:, 512:1024]]
        vtg_toks = [f"VTg.{jt}" for jt in range(NT)]
        vws_toks = [f"VWS9.{jt}" for jt in range(NT)] + ["VWS9.ones"]

        iters = [(h, ib) for h in range(16) for ib in range(4)]
        nS = len(iters)

        def st_thunks(s):
            h, ib = iters[s]
            ft, pb = h // 2, 64 * (h % 2)
            par = s % 2
            out = []
            for jt in range(NT):
                pst = "ps2a" if jt % 2 == 0 else "ps2b"
                psv = (self.ps[2][:, 0:512] if jt % 2 == 0
                       else self.ps[2][:, 512:1024])

                def mm(jt=jt, pst=pst, psv=psv):
                    @p.g("tensor", reads=[f"KFg.{ft}", f"QFg.{ft}"], writes=[pst])
                    def _(eng):
                        return eng.matmul(
                            psv,
                            self.bf(KFg, 128, ft * 2048 + jt * 128)[pb:pb + 64, :],
                            self.bf(QFg, 512, ft * 2048 + ib * 512)[pb:pb + 64, :],
                            start=True, stop=True, tile_position=(pb, 0))

                def ex(jt=jt, pst=pst, psv=psv):
                    @p.g("scalar", reads=[pst], writes=[f"PTB{par}.{jt}"])
                    def _(eng):
                        return eng.activation(self.bf(PTB[par], 512, jt * 512),
                                              psv, AF.Exp)
                out.append(mm)
                out.append(ex)
            return out

        def pv_thunks(s):
            par = s % 2
            ptb_toks = [f"PTB{par}.{jt}" for jt in range(NT)]
            out = []
            for c in range(4):
                def g(c=c):
                    @p.g("tensor", reads=ptb_toks + vtg_toks, writes=[utoks[c]])
                    def _(eng):
                        last = None
                        for jt in range(NT):
                            last = eng.matmul(
                                ups[c],
                                self.bf(VTg, 128, jt * 512 + c * 128),
                                self.bf(PTB[par], 512, jt * 512),
                                start=(jt == 0), stop=(jt == NT - 1))
                        return last
                out.append(g)

            def gcombo():
                @p.g("tensor", reads=ptb_toks + vws_toks, writes=["ps3a"])
                def _(eng):
                    last = None
                    for jt in range(NT):
                        last = eng.matmul(
                            self.ps3a[0:9, :],
                            self.bf(VWS9, 9, jt * 9),
                            self.bf(PTB[par], 512, jt * 512),
                            start=(jt == 0), stop=(jt == NT - 1))
                    return last
            out.append(gcombo)
            return out

        def uevac_thunks(s):
            out = []
            for c in range(4):
                def g(c=c):
                    @p.g("vector", reads=[utoks[c]], writes=[f"USB.{c}"])
                    def _(eng):
                        return eng.tensor_copy(self.f32(USB, 512, c * 512),
                                               ups[c])
                out.append(g)
            return out

        def epi_thunks(s):
            h, ib = iters[s]
            ft, pb = h // 2, 64 * (h % 2)
            qq, hm = h // 4, h % 4
            out = []

            def c9_evac():
                @p.g("scalar", reads=["ps3a"], writes=["C9"])
                def _(eng):
                    return eng.copy(self.f32(C9, 512)[0:9, :], self.ps3a[0:9, :])

            def nrow_dma():
                def emit(eng):
                    return eng.dma_start(
                        out=self.arena_t[0:1, RECNF.off:RECNF.off + 2048].bitcast(F32),
                        in_=self.arena_t[8:9, C9.off:C9.off + 2048].bitcast(F32))
                p.dma("gpsimd", emit, reads=["C9"], writes=["RECNF.raw"], n=1)

            def recn_f():
                @p.g("vector", reads=["RECNF.raw"], writes=["RECNF"])
                def _(eng):
                    return eng.reciprocal(self.f32(RECNF, 512)[0:1, :],
                                          self.f32(RECNF, 512)[0:1, :])

            def recn_b():
                @p.g("vector", reads=["RECNF"], writes=["RECNB"])
                def _(eng):
                    return eng.tensor_copy(self.bf(RECNB, 512)[0:1, :],
                                           self.f32(RECNF, 512)[0:1, :])

            def bc_recn():
                @p.g("tensor", reads=["RECNB", "consts"], writes=["ps3t"])
                def _(eng):
                    return eng.matmul(ps3tf[0:8, :],
                                      self.cview("C_ONES1")[0:1, 0:8],
                                      self.bf(RECNB, 512)[0:1, :],
                                      start=True, stop=True)

            def za():
                @p.g("vector", reads=["C9", "ps3t"], writes=["ZA"])
                def _(eng):
                    return eng.tensor_mul(self.f32(ZA, 512)[0:8, :],
                                          self.f32(C9, 512)[0:8, :],
                                          ps3tf[0:8, :])

            def bc_qvd():
                @p.g("tensor", reads=[f"QVD.{qq}.{ib}", "consts"], writes=["ps3t"])
                def _(eng):
                    return eng.matmul(
                        ps3tf[0:8, :],
                        self.cview("C_IND4")[0:4, hm * 8:hm * 8 + 8],
                        self.bf(QVD, 512, qq * 2048 + ib * 512)[0:4, :],
                        start=True, stop=True)

            def z_add():
                @p.g("vector", reads=["ZA", "ps3t"], writes=["ZA"])
                def _(eng):
                    return eng.tensor_add(self.f32(ZA, 512)[0:8, :],
                                          self.f32(ZA, 512)[0:8, :],
                                          ps3tf[0:8, :])

            def e_exp():
                @p.g("scalar", reads=["ZA", "bpack"], writes=["E8"],
                     ptr_reads=["bpack"])
                def _(eng):
                    return eng.activation(self.bf(E8, 512)[0:8, :],
                                          self.f32(ZA, 512)[0:8, :], AF.Exp,
                                          bias=self.bcol("bs")[0:8, :])

            def e_sum():
                @p.g("tensor", reads=["E8", "consts"], writes=["ps3t"])
                def _(eng):
                    return eng.matmul(ps3tf[0:1, :],
                                      self.cview("C_IND8")[0:8, 512:513],
                                      self.bf(E8, 512)[0:8, :],
                                      start=True, stop=True)

            def recip_e():
                @p.g("vector", reads=["ps3t"], writes=["ZA"])
                def _(eng):
                    return eng.reciprocal(self.f32(ZA, 512)[0:1, :],
                                          ps3tf[0:1, :])

            def recen_b():
                @p.g("vector", reads=["ZA", "RECNF"], writes=["RECENB"])
                def _(eng):
                    return eng.tensor_mul(self.bf(RECENB, 512)[0:1, :],
                                          self.f32(ZA, 512)[0:1, :],
                                          self.f32(RECNF, 512)[0:1, :])

            def bc_recen():
                @p.g("tensor", reads=["RECENB", "consts"], writes=["ps3t"])
                def _(eng):
                    return eng.matmul(ps3tf[0:8, :],
                                      self.cview("C_ONES1")[0:1, 0:8],
                                      self.bf(RECENB, 512)[0:1, :],
                                      start=True, stop=True)

            def m8():
                @p.g("vector", reads=["E8", "ps3t"], writes=["M8"])
                def _(eng):
                    return eng.tensor_mul(self.bf(M8, 512)[0:8, :],
                                          self.bf(E8, 512)[0:8, :],
                                          ps3tf[0:8, :])

            out += [c9_evac, nrow_dma, recn_f, recn_b, bc_recn, za, bc_qvd,
                    z_add, e_exp, e_sum, recip_e, recen_b, bc_recen, m8]
            for c in range(4):
                def ind8(c=c):
                    @p.g("tensor", reads=["M8", "consts"], writes=["ps3t"])
                    def _(eng):
                        return eng.matmul(
                            ps3tf[:, :],
                            self.cview("C_IND8")[0:8, c * 128:(c + 1) * 128],
                            self.bf(M8, 512)[0:8, :],
                            start=True, stop=True)

                def bcc(c=c):
                    @p.g("scalar", reads=["ps3t"], writes=[f"BCC.{c % 2}"])
                    def _(eng):
                        return eng.copy(self.f32(BCC, 512, (c % 2) * 512),
                                        ps3tf[:, :])

                def mul(c=c):
                    if c == 0:
                        @p.g("vector", reads=["USB.0", "BCC.0"], writes=["TACC"])
                        def _(eng):
                            return eng.tensor_mul(self.f32(TACC, 512),
                                                  self.f32(USB, 512, 0),
                                                  self.f32(BCC, 512, 0))
                    else:
                        @p.g("vector", reads=[f"USB.{c}", f"BCC.{c % 2}"],
                             writes=["TC"])
                        def _(eng):
                            return eng.tensor_mul(self.f32(TC, 512),
                                                  self.f32(USB, 512, c * 512),
                                                  self.f32(BCC, 512, (c % 2) * 512))

                def acc(c=c):
                    @p.g("vector", reads=["TC", "TACC"], writes=["TACC"])
                    def _(eng):
                        return eng.tensor_add(self.f32(TACC, 512),
                                              self.f32(TACC, 512),
                                              self.f32(TC, 512))
                out.append(ind8)
                out.append(bcc)
                out.append(mul)
                if c > 0:
                    out.append(acc)

            def fold_cp():
                @p.g("vector", reads=["TACC"], writes=["TC"])
                def _(eng):
                    return eng.tensor_copy(self.f32(TC, 512)[0:64, :],
                                           self.f32(TACC, 512)[64:128, :])

            def fold():
                dst = self.arena_t[pb:pb + 64,
                                   XMF.off + (ft * 2048 + ib * 512) * 2:
                                   XMF.off + (ft * 2048 + ib * 512 + 512) * 2]

                @p.g("vector", reads=["TACC", "TC"], writes=[f"XMF.{ft}"])
                def _(eng):
                    return eng.tensor_add(dst.bitcast(BF16),
                                          self.f32(TACC, 512)[0:64, :],
                                          self.f32(TC, 512)[0:64, :])
            out.append(fold_cp)
            out.append(fold)
            return out

        for s in range(nS + 2):
            st = st_thunks(s) if s < nS else []
            pv = pv_thunks(s - 1) if 1 <= s <= nS else []
            epi = epi_thunks(s - 2) if s >= 2 else []
            uev = uevac_thunks(s - 1) if 1 <= s <= nS else []
            for t in st[0:8]:
                t()
            for t in epi[0:8]:
                t()
            for t in st[8:16]:
                t()
            for t in epi[8:14]:
                t()
            for t in st[16:24]:
                t()
            if pv:
                pv[0]()
            for t in epi[14:18]:
                t()
            for t in st[24:32]:
                t()
            if pv:
                pv[1]()
            for t in epi[18:22]:
                t()
            if pv:
                pv[2]()
            for t in epi[22:26]:
                t()
            if pv:
                pv[3]()
            for t in epi[26:]:
                t()
            if pv:
                pv[4]()
            for t in uev:
                t()
        for b in [PTB[0], PTB[1], USB, C9, RECNF, RECNB, ZA, E8,
                  RECENB, M8, BCC, TACC, TC]:
            p.free(b)
        self.tap("vws9", VWS9, "raw_bf16", rows=128, cols=144)
        self.tap("qvd", QVD, "raw_bf16", rows=4, cols=8192)
        self.tap("xmf", XMF, "raw_bf16", rows=128, cols=8 * 2048)
        p.free(VTg); p.free(QFg); p.free(KFg); p.free(VWS9); p.free(QVD)
        WFT = self.transp_weight("L0_attn_Wf", 8, 8, "WFT")
        xb2 = p.alloc("RES2", T * D * 4 // 128)
        dst = self.f32(xb2).rearrange("p (t d) -> p t d", t=NT)
        p.dma("sync", lambda eng: eng.dma_start(
            out=dst, in_=self.sp_xb.rearrange("(t p) d -> p t d", p=128)),
            reads=["dram.res"],
            writes=[xb2] + [f"{xb2.name}.{t}" for t in range(NT)], n=1)
        TMPF = p.alloc("TMPF", 1024 * 4)
        BFB = self.load_bbc("bf")
        for t in range(NT):
            pw = self.ps[0] if t % 2 == 0 else self.ps[1]
            ptoks = ["ps0a", "ps0b"] if t % 2 == 0 else ["ps1a", "ps1b"]

            @p.g("tensor", reads=[WFT] + [f"XMF.{k}" for k in range(8)],
                 writes=ptoks)
            def _(eng, t=t, pw=pw):
                last = None
                for c in range(2):
                    for k in range(8):
                        last = eng.matmul(
                            pw[:, c * 512:(c + 1) * 512],
                            self.bf(XMF, 128, k * 2048 + t * 128),
                            self.bf(WFT, 512, k * 1024 + c * 512),
                            start=(k == 0), stop=(k == 7))
                return last

            @p.g("vector", reads=ptoks + [BFB], writes=[TMPF])
            def _(eng, pw=pw):
                return eng.tensor_add(self.f32(TMPF, 1024), pw[:, 0:1024],
                                      self.bbc_view(BFB))

            @p.g("vector", reads=[TMPF, f"{xb2.name}.{t}"],
                 writes=[f"{xb2.name}.{t}"])
            def _(eng, t=t):
                xv = self.f32(xb2, 1024, t * 1024)
                return eng.tensor_add(xv, xv, self.f32(TMPF, 1024))
        p.free(TMPF)
        p.free(BFB)
        p.free(XMF)
        p.free(WFT)
        self.tap("res2", xb2, "tok_f32")
        return self.attn_relu_ffn(xb2)

    def attn_relu_ffn(self, res2):
        p = self.p
        tag = "G2"
        self.ln_stats_plain(res2, tag)
        self.ln_finalize(tag)
        ON = p.alloc("ON.G2", 16 * 1024 * 2)
        for t in range(NT):
            @p.g("scalar", reads=[f"{res2.name}.{t}", f"st.r.{tag}"],
                 writes=[f"ON.G2.{t}"],
                 ptr_reads=[f"st.r.{tag}", f"st.negmr.{tag}"])
            def _(eng, t=t):
                return eng.activation(self.bf(ON, 1024, t * 1024),
                                      self.f32(res2, 1024, t * 1024), AF.Identity,
                                      scale=self.sv(f"r.{tag}", t, 1),
                                      bias=self.sv(f"negmr.{tag}", t, 1))
        ONF = p.alloc("ONF", 8 * 2048 * 2)
        self.transpose_tokmajor(
            ON, 1024, ONF, 8, tag,
            affine=(lambda ft: self.bcol("g2", ft), lambda ft: self.bcol("b2", ft)))
        p.free(ON)
        xc = self.reload_add_half(self.sp_xa, "x1", res2, "x2p", scale=1.0)
        WR1T = self.transp_weight("L0_attn_Wr1", 16, 8, "WR1T", oc_group=4)
        WR2T = self.transp_weight("L0_attn_Wr2", 8, 16, "WR2T", oc_group=2)
        TMPR = p.alloc("TMPR", 1024 * 4)
        BR2B = self.load_bbc("br2")
        MIDs = [p.alloc(f"MID.{i}", 16 * 128 * 2) for i in range(2)]
        for t in range(NT):
            MID = MIDs[t % 2]
            for mg in range(2):
                pm = self.ps[mg]
                ptoks = ["ps0a", "ps0b"] if mg == 0 else ["ps1a", "ps1b"]

                @p.g("tensor", reads=[WR1T] + [f"ONF.{k}" for k in range(8)],
                     writes=ptoks)
                def _(eng, t=t, mg=mg, pm=pm):
                    last = None
                    for ms in range(8):
                        m = mg * 8 + ms
                        for k in range(8):
                            last = eng.matmul(
                                pm[:, ms * 128:ms * 128 + 128],
                                self.bf(WR1T, 128, k * 2048 + m * 128),
                                self.bf(ONF, 128, k * 2048 + t * 128),
                                start=(k == 0 and ms % 4 == 0),
                                stop=(k == 7 and ms % 4 == 3))
                    return last

                @p.g("scalar", reads=ptoks + ["bpack"],
                     writes=[f"MID.{t % 2}.{mg}"])
                def _(eng, t=t, mg=mg, pm=pm, MID=MID):
                    last = None
                    for ms in range(8):
                        m = mg * 8 + ms
                        last = eng.activation(
                            self.bf(MID, 128, m * 128),
                            pm[:, ms * 128:ms * 128 + 128],
                            AF.Relu, bias=self.bcol("br1", m))
                    return last
            pw = self.ps[2]
            wtoks = ["ps2a", "ps2b"]

            @p.g("tensor",
                 reads=[WR2T, f"MID.{t % 2}.0", f"MID.{t % 2}.1"], writes=wtoks)
            def _(eng, t=t, pw=pw, MID=MID):
                last = None
                for c in range(2):
                    for k in range(16):
                        last = eng.matmul(
                            pw[:, c * 512:(c + 1) * 512],
                            self.bf(MID, 128, k * 128),
                            self.bf(WR2T, 512, k * 1024 + c * 512),
                            start=(k == 0), stop=(k == 15))
                return last

            @p.g("vector", reads=wtoks + [BR2B], writes=[TMPR])
            def _(eng, pw=pw):
                return eng.tensor_add(self.f32(TMPR, 1024), pw[:, 0:1024],
                                      self.bbc_view(BR2B))

            @p.g("vector", reads=[TMPR, f"{xc.name}.{t}"],
                 writes=[f"{xc.name}.{t}"])
            def _(eng, t=t):
                xv = self.f32(xc, 1024, t * 1024)
                return eng.tensor_add(xv, xv, self.f32(TMPR, 1024))
        self.tap("x2", xc, "tok_f32")
        p.free(MIDs[0])
        p.free(MIDs[1])
        p.free(TMPR)
        p.free(BR2B)
        p.free(WR1T)
        p.free(WR2T)
        p.free(ONF)
        return xc

    def emit_block(self):
        nc, p, sems = self.nc, self.p, self.sems

        def mk(engine):
            plan = p.ops[engine]

            def run(eng):
                for gi, (waits, emit, inc, n_dma) in enumerate(plan):
                    for (we, wc) in waits:
                        eng.wait_ge(sems[we], wc)
                    if emit is None:
                        continue
                    if n_dma:
                        insts = emit(eng)
                        if not isinstance(insts, (list, tuple)):
                            insts = [insts]
                        assert len(insts) == n_dma, (engine, len(insts), n_dma)
                        for ins in insts:
                            ins.then_inc(sems[engine], 16)
                        p.group_insts[insts[-1].ins.name] = (engine, gi)
                    else:
                        last = emit(eng)
                        assert last is not None, f"group on {engine} returned None"
                        last.then_inc(sems[engine], 1)
                        p.group_insts[last.ins.name] = (engine, gi)
            return run

        with nc.Block() as block:
            block.tensor(mk("tensor"))
            block.vector(mk("vector"))
            block.scalar(mk("scalar"))
            block.sync(mk("sync"))
            block.gpsimd(mk("gpsimd"))
        self.es.close()



def build_kernel(debug_taps=(), stop_after="", detect_races=True, paranoid=False):
    PARANOID[0] = paranoid
    b = Builder(debug_taps, detect_races)
    b.stop_after = stop_after
    b.build()
    b.emit_block()
    return b


def kernel(**inputs):
    x = np.ascontiguousarray(np.asarray(inputs["L0_attn_x"], dtype=np.float32))
    assert x.shape == (8, T, D)
    consts = host_constants()
    bp, bbc, wsblk = host_packs(inputs)
    b = build_kernel()
    common = {}
    for name in INPUT_SPECS:
        common[name] = np.ascontiguousarray(np.asarray(inputs[name], np.float32))
    for name, arr in consts.items():
        common[name] = np.ascontiguousarray(arr)
    common["BPACK"] = np.ascontiguousarray(bp)
    common["BBC"] = np.ascontiguousarray(bbc)
    common["BPACKBF"] = np.ascontiguousarray(wsblk)
    in_maps = [dict(common, x=np.ascontiguousarray(x[c])) for c in range(8)]
    res = run_bass_kernel_spmd(b.nc, in_maps, core_ids=list(range(8)))
    return np.stack([np.asarray(res.results[c]["out"], np.float32)
                     for c in range(8)], axis=0)


# version 09 (final)
# speedup vs baseline: 1.1830x; 1.1013x over previous
import contextlib
import math

import numpy as np
import ml_dtypes

import concourse.bass as bass
import concourse.mybir as mybir
from concourse.bass_utils import run_bass_kernel_spmd

F32 = mybir.dt.float32
BF16 = mybir.dt.bfloat16
U8 = mybir.dt.uint8
AF = mybir.ActivationFunctionType
ALU = mybir.AluOpType
AX = mybir.AxisListType

T = 2048
D = 1024
NT = 16
EPS = 1e-5

ARENA_KB = 207


ENGINES = ("tensor", "vector", "scalar", "sync", "gpsimd")


class Buf:

    def __init__(self, name, off, nbytes):
        self.name = name
        self.off = off
        self.nbytes = nbytes
        self.ghosts = []

    def __repr__(self):
        return f"Buf({self.name}@{self.off}:{self.off+self.nbytes})"


class Arena:
    def __init__(self, nbytes):
        self.nbytes = nbytes
        self.free = [(0, nbytes)]
        self.dead = []
        self.live = {}
        self.peak = 0

    def alloc(self, name, nbytes, align=128, side="auto"):
        nbytes = (nbytes + align - 1) // align * align
        if side == "auto":
            side = "top" if nbytes <= 8 * 1024 else "bottom"
        order = (range(len(self.free)) if side == "bottom"
                 else range(len(self.free) - 1, -1, -1))
        for i in order:
            off, size = self.free[i]
            if side == "top":
                a_off = (off + size - nbytes) // align * align
                if a_off < off:
                    continue
            else:
                a_off = (off + align - 1) // align * align
            if a_off + nbytes <= off + size:
                pre = a_off - off
                post = (off + size) - (a_off + nbytes)
                repl = []
                if pre:
                    repl.append((off, pre))
                if post:
                    repl.append((a_off + nbytes, post))
                self.free[i:i + 1] = repl
                buf = Buf(name, a_off, nbytes)
                keep = []
                for (doff, dsize, tok) in self.dead:
                    if doff < a_off + nbytes and a_off < doff + dsize:
                        buf.ghosts.append(tok)
                        keep.append((doff, dsize, tok))
                    else:
                        keep.append((doff, dsize, tok))
                self.dead = keep
                self.live[name] = buf
                for ob in self.live.values():
                    if ob is not buf and ob.off < buf.off + buf.nbytes and buf.off < ob.off + ob.nbytes:
                        raise AssertionError(f"alloc overlap: {buf} vs {ob}")
                used = self.nbytes - sum(s for _, s in self.free)
                self.peak = max(self.peak, used)
                return buf
        raise MemoryError(
            f"arena: cannot alloc {name} ({nbytes}B); free={self.free}")

    def freebuf(self, buf, tokens=None):
        assert self.live.pop(buf.name) is buf
        for tok in (tokens if tokens is not None else [buf.name]):
            self.dead.append((buf.off, buf.nbytes, tok))
        self.free.append((buf.off, buf.nbytes))
        self.free.sort()
        merged = []
        for off, size in self.free:
            if merged and merged[-1][0] + merged[-1][1] == off:
                merged[-1] = (merged[-1][0], merged[-1][1] + size)
            else:
                merged.append((off, size))
        self.free = [tuple(x) for x in merged]


PARANOID = [False]


class Prog:

    def __init__(self):
        self.ops = {e: [] for e in ENGINES}
        self.count = {e: 0 for e in ENGINES}
        self.last_write = {}
        self.readers = {}
        self.guaranteed = {e: {f: 0 for f in ENGINES} for e in ENGINES}
        self.arena = Arena(ARENA_KB * 1024)
        self.ngroups = 0
        self.alloc_log = []
        self.group_insts = {}

    def alloc(self, name, nbytes, align=128, side="auto"):
        buf = self.arena.alloc(name, nbytes, align, side)
        self.alloc_log.append((self.ngroups, "A", name, buf.off, buf.nbytes))
        return buf

    def free(self, buf):
        self.alloc_log.append((self.ngroups, "F", buf.name, buf.off, buf.nbytes))
        pref = buf.name + "."
        toks = set([buf.name])
        for tok in list(self.last_write) + list(self.readers):
            if tok.startswith(pref):
                toks.add(tok)
        self.arena.freebuf(buf, sorted(toks))

    def group(self, engine, emit, reads=(), writes=(), ptr_reads=(), n_dma=0):
        def owner_of(tok):
            live = self.arena.live
            parts = tok.split(".")
            for i in range(len(parts), 0, -1):
                cand = ".".join(parts[:i])
                if cand in live:
                    return live[cand]
            return None

        def toks(xs, consume_ghosts=False):
            out = []
            ghost_owners = []
            for x in xs:
                if isinstance(x, Buf):
                    out.append(x.name)
                    buf = x
                else:
                    out.append(x)
                    buf = owner_of(x) if consume_ghosts else None
                if consume_ghosts and buf is not None and buf.ghosts:
                    out.extend(buf.ghosts)
                    ghost_owners.append(buf)
            for buf in ghost_owners:
                buf.ghosts = []
            return out

        rtoks = toks(reads)
        wtoks = toks(writes, consume_ghosts=True)
        ptoks = toks(ptr_reads)

        needs_w = {}
        needs_r = {}

        def need(d, e, c):
            if c > d.get(e, 0):
                d[e] = c

        if PARANOID[0]:
            for e in ENGINES:
                if e != engine and self.count[e] > 0:
                    need(needs_w, e, self.count[e])

        for tok in rtoks + ptoks:
            lw = self.last_write.get(tok)
            if lw:
                need(needs_w, *lw)
        for tok in wtoks:
            lw = self.last_write.get(tok)
            if lw:
                need(needs_w, *lw)
            for e, c in self.readers.get(tok, {}).items():
                need(needs_r, e, c)

        needs = dict(needs_r)
        for e, c in needs_w.items():
            if c > needs.get(e, 0):
                needs[e] = c
        waits = []
        for e, c in sorted(needs.items()):
            if e == engine:
                raww = needs_w.get(e, 0)
                if engine in ("tensor", "sync", "gpsimd"):
                    continue
                if raww == 0:
                    continue
                c = raww
                if self.guaranteed[engine].get(e, 0) >= c:
                    continue
            elif self.guaranteed[engine].get(e, 0) >= c:
                continue
            waits.append((e, c))
            self.guaranteed[engine][e] = c

        inc = 16 * n_dma if n_dma else 1
        newc = self.count[engine] + inc
        self.count[engine] = newc
        self.ops[engine].append((waits, emit, inc, n_dma))
        self.ngroups += 1

        for tok in wtoks:
            self.last_write[tok] = (engine, newc)
            self.readers.pop(tok, None)
        for tok in rtoks + ptoks:
            self.readers.setdefault(tok, {})[engine] = newc

    def g(self, engine, reads=(), writes=(), ptr_reads=()):
        def deco(fn):
            self.group(engine, fn, reads, writes, ptr_reads)
        return deco

    def dma(self, queue, emit, reads=(), writes=(), n=1):
        self.group(queue, emit, reads, writes, n_dma=n)

    def emit_all(self, nc, sems):
        for engine in ENGINES:
            plan = self.ops[engine]
            eng = getattr(nc, engine)

            def run(eng, plan=plan, engine=engine):
                for waits, emit, inc, n_dma in plan:
                    for (we, wc) in waits:
                        eng.wait_ge(sems[we], wc)
                    if n_dma:
                        insts = emit(eng)
                        if not isinstance(insts, (list, tuple)):
                            insts = [insts]
                        assert len(insts) == n_dma, (engine, len(insts), n_dma)
                        for ins in insts:
                            ins.then_inc(sems[engine], 16)
                    else:
                        last = emit(eng)
                        assert last is not None, f"group on {engine} returned None"
                        last.then_inc(sems[engine], 1)
            yield engine, run



def host_constants():
    n, dh, window = T, 64, 512
    half = np.arange(0, dh, 2, dtype=np.float64)
    inv = 1.0 / 10000.0 ** (half / dh)
    t = np.arange(n, dtype=np.float64)
    fr = t[:, None] * inv[None, :]
    base = (half + 0.4 * dh) / (1.4 * dh)
    power = (t - n // 2) / float(window // 2)
    sc = base[None, :] ** power[:, None]
    cos = np.cos(fr)
    sin = np.sin(fr)
    cos64 = np.concatenate([cos, cos], axis=1)
    sin64 = np.concatenate([sin, sin], axis=1)
    sc64 = np.concatenate([sc, sc], axis=1)
    qa = (cos64 * sc64).T
    qb = (sin64 * sc64).T
    ka = (cos64 / sc64).T
    kb = (sin64 / sc64).T
    rep = lambda a: np.concatenate([a, a], axis=0).astype(np.float32)
    consts = {
        "C_ROTQA": rep(qa), "C_ROTQB": rep(qb),
        "C_ROTKA": rep(ka), "C_ROTKB": rep(kb),
    }
    consts["C_ID"] = np.eye(128, dtype=ml_dtypes.bfloat16)
    M = np.zeros((64, 64), np.float32)
    for dp in range(32):
        M[dp, dp + 32] = -1.0
    for dp in range(32, 64):
        M[dp, dp - 32] = 1.0
    M128 = np.zeros((128, 128), np.float32)
    M128[0:64, 0:64] = M
    M128[64:128, 64:128] = M
    consts["C_ROTM"] = M128.T.astype(ml_dtypes.bfloat16)
    i = np.arange(128)
    own = np.where(i[None, :] >= i[:, None], 0.0, -1e30).astype(np.float32)
    prev = np.where(i[None, :] <= i[:, None], 0.0, -1e30).astype(np.float32)
    consts["C_MASK_OWN"] = own.astype(ml_dtypes.bfloat16)
    consts["C_MASK_PREV"] = prev.astype(ml_dtypes.bfloat16)
    blk2 = np.zeros((2, 128), np.float32)
    blk2[0, 0:64] = 1.0
    blk2[1, 64:128] = 1.0
    consts["C_BLK2"] = blk2
    consts["C_ONES1"] = np.ones((1, 128), dtype=ml_dtypes.bfloat16)
    ind4 = np.zeros((128, 32), np.float32)
    for m in range(4):
        ind4[m, m * 8:m * 8 + 8] = 1.0
    consts["C_IND4"] = ind4.astype(ml_dtypes.bfloat16)
    ind8 = np.zeros((128, 544), np.float32)
    for c in range(4):
        for pp in range(128):
            ind8[2 * c + pp // 64, c * 128 + pp] = 1.0
    ind8[0:8, 512] = 1.0
    consts["C_IND8"] = ind8.astype(ml_dtypes.bfloat16)
    lm = np.zeros((128, 8 * 512), np.float32)
    pp = np.arange(128)[:, None]
    cc = np.arange(512)[None, :]
    for k in range(8):
        lm[:, k * 512:(k + 1) * 512] = ((cc - pp >= 128 * k - 512) &
                                        (cc - pp <= 128 * k)).astype(np.float32)
    consts["C_LMASK"] = lm.astype(ml_dtypes.bfloat16)
    return consts


CONST_SPECS = {
    "C_ROTQA": ([128, 2048], F32), "C_ROTQB": ([128, 2048], F32),
    "C_ROTKA": ([128, 2048], F32), "C_ROTKB": ([128, 2048], F32),
    "C_ID": ([128, 128], BF16), "C_ROTM": ([128, 128], BF16),
    "C_MASK_OWN": ([128, 128], BF16), "C_MASK_PREV": ([128, 128], BF16),
    "C_BLK2": ([2, 128], F32), "C_ONES1": ([1, 128], BF16),
    "C_IND4": ([128, 32], BF16), "C_IND8": ([128, 544], BF16),
    "C_LMASK": ([128, 4096], BF16),
}

INPUT_SPECS = {
    "L0_attn_Wq": ([1024, 1024], F32), "L0_attn_bq": ([1024], F32),
    "L0_attn_Wk": ([1024, 1024], F32), "L0_attn_bk": ([1024], F32),
    "L0_attn_Wv": ([512, 1024], F32), "L0_attn_bv": ([512], F32),
    "L0_attn_Wqv": ([512, 1024], F32), "L0_attn_bqv": ([512], F32),
    "L0_attn_Ws": ([1, 96], F32), "L0_attn_bs": ([1], F32),
    "L0_attn_Wf": ([1024, 1024], F32), "L0_attn_bf": ([1024], F32),
    "L0_attn_Wr1": ([2048, 1024], F32), "L0_attn_br1": ([2048], F32),
    "L0_attn_Wr2": ([1024, 2048], F32), "L0_attn_br2": ([1024], F32),
    "L0_attn_g1": ([1024], F32), "L0_attn_b1": ([1024], F32),
    "L0_attn_g2": ([1024], F32), "L0_attn_b2": ([1024], F32),
    "L0_ffn_wqkv": ([2, 1024, 1536], F32),
    "L0_ffn_q_scale": ([2, 64], F32), "L0_ffn_k_scale": ([2, 64], F32),
    "L0_ffn_wo": ([2, 512, 1024], F32),
    "L0_ffn_ln_g": ([2, 1024], F32), "L0_ffn_ln_b": ([2, 1024], F32),
    "L0_ffn_w1": ([2, 1024, 5460], F32), "L0_ffn_b1": ([2, 5460], F32),
    "L0_ffn_w2": ([2, 2730, 1024], F32), "L0_ffn_b2": ([2, 1024], F32),
}



class Kern:
    def __init__(self, debug_taps=(), detect_races=True):
        self.nc = bass.Bass("TRN2", target_bir_lowering=False,
                            detect_race_conditions=detect_races)
        nc = self.nc
        self.p = Prog()
        self.es = contextlib.ExitStack()
        self.debug_taps = list(debug_taps)
        self.taps_done = []

        self.x_in = nc.dram_tensor("x", [T, D], F32, kind="ExternalInput").ap()
        self.out = nc.dram_tensor("out", [T, D], F32, kind="ExternalOutput").ap()
        self.inp = {}
        for name, (shape, dt) in INPUT_SPECS.items():
            self.inp[name] = nc.dram_tensor(name, shape, dt, kind="ExternalInput").ap()
        self.cst = {}
        for name, (shape, dt) in CONST_SPECS.items():
            self.cst[name] = nc.dram_tensor(name, shape, dt, kind="ExternalInput").ap()
        self.sp_xa = nc.dram_tensor("sp_xa", [T, D], F32).ap()
        self.sp_xb = nc.dram_tensor("sp_xb", [T, D], F32).ap()

        self.arena_t = self.es.enter_context(
            nc.sbuf_tensor("arena", [128, ARENA_KB * 1024], U8))
        self.ps = [self.es.enter_context(nc.psum_tensor(f"ps{i}", [128, 1024], F32))
                   for i in range(3)]
        self.ps3a = self.es.enter_context(nc.psum_tensor("ps3a", [128, 512], F32))
        self.ps3t = self.es.enter_context(nc.psum_tensor("ps3t", [128, 1024], BF16))
        self.sems = {e: self.es.enter_context(nc.semaphore(f"sem_{e}"))
                     for e in ENGINES}

        self.dbg_cnt = 0

    def f32(self, buf, cols=None, off_el=0):
        ap = self.arena_t[:, buf.off + off_el * 4:
                          buf.off + (off_el + (cols or (buf.nbytes // 4 - off_el))) * 4]
        return ap.bitcast(F32)

    def bf(self, buf, cols=None, off_el=0):
        ap = self.arena_t[:, buf.off + off_el * 2:
                          buf.off + (off_el + (cols or (buf.nbytes // 2 - off_el))) * 2]
        return ap.bitcast(BF16)

    def pstok(self, i, half):
        return f"ps{i}{'ab'[half]}"

    def tap_f32(self, name, get_ap, rows, cols):
        if name not in self.debug_taps:
            return
        o = self.nc.dram_tensor(f"tap_{name}", [rows, cols], F32,
                                kind="ExternalOutput").ap()
        ap, reads = get_ap()
        self.p.dma("sync", lambda eng: eng.dma_start(out=o, in_=ap),
                   reads=reads, n=1)
        self.taps_done.append(name)

    def tap_bf16(self, name, get_ap, rows, cols):
        if name not in self.debug_taps:
            return
        o = self.nc.dram_tensor(f"tap_{name}", [rows, cols], BF16,
                                kind="ExternalOutput").ap()
        ap, reads = get_ap()
        self.p.dma("sync", lambda eng: eng.dma_start(out=o, in_=ap),
                   reads=reads, n=1)
        self.taps_done.append(name)


def ceil_div(a, b):
    return (a + b - 1) // b




def bpack_layout():
    entries = [
        ("bq", 8), ("bk", 8), ("bqv", 4), ("br1", 16),
        ("g1", 8), ("b1", 8), ("g2", 8), ("b2", 8),
        ("lng.0", 8), ("lnb.0", 8), ("lng.1", 8), ("lnb.1", 8),
        ("b1a.0", 22), ("b1g.0", 22), ("b1a.1", 22), ("b1g.1", 22),
        ("qs.0", 1), ("qssw.0", 1), ("ks.0", 1), ("kssw.0", 1),
        ("qs.1", 1), ("qssw.1", 1), ("ks.1", 1), ("kssw.1", 1),
        ("bs", 1), ("c_eps", 1), ("c_eps2", 1),
    ]
    m, c = {}, 0
    for name, n in entries:
        m[name] = (c, n)
        c += n
    return m, c


def bbc_layout():
    entries = [("bv", 512), ("bf", 1024), ("br2", 1024),
               ("b2f.0", 1024), ("b2f.1", 1024), ("wsout", 512)]
    m, c = {}, 0
    for name, n in entries:
        m[name] = (c, n)
        c += n
    return m, c


BPACK_MAP, BPACK_COLS = bpack_layout()
BBC_MAP, BBC_COLS = bbc_layout()


def host_packs(inp):
    def cols(vec, n):
        v = np.zeros((128, n), np.float32)
        flat = np.asarray(vec, np.float32).reshape(-1)
        for c in range(n):
            seg = flat[c * 128:(c + 1) * 128]
            v[:len(seg), c] = seg
        return v

    bp = np.zeros((128, BPACK_COLS), np.float32)

    def put(name, arr):
        c0, n = BPACK_MAP[name]
        bp[:, c0:c0 + n] = arr

    put("bq", cols(inp["L0_attn_bq"], 8))
    put("bk", cols(inp["L0_attn_bk"], 8))
    put("bqv", cols(inp["L0_attn_bqv"], 4))
    put("br1", cols(inp["L0_attn_br1"], 16))
    put("g1", cols(inp["L0_attn_g1"], 8))
    put("b1", cols(inp["L0_attn_b1"], 8))
    put("g2", cols(inp["L0_attn_g2"], 8))
    put("b2", cols(inp["L0_attn_b2"], 8))
    for l in range(2):
        put(f"lng.{l}", cols(inp["L0_ffn_ln_g"][l], 8))
        put(f"lnb.{l}", cols(inp["L0_ffn_ln_b"][l], 8))
        put(f"b1a.{l}", cols(inp["L0_ffn_b1"][l][0:2730], 22))
        put(f"b1g.{l}", cols(inp["L0_ffn_b1"][l][2730:5460], 22))
        qs = np.asarray(inp["L0_ffn_q_scale"][l], np.float32)
        ks = np.asarray(inp["L0_ffn_k_scale"][l], np.float32)
        put(f"qs.{l}", np.concatenate([qs, qs]).reshape(128, 1))
        put(f"qssw.{l}", np.concatenate([qs[32:], qs[:32], qs[32:], qs[:32]]).reshape(128, 1))
        put(f"ks.{l}", np.concatenate([ks, ks]).reshape(128, 1))
        put(f"kssw.{l}", np.concatenate([ks[32:], ks[:32], ks[32:], ks[:32]]).reshape(128, 1))
    put("bs", np.broadcast_to(np.asarray(inp["L0_attn_bs"], np.float32), (128, 1)))
    put("c_eps", np.full((128, 1), EPS, np.float32))
    put("c_eps2", np.full((128, 1), EPS * EPS, np.float32))

    bbc = np.zeros((128, BBC_COLS), np.float32)

    def putb(name, vec):
        c0, n = BBC_MAP[name]
        bbc[:, c0:c0 + n] = np.broadcast_to(np.asarray(vec, np.float32).reshape(1, n),
                                            (128, n))

    putb("bv", inp["L0_attn_bv"])
    putb("bf", inp["L0_attn_bf"])
    putb("br2", inp["L0_attn_br2"])
    putb("b2f.0", inp["L0_ffn_b2"][0])
    putb("b2f.1", inp["L0_ffn_b2"][1])
    ws = np.asarray(inp["L0_attn_Ws"], np.float32).reshape(96)
    putb("wsout", np.concatenate([ws[32:96]] * 8))

    wsblk = np.zeros((128, 4), np.float32)
    for m in range(4):
        wsblk[32 * m:32 * m + 32, m] = ws[0:32]
    return bp, bbc, wsblk.astype(ml_dtypes.bfloat16)


PACK_SPECS = {
    "BPACK": ([128, BPACK_COLS], F32),
    "BBC": ([128, BBC_COLS], F32),
    "BPACKBF": ([128, 4], BF16),
}



class Builder(Kern):
    def __init__(self, debug_taps=(), detect_races=True):
        super().__init__(debug_taps, detect_races)
        for name, (shape, dt) in PACK_SPECS.items():
            self.inp[name] = self.nc.dram_tensor(name, shape, dt,
                                                 kind="ExternalInput").ap()
        self._stat_used = 0
        self._stat_map = {}

    def stat_cols(self, name, n):
        if name in self._stat_map:
            return self._stat_map[name][0]
        c = self._stat_used
        assert c + n <= self.STAT.nbytes // 4, f"STAT overflow at {name}"
        self._stat_map[name] = (c, n)
        self._stat_used += n
        return c

    def sv(self, name, col=0, n=None):
        base, cnt = self._stat_map[name]
        if n is None:
            n = cnt - col
        return self.f32(self.STAT, n, base + col)

    def bcol(self, key, chunk=0, n=1):
        c0, cnt = BPACK_MAP[key]
        return self.f32(self.BPACK_B, n, c0 + chunk)

    def load_bbc(self, key):
        p = self.p
        c0, cnt = BBC_MAP[key]
        self._bbc_n += 1
        buf = p.alloc(f"BBC.{key}.{self._bbc_n}", cnt * 4)
        p.dma("sync", lambda eng, buf=buf, c0=c0, cnt=cnt: eng.dma_start(
            out=self.f32(buf, cnt), in_=self.inp["BBC"][:, c0:c0 + cnt]),
            writes=[buf], n=1)
        return buf

    def bbc_view(self, buf, off=0, n=None):
        if n is None:
            n = buf.nbytes // 4 - off
        return self.f32(buf, n, off)

    def build(self):
        p = self.p
        self.XA = p.alloc("XA", T * D * 4 // 128)
        self.CONST = p.alloc("CONST", 3 * 1024, side="top")
        self.STAT = p.alloc("STAT", 14 * 1024, side="top")
        self.BPACK_B = p.alloc("BPACK", BPACK_COLS * 4, side="top")
        self.WSBLK = p.alloc("WSBLK", 4 * 2)
        self._bbc_n = 0

        self.load_input_x()
        self.load_consts()
        p.dma("sync", lambda eng: [
            eng.dma_start(out=self.f32(self.BPACK_B, BPACK_COLS), in_=self.inp["BPACK"]),
            eng.dma_start(out=self.bf(self.WSBLK, 4), in_=self.inp["BPACKBF"]),
        ], writes=[self.BPACK_B, self.WSBLK, "bpack"], n=2)

        xcur = self.XA
        xb = p.alloc("XB1", T * D * 4 // 128)
        self.ln_plain(xcur, xb, "ln0")
        self.tap("ln0", xb, "tok_f32")
        self.spill(xcur, self.sp_xa, "spA")
        p.free(xcur)
        xb = self.ffn_sublayer(xb, "F1")
        xcur = self.reload_add_half(self.sp_xa, "spA", xb, "mac1")
        self.tap("x1", xcur, "tok_f32")
        if self.stop_after == "A":
            self.store_output(xcur); return
        xcur = self.attn_sublayer(xcur)
        if self.stop_after == "B":
            self.store_output(xcur); return
        xb = p.alloc("XB2", T * D * 4 // 128)
        self.ln_plain(xcur, xb, "ln2")
        self.spill(xcur, self.sp_xa, "spC")
        p.free(xcur)
        xb = self.ffn_sublayer(xb, "F2")
        xcur = self.reload_add_half(self.sp_xa, "spC", xb, "mac2")
        self.store_output(xcur)

    stop_after = ""

    def load_input_x(self):
        p = self.p
        dst = self.f32(self.XA).rearrange("p (t d) -> p t d", t=NT)
        src = self.x_in.rearrange("(t p) d -> p t d", p=128)
        p.dma("sync", lambda eng: eng.dma_start(out=dst, in_=src),
              writes=[self.XA] + [f"XA.{t}" for t in range(NT)], n=1)

    def load_consts(self):
        p = self.p
        off = 0
        self.c_off = {}
        plan = [("C_ID", 128 * 2), ("C_ROTM", 128 * 2), ("C_MASK_OWN", 128 * 2),
                ("C_MASK_PREV", 128 * 2), ("C_ONES1", 128 * 2), ("C_BLK2", 128 * 4),
                ("C_IND4", 32 * 2), ("C_IND8", 544 * 2)]
        for name, nb in plan:
            self.c_off[name] = off
            off += nb
        assert off <= self.CONST.nbytes

        def emit(eng):
            outs = []
            for name, nb in plan:
                shape, dt = CONST_SPECS[name]
                v = self.arena_t[0:shape[0],
                                 self.CONST.off + self.c_off[name]:
                                 self.CONST.off + self.c_off[name] + nb]
                outs.append(eng.dma_start(out=v.bitcast(dt), in_=self.cst[name]))
            return outs
        p.dma("sync", emit, writes=[self.CONST, "consts"], n=len(plan))

    def cview(self, name, rows=None):
        shape, dt = CONST_SPECS[name]
        nb = shape[1] * (2 if dt == BF16 else 4)
        v = self.arena_t[0:(rows or shape[0]),
                         self.CONST.off + self.c_off[name]:
                         self.CONST.off + self.c_off[name] + nb]
        return v.bitcast(dt)

    def ln_stats_plain(self, src, tag):
        p = self.p
        sq = p.alloc(f"sq.{tag}", 1024 * 4)
        self.stat_cols(f"s.{tag}", 16)
        self.stat_cols(f"q.{tag}", 16)
        for t in range(NT):
            xt = self.f32(src, 1024, t * 1024)

            @p.g("scalar", reads=[f"{src.name}.{t}"], writes=[sq, f"st.q.{tag}"])
            def _(eng, xt=xt, t=t):
                return eng.activation(self.f32(sq, 1024), xt, AF.Square,
                                      accum_out=self.sv(f"q.{tag}", t, 1))

            @p.g("vector", reads=[f"{src.name}.{t}"], writes=[f"st.s.{tag}"])
            def _(eng, xt=xt, t=t):
                return eng.reduce_sum(self.sv(f"s.{tag}", t, 1), xt, axis=AX.X)
        p.free(sq)

    def ln_finalize(self, tag, extra_r2=False, n=16):
        p = self.p
        self.stat_cols(f"m.{tag}", n)
        self.stat_cols(f"v.{tag}", n)
        self.stat_cols(f"r.{tag}", n)
        self.stat_cols(f"negmr.{tag}", n)
        inv = 1.0 / D

        @p.g("vector", reads=[f"st.s.{tag}"], writes=[f"st.m.{tag}"])
        def _(eng):
            return eng.tensor_scalar(self.sv(f"m.{tag}"), self.sv(f"s.{tag}"),
                                     inv, None, ALU.mult)

        @p.g("vector", reads=[f"st.m.{tag}"], writes=[f"st.v.{tag}"])
        def _(eng):
            return eng.tensor_mul(self.sv(f"v.{tag}"), self.sv(f"m.{tag}"),
                                  self.sv(f"m.{tag}"))

        @p.g("vector", reads=[f"st.q.{tag}", f"st.v.{tag}"], writes=[f"st.v.{tag}"])
        def _(eng):
            return eng.scalar_tensor_tensor(self.sv(f"v.{tag}"), self.sv(f"q.{tag}"),
                                            inv, self.sv(f"v.{tag}"),
                                            ALU.mult, ALU.subtract)

        @p.g("scalar", reads=[f"st.v.{tag}", "bpack"], writes=[f"st.r.{tag}"],
             ptr_reads=["bpack"])
        def _(eng):
            return eng.activation(self.sv(f"r.{tag}"), self.sv(f"v.{tag}"),
                                  AF.Ln, bias=self.bcol("c_eps"))

        @p.g("scalar", reads=[f"st.r.{tag}"], writes=[f"st.r.{tag}"])
        def _(eng):
            return eng.activation(self.sv(f"r.{tag}"), self.sv(f"r.{tag}"),
                                  AF.Exp, scale=-0.5)

        @p.g("vector", reads=[f"st.r.{tag}", f"st.m.{tag}"], writes=[f"st.negmr.{tag}"])
        def _(eng):
            return eng.scalar_tensor_tensor(self.sv(f"negmr.{tag}"),
                                            self.sv(f"m.{tag}"), -1.0,
                                            self.sv(f"r.{tag}"), ALU.mult, ALU.mult)
        if extra_r2:
            self.stat_cols(f"r2.{tag}", n)
            self.stat_cols(f"negmr2.{tag}", n)

            @p.g("scalar", reads=[f"st.v.{tag}", "bpack"], writes=[f"st.r2.{tag}"],
                 ptr_reads=["bpack"])
            def _(eng):
                return eng.activation(self.sv(f"r2.{tag}"), self.sv(f"v.{tag}"),
                                      AF.Ln, scale=1.0 + EPS, bias=self.bcol("c_eps2"))

            @p.g("scalar", reads=[f"st.r2.{tag}"], writes=[f"st.r2.{tag}"])
            def _(eng):
                return eng.activation(self.sv(f"r2.{tag}"), self.sv(f"r2.{tag}"),
                                      AF.Exp, scale=-0.5)

            @p.g("vector", reads=[f"st.r2.{tag}", f"st.m.{tag}"],
                 writes=[f"st.negmr2.{tag}"])
            def _(eng):
                return eng.scalar_tensor_tensor(self.sv(f"negmr2.{tag}"),
                                                self.sv(f"m.{tag}"), -1.0,
                                                self.sv(f"r2.{tag}"),
                                                ALU.mult, ALU.mult)

    def ln_plain(self, src, dst, tag):
        p = self.p
        self.ln_stats_plain(src, tag)
        self.ln_finalize(tag)
        for t in range(NT):
            xt = self.f32(src, 1024, t * 1024)
            yt = self.f32(dst, 1024, t * 1024)

            @p.g("scalar", reads=[f"{src.name}.{t}"], writes=[f"{dst.name}.{t}"],
                 ptr_reads=[f"st.r.{tag}", f"st.negmr.{tag}"])
            def _(eng, xt=xt, yt=yt, t=t):
                return eng.activation(yt, xt, AF.Identity,
                                      scale=self.sv(f"r.{tag}", t, 1),
                                      bias=self.sv(f"negmr.{tag}", t, 1))

    def spill(self, buf, dram, tag):
        p = self.p
        src = self.f32(buf).rearrange("p (t d) -> p t d", t=NT)
        dst = dram.rearrange("(t p) d -> p t d", p=128)
        p.dma("sync", lambda eng: eng.dma_start(out=dst, in_=src),
              reads=[buf] + [f"{buf.name}.{t}" for t in range(NT)],
              writes=[f"dram.{tag}"], n=1)

    def reload_add_half(self, dram, dtag, other, tag, scale=0.5):
        p = self.p
        stage = p.alloc(f"rl.{tag}", 2 * 1024 * 4)
        for t in range(NT):
            half = t % 2
            sview = self.f32(stage, 1024, half * 1024)
            src = dram.rearrange("(t p) d -> p t d", p=128)[:, t]
            p.dma("sync",
                  lambda eng, sv_=sview, src=src: eng.dma_start(out=sv_, in_=src),
                  reads=[f"dram.{dtag}", f"rl.{tag}.{half}"],
                  writes=[f"rl.{tag}.w{half}"], n=1)
            ot = self.f32(other, 1024, t * 1024)

            @p.g("vector", reads=[f"rl.{tag}.w{half}", f"{other.name}.{t}"],
                 writes=[f"{other.name}.{t}", f"rl.{tag}.{half}"])
            def _(eng, ot=ot, sv_=sview, scale=scale):
                return eng.scalar_tensor_tensor(ot, ot, scale, sv_, ALU.mult, ALU.add)
        p.free(stage)
        return other

    def store_output(self, buf):
        p = self.p
        src = self.f32(buf).rearrange("p (t d) -> p t d", t=NT)
        dst = self.out.rearrange("(t p) d -> p t d", p=128)
        p.dma("sync", lambda eng: eng.dma_start(out=dst, in_=src),
              reads=[buf] + [f"{buf.name}.{t}" for t in range(NT)],
              writes=["output"], n=1)
        p.ops["sync"].append(([("sync", p.count["sync"])], None, 0, 0))

    def transpose_tokmajor(self, srcbuf, src_cols, dstbuf, n_ftiles, tag,
                           shift_ftiles=(), evac="vector", affine=None,
                           src_tile_cols=None):
        p = self.p
        stc = src_tile_cols if src_tile_cols is not None else src_cols
        for ft in range(n_ftiles):
            shifted = ft in shift_ftiles
            if shifted:
                @p.g("vector", writes=[f"{dstbuf.name}.{ft}"])
                def _(eng, ft=ft):
                    return eng.memset(self.bf(dstbuf, 1, ft * 2048), 0.0)
            for grp in range(4):
                slots = []
                for s in range(4):
                    tc = grp * 4 + s
                    chunk = self.bf(srcbuf, 128, tc * stc + ft * 128)
                    dstp = self.ps3t[:, s * 128:(s + 1) * 128]
                    slots.append((chunk, dstp, tc))

                @p.g("tensor",
                     reads=[f"{srcbuf.name}.{tc}" for _, _, tc in slots] + ["consts"],
                     writes=["ps3t"])
                def _(eng, slots=slots):
                    last = None
                    for chunk, dstp, tc in slots:
                        last = eng.transpose(dstp, chunk, self.cview("C_ID"))
                    return last

                base = ft * 2048 + grp * 512
                if shifted:
                    width = 512 if grp < 3 else 511
                    dst = self.bf(dstbuf, width, base + 1)
                else:
                    width = 512
                    dst = self.bf(dstbuf, width, base)
                src_ps = self.ps3t[:, 0:width]
                if affine is not None:
                    sc, bi = affine

                    @p.g("scalar", reads=["ps3t", "bpack"],
                         writes=[f"{dstbuf.name}.{ft}"],
                         ptr_reads=[])
                    def _(eng, dst=dst, src_ps=src_ps, ft=ft, sc=sc, bi=bi):
                        return eng.activation(dst, src_ps, AF.Identity,
                                              scale=sc(ft), bias=bi(ft))
                else:
                    @p.g(evac, reads=["ps3t"], writes=[f"{dstbuf.name}.{ft}"])
                    def _(eng, dst=dst, src_ps=src_ps):
                        return eng.tensor_copy(dst, src_ps)

    def ffn_sublayer(self, xb, tag):
        for l in range(2):
            xb = self.ffn_layer(xb, l, f"{tag}L{l}")
        return xb

    def ffn_layer(self, xb, l, lt):
        p = self.p
        WQKV = p.alloc(f"WQKV.{lt}", 8 * 1536 * 2)
        src = self.inp["L0_ffn_wqkv"][l].rearrange("(k p) m -> p k m", p=128)
        p.dma("gpsimd", lambda eng, s=src, b=WQKV: eng.dma_start(
            out=self.bf(b).rearrange("p (k m) -> p k m", k=8), in_=s),
            writes=[WQKV], n=1)
        XBF = p.alloc(f"XBF.{lt}", 16 * 1024 * 2)
        for t in range(NT):
            @p.g("scalar", reads=[f"{xb.name}.{t}"], writes=[f"XBF.{lt}.{t}"])
            def _(eng, t=t):
                return eng.copy(self.bf(XBF, 1024, t * 1024),
                                self.f32(xb, 1024, t * 1024))
        HF = p.alloc(f"HF.{lt}", 8 * 2048 * 2)
        self.transpose_tokmajor(XBF, 1024, HF, 8, lt, shift_ftiles=(4, 5, 6, 7))
        p.free(XBF)
        self.tap(f"hf.{lt}", HF, "raw_bf16", rows=128, cols=8 * 2048)

        QT = p.alloc(f"QT.{lt}", 16 * 512 * 2)
        KT = p.alloc(f"KT.{lt}", 16 * 512 * 2)
        VT = p.alloc(f"VT.{lt}", 16 * 520 * 2)

        @p.g("vector", writes=[f"VT.{lt}.ones"])
        def _(eng):
            return eng.memset(self.bf(VT, 16 * 520).rearrange(
                "p (t h c) -> p t h c", t=16, h=8)[:, :, :, 64:65], 1.0)
        SQ = p.alloc(f"SQ.{lt}", 1024 * 4)
        nrm0 = self.stat_cols(f"nrm.{lt}", 256)
        wv = self.bf(WQKV).rearrange("p (k m) -> p k m", k=8)
        for t in range(NT):
            pa, pb, pc = (("ps0a", "ps0b", "ps1a") if t % 2 == 0
                          else ("ps1b", "ps2a", "ps2b"))
            pmap = {"ps0a": self.ps[0][:, 0:512], "ps0b": self.ps[0][:, 512:1024],
                    "ps1a": self.ps[1][:, 0:512], "ps1b": self.ps[1][:, 512:1024],
                    "ps2a": self.ps[2][:, 0:512], "ps2b": self.ps[2][:, 512:1024]}
            for seg, (c0, ptok) in enumerate([(0, pa), (512, pb), (1024, pc)]):
                @p.g("tensor", reads=[f"HF.{lt}.{k}" for k in range(8)] + [WQKV],
                     writes=[ptok])
                def _(eng, t=t, c0=c0, ptok=ptok, pmap=pmap):
                    last = None
                    for k in range(8):
                        last = eng.matmul(
                            pmap[ptok],
                            self.bf(HF, 128, k * 2048 + t * 128),
                            wv[:, k, c0:c0 + 512],
                            start=(k == 0), stop=(k == 7))
                    return last
            @p.g("vector", reads=[pc], writes=[f"VT.{lt}.{t}"])
            def _(eng, t=t, pc=pc, pmap=pmap):
                return eng.tensor_copy(
                    self.bf(VT, 520, t * 520).rearrange(
                        "p (h c) -> p h c", h=8)[:, :, 0:64],
                    pmap[pc].rearrange("p (h c) -> p h c", h=8))
            @p.g("scalar", reads=[pa, pb], writes=[SQ])
            def _(eng, pa=pa, pb=pb, pmap=pmap, t=t):
                eng.activation(self.f32(SQ, 512, 0), pmap[pa], AF.Square)
                return eng.activation(self.f32(SQ, 512, 512), pmap[pb], AF.Square)

            @p.g("vector", reads=[SQ], writes=[f"st.nrm.{lt}.{t}"])
            def _(eng, t=t):
                return eng.reduce_sum(
                    self.sv(f"nrm.{lt}", 16 * t, 16),
                    self.f32(SQ, 1024).rearrange("p (h c) -> p h c", h=16), axis=AX.X)

            @p.g("vector", reads=[f"st.nrm.{lt}.{t}"], writes=[f"st.nrm.{lt}.{t}"])
            def _(eng, t=t):
                return eng.tensor_scalar(self.sv(f"nrm.{lt}", 16 * t, 16),
                                         self.sv(f"nrm.{lt}", 16 * t, 16),
                                         1e-24, None, ALU.max)

            @p.g("scalar", reads=[f"st.nrm.{lt}.{t}"], writes=[f"st.nrm.{lt}.{t}"])
            def _(eng, t=t):
                return eng.activation(self.sv(f"nrm.{lt}", 16 * t, 16),
                                      self.sv(f"nrm.{lt}", 16 * t, 16), AF.Ln)

            @p.g("scalar", reads=[f"st.nrm.{lt}.{t}"], writes=[f"st.nrm.{lt}.{t}"])
            def _(eng, t=t):
                return eng.activation(self.sv(f"nrm.{lt}", 16 * t, 16),
                                      self.sv(f"nrm.{lt}", 16 * t, 16),
                                      AF.Exp, scale=-0.5)
            @p.g("vector", reads=[pa, pb, f"st.nrm.{lt}.{t}"],
                 writes=[f"QT.{lt}.{t}", f"KT.{lt}.{t}"],
                 ptr_reads=[f"st.nrm.{lt}.{t}"])
            def _(eng, t=t, pa=pa, pb=pb, pmap=pmap):
                last = None
                for h in range(8):
                    last = eng.tensor_scalar(
                        self.bf(QT, 64, t * 512 + h * 64),
                        pmap[pa][:, h * 64:(h + 1) * 64],
                        self.sv(f"nrm.{lt}", 16 * t + h, 1), None, ALU.mult)
                for h in range(8):
                    last = eng.tensor_scalar(
                        self.bf(KT, 64, t * 512 + h * 64),
                        pmap[pb][:, h * 64:(h + 1) * 64],
                        self.sv(f"nrm.{lt}", 16 * t + 8 + h, 1), None, ALU.mult)
                return last
        self.tap(f"qt.{lt}", QT, "tok_bf16")
        self.tap(f"kt.{lt}", KT, "tok_bf16")
        self.tap(f"vt.{lt}", VT, "tok_bf16")
        p.free(SQ)
        p.free(HF)
        p.free(WQKV)

        QF = p.alloc(f"QF.{lt}", 4 * 2048 * 2)
        KF = p.alloc(f"KF.{lt}", 4 * 2048 * 2)
        self.transpose_tokmajor(QT, 512, QF, 4, lt)
        self.transpose_tokmajor(KT, 512, KF, 4, lt)
        p.free(QT)
        p.free(KT)
        ROT = p.alloc(f"ROT.{lt}", 4 * 2048 * 4)
        rot_off = {"qa": 0, "qb": 2048, "ka": 4096, "kb": 6144}
        p.dma("sync", lambda eng: [
            eng.dma_start(out=self.f32(ROT, 2048, rot_off["qa"]), in_=self.cst["C_ROTQA"]),
            eng.dma_start(out=self.f32(ROT, 2048, rot_off["qb"]), in_=self.cst["C_ROTQB"]),
            eng.dma_start(out=self.f32(ROT, 2048, rot_off["ka"]), in_=self.cst["C_ROTKA"]),
            eng.dma_start(out=self.f32(ROT, 2048, rot_off["kb"]), in_=self.cst["C_ROTKB"]),
        ], writes=[ROT], n=4)
        for nm, col, s2 in [("qa", f"qs.{l}", 8.0), ("qb", f"qssw.{l}", 8.0),
                            ("ka", f"ks.{l}", None), ("kb", f"kssw.{l}", None)]:
            @p.g("vector", reads=[ROT, "bpack"], writes=[ROT],
                 ptr_reads=["bpack"])
            def _(eng, nm=nm, col=col, s2=s2):
                if s2 is None:
                    return eng.tensor_scalar(self.f32(ROT, 2048, rot_off[nm]),
                                             self.f32(ROT, 2048, rot_off[nm]),
                                             self.bcol(col), None, ALU.mult)
                return eng.tensor_scalar(self.f32(ROT, 2048, rot_off[nm]),
                                         self.f32(ROT, 2048, rot_off[nm]),
                                         self.bcol(col), s2, ALU.mult,
                                         ALU.mult)
        TMP1 = p.alloc(f"TMP1.{lt}", 1024 * 4)
        TMP2 = p.alloc(f"TMP2.{lt}", 1024 * 4)
        for (dstb, ta, tb) in [(QF, "qa", "qb"), (KF, "ka", "kb")]:
            for qt in range(4):
                for hf in range(2):
                    base = qt * 2048 + hf * 1024
                    qraw = self.bf(dstb, 1024, base)
                    @p.g("tensor", reads=[f"{dstb.name}.{qt}", "consts"],
                         writes=["ps0a", "ps0b"])
                    def _(eng, qraw=qraw):
                        eng.matmul(self.ps[0][:, 0:512], self.cview("C_ROTM"),
                                   qraw[:, 0:512], start=True, stop=True)
                        return eng.matmul(self.ps[0][:, 512:1024], self.cview("C_ROTM"),
                                          qraw[:, 512:1024], start=True, stop=True)
                    @p.g("vector", reads=[f"{dstb.name}.{qt}", ROT], writes=[TMP1])
                    def _(eng, qraw=qraw, ta=ta, hf=hf):
                        return eng.tensor_mul(self.f32(TMP1, 1024), qraw,
                                              self.f32(ROT, 1024, rot_off[ta] + hf * 1024))
                    @p.g("vector", reads=["ps0a", "ps0b", ROT], writes=[TMP2])
                    def _(eng, tb=tb, hf=hf):
                        return eng.tensor_mul(self.f32(TMP2, 1024),
                                              self.ps[0][:, 0:1024],
                                              self.f32(ROT, 1024, rot_off[tb] + hf * 1024))
                    @p.g("vector", reads=[TMP1, TMP2], writes=[f"{dstb.name}.{qt}"])
                    def _(eng, qraw=qraw):
                        return eng.tensor_add(qraw, self.f32(TMP1, 1024),
                                              self.f32(TMP2, 1024))
        p.free(TMP1)
        p.free(TMP2)
        p.free(ROT)
        self.tap(f"qf.{lt}", QF, "raw_bf16", rows=128, cols=4 * 2048)
        self.tap(f"kf.{lt}", KF, "raw_bf16", rows=128, cols=4 * 2048)
        WO = p.alloc(f"WO.{lt}", 4 * 1024 * 2)
        wosrc = self.inp["L0_ffn_wo"][l].rearrange("(k p) m -> p k m", p=128)
        p.dma("gpsimd", lambda eng, s=wosrc, b=WO: eng.dma_start(
            out=self.bf(b).rearrange("p (k m) -> p k m", k=4), in_=s),
            writes=[WO], n=1)
        self.local_attention(xb, l, lt, QF, KF, VT, WO)
        self.tap(f"xwo.{lt}", xb, "tok_f32")
        p.free(QF)
        p.free(KF)
        p.free(VT)
        p.free(WO)
        return self.ffn_mlp(xb, l, lt)


    def local_attention(self, xb, l, lt, QF, KF, VTE, WO):
        p = self.p
        OF = p.alloc(f"OF.{lt}", 4 * 2048 * 2)
        LM = p.alloc(f"LM.{lt}", 8 * 512 * 2)
        PL = p.alloc(f"PL.{lt}", 2 * 8 * 512 * 2)
        LNR = p.alloc(f"LNR.{lt}", 512 * 4)
        RCB = p.alloc(f"RCB.{lt}", 2 * 512 * 2)
        BCL = p.alloc(f"BCL.{lt}", 2 * 512 * 4)
        TFL = p.alloc(f"TFL.{lt}", 2 * 512 * 4)
        p.dma("sync", lambda eng: eng.dma_start(
            out=self.bf(LM, 8 * 512), in_=self.cst["C_LMASK"]),
            writes=[LM], n=1)
        blocks = [(h, w) for h in range(8) for w in range(4)]
        nB = len(blocks)

        def ks_of(w):
            return list(range(0, 8)) if w > 0 else list(range(4, 8))

        def st_thunks(b):
            h, w = blocks[b]
            qt, pb = h // 2, 64 * (h % 2)
            par = b % 2
            out = []
            for k in ks_of(w):
                jb = 4 * (w - 1) + k
                pst = "ps0a" if k % 2 == 0 else "ps0b"
                psv = (self.ps[0][:, 0:512] if k % 2 == 0
                       else self.ps[0][:, 512:1024])

                def mm(k=k, jb=jb, pst=pst, psv=psv):
                    @p.g("tensor", reads=[f"QF.{lt}.{qt}", f"KF.{lt}.{qt}"],
                         writes=[pst])
                    def _(eng):
                        return eng.matmul(
                            psv,
                            self.bf(KF, 128, qt * 2048 + jb * 128)[pb:pb + 64, :],
                            self.bf(QF, 512, qt * 2048 + w * 512)[pb:pb + 64, :],
                            start=True, stop=True, tile_position=(pb, 0))

                def ex(k=k, pst=pst, psv=psv):
                    @p.g("scalar", reads=[pst], writes=[f"PLx.{lt}.{par}.{k}"])
                    def _(eng):
                        return eng.activation(
                            self.bf(PL, 512, (par * 8 + k) * 512), psv, AF.Exp)

                def msk(k=k):
                    @p.g("vector", reads=[f"PLx.{lt}.{par}.{k}", LM],
                         writes=[f"PL.{lt}.{par}.{k}"])
                    def _(eng):
                        v = self.bf(PL, 512, (par * 8 + k) * 512)
                        return eng.tensor_mul(v, v, self.bf(LM, 512, k * 512))
                out.append((mm, ex, msk))
            return out

        def pv_thunk(b):
            h, w = blocks[b]
            par = b % 2
            ks = ks_of(w)
            pu = "ps1a" if par == 0 else "ps1b"
            puv = self.ps[1][:, 0:512] if par == 0 else self.ps[1][:, 512:1024]

            def g():
                @p.g("tensor",
                     reads=[f"PL.{lt}.{par}.{k}" for k in ks] +
                           [f"VT.{lt}.{4 * (w - 1) + k}" for k in ks] +
                           [f"VT.{lt}.ones"],
                     writes=[pu])
                def _(eng):
                    last = None
                    for i, k in enumerate(ks):
                        jb = 4 * (w - 1) + k
                        last = eng.matmul(
                            puv[0:65, :],
                            self.bf(VTE, 65, jb * 520 + h * 65),
                            self.bf(PL, 512, (par * 8 + k) * 512),
                            start=(i == 0), stop=(i == len(ks) - 1))
                    return last
            return g

        def norm_thunks(b):
            h, w = blocks[b]
            qt, pb = h // 2, 64 * (h % 2)
            par = b % 2
            pu = "ps1a" if par == 0 else "ps1b"
            puv = self.ps[1][:, 0:512] if par == 0 else self.ps[1][:, 512:1024]
            pbc = "ps2a" if par == 0 else "ps2b"
            pbv = self.ps[2][:, 0:512] if par == 0 else self.ps[2][:, 512:1024]

            def lnr():
                @p.g("scalar", reads=[pu], writes=[f"LNR.{lt}x"])
                def _(eng):
                    return eng.activation(self.f32(LNR, 512)[0:1, :],
                                          puv[64:65, :], AF.Ln)

            def rcb():
                @p.g("scalar", reads=[f"LNR.{lt}x"], writes=[f"RCB.{lt}.{par}"])
                def _(eng):
                    return eng.activation(self.bf(RCB, 512, par * 512)[0:1, :],
                                          self.f32(LNR, 512)[0:1, :],
                                          AF.Exp, scale=-1.0)

            def bc():
                @p.g("tensor", reads=[f"RCB.{lt}.{par}", "consts"], writes=[pbc])
                def _(eng):
                    return eng.matmul(pbv[0:64, :],
                                      self.cview("C_ONES1")[0:1, 0:64],
                                      self.bf(RCB, 512, par * 512)[0:1, :],
                                      start=True, stop=True)

            def evac():
                @p.g("vector", reads=[pbc], writes=[f"BCL.{lt}.{par}"])
                def _(eng):
                    return eng.tensor_copy(self.f32(BCL, 512, par * 512)[0:64, :],
                                           pbv[0:64, :])

            def mul():
                @p.g("vector", reads=[pu, f"BCL.{lt}.{par}"],
                     writes=[f"TFL.{lt}.{par}"])
                def _(eng):
                    return eng.tensor_mul(self.f32(TFL, 512, par * 512)[0:64, :],
                                          puv[0:64, :],
                                          self.f32(BCL, 512, par * 512)[0:64, :])

            def store():
                dst = self.arena_t[pb:pb + 64,
                                   OF.off + (qt * 2048 + 512 * w) * 2:
                                   OF.off + (qt * 2048 + 512 * w + 512) * 2]

                @p.g("vector", reads=[f"TFL.{lt}.{par}"], writes=[f"OF.{lt}.{qt}"])
                def _(eng):
                    return eng.tensor_copy(dst.bitcast(BF16),
                                           self.f32(TFL, 512, par * 512)[0:64, :])
            return [lnr, rcb, bc, evac, mul, store]

        for b in range(nB + 2):
            st = st_thunks(b) if b < nB else []
            pv = pv_thunk(b - 1) if 1 <= b <= nB else None
            nrm = norm_thunks(b - 2) if b >= 2 else []
            for i, (mm, ex, msk) in enumerate(st):
                mm()
                ex()
                msk()
                if i == 1 and pv is not None:
                    pv()
                    pv = None
                if i >= 2 and nrm:
                    nrm.pop(0)()
            if pv is not None:
                pv()
            for t in nrm:
                t()
        p.free(LM)
        p.free(PL)
        p.free(LNR)
        p.free(RCB)
        p.free(BCL)
        p.free(TFL)

        wo = self.bf(WO).rearrange("p (k m) -> p k m", k=4)
        for t in range(NT):
            pst = self.ps[1] if t % 2 == 0 else self.ps[2]
            ptoks = [f"ps{1 if t % 2 == 0 else 2}a", f"ps{1 if t % 2 == 0 else 2}b"]

            @p.g("tensor", reads=[f"OF.{lt}.{k}" for k in range(4)] + [WO],
                 writes=ptoks)
            def _(eng, t=t, pst=pst):
                last = None
                for c in range(2):
                    for k in range(4):
                        last = eng.matmul(pst[:, c * 512:(c + 1) * 512],
                                          self.bf(OF, 128, k * 2048 + 128 * t),
                                          wo[:, k, c * 512:(c + 1) * 512],
                                          start=(k == 0), stop=(k == 3))
                return last

            @p.g("vector", reads=ptoks + [f"{xb.name}.{t}"],
                 writes=[f"{xb.name}.{t}"])
            def _(eng, t=t, pst=pst):
                return eng.tensor_add(self.f32(xb, 1024, t * 1024),
                                      self.f32(xb, 1024, t * 1024), pst[:, 0:1024])
        p.free(OF)

    def ffn_mlp(self, xb, l, lt):
        p = self.p
        sq = p.alloc(f"sq2.{lt}", 1024 * 4)
        c_slo = self.stat_cols(f"slo.{lt}", 16)
        c_shi = self.stat_cols(f"shi.{lt}", 16)
        c_qhi = self.stat_cols(f"qhi.{lt}", 16)
        c_qlo = self.stat_cols(f"qlo.{lt}", 16)
        c_shis = self.stat_cols(f"shis.{lt}", 16)
        c_qhis = self.stat_cols(f"qhis.{lt}", 16)
        for t in range(NT):
            xt = self.f32(xb, 1024, t * 1024)

            @p.g("scalar", reads=[f"{xb.name}.{t}"], writes=[sq])
            def _(eng, xt=xt):
                return eng.activation(self.f32(sq, 1024), xt, AF.Square)

            @p.g("vector", reads=[sq], writes=[f"st.qh.{lt}"])
            def _(eng, t=t):
                eng.reduce_sum(self.sv(f"qlo.{lt}", t, 1),
                               self.f32(sq, 512, 0), axis=AX.X)
                return eng.reduce_sum(self.sv(f"qhi.{lt}", t, 1),
                                      self.f32(sq, 512, 512), axis=AX.X)

            @p.g("vector", reads=[f"{xb.name}.{t}"], writes=[f"st.sh.{lt}"])
            def _(eng, xt=xt, t=t):
                eng.reduce_sum(self.sv(f"slo.{lt}", t, 1), xt[:, 0:512], axis=AX.X)
                return eng.reduce_sum(self.sv(f"shi.{lt}", t, 1),
                                      xt[:, 512:1024], axis=AX.X)
        p.free(sq)
        hi0 = self._stat_map[f"shi.{lt}"][0]
        his0 = self._stat_map[f"shis.{lt}"][0]
        stat_bytes = lambda c: self.STAT.off + c * 4

        def emit_shift(eng):
            a = eng.dma_start(
                out=self.arena_t[1:128, stat_bytes(his0):stat_bytes(his0 + 32)].bitcast(F32),
                in_=self.arena_t[0:127, stat_bytes(hi0):stat_bytes(hi0 + 32)].bitcast(F32))
            b = eng.dma_start(
                out=self.arena_t[0:1, stat_bytes(his0):stat_bytes(his0 + 32)].bitcast(F32)
                .rearrange("p (a c) -> p a c", a=2)[:, :, 1:16],
                in_=self.arena_t[127:128, stat_bytes(hi0):stat_bytes(hi0 + 32)].bitcast(F32)
                .rearrange("p (a c) -> p a c", a=2)[:, :, 0:15])
            return [a, b]
        p.dma("sync", emit_shift, reads=[f"st.sh.{lt}", f"st.qh.{lt}"],
              writes=[f"st.shs.{lt}"], n=2)

        @p.g("vector", reads=[f"st.shs.{lt}"], writes=[f"st.shs.{lt}"])
        def _(eng):
            return eng.memset(
                self.arena_t[0:1, stat_bytes(his0):stat_bytes(his0 + 32)].bitcast(F32)
                .rearrange("p (a c) -> p a c", a=2)[:, :, 0:1], 0.0)

        self.stat_cols(f"s.{lt}", 16)
        self.stat_cols(f"q.{lt}", 16)

        @p.g("vector", reads=[f"st.sh.{lt}", f"st.shs.{lt}"],
             writes=[f"st.s.{lt}", f"st.q.{lt}"])
        def _(eng):
            eng.tensor_add(self.sv(f"s.{lt}"), self.sv(f"slo.{lt}"),
                           self.sv(f"shis.{lt}"))
            return eng.tensor_add(self.sv(f"q.{lt}"), self.sv(f"qlo.{lt}"),
                                  self.sv(f"qhis.{lt}"))
        self.ln_finalize(lt)
        c_r = self._stat_map[f"r.{lt}"][0]
        c_negmr = self._stat_map[f"negmr.{lt}"][0]
        assert c_negmr == c_r + 16
        c_rs = self.stat_cols(f"rs.{lt}", 16)
        c_negmrs = self.stat_cols(f"negmrs.{lt}", 16)
        assert c_negmrs == c_rs + 16

        def emit_shift2(eng):
            a = eng.dma_start(
                out=self.arena_t[0:127, stat_bytes(c_rs):stat_bytes(c_rs + 32)].bitcast(F32),
                in_=self.arena_t[1:128, stat_bytes(c_r):stat_bytes(c_r + 32)].bitcast(F32))
            b = eng.dma_start(
                out=self.arena_t[127:128, stat_bytes(c_rs):stat_bytes(c_rs + 32)].bitcast(F32)
                .rearrange("p (a c) -> p a c", a=2)[:, :, 0:15],
                in_=self.arena_t[0:1, stat_bytes(c_r):stat_bytes(c_r + 32)].bitcast(F32)
                .rearrange("p (a c) -> p a c", a=2)[:, :, 1:16])
            c1 = eng.dma_start(
                out=self.arena_t[127:128, stat_bytes(c_rs + 15):stat_bytes(c_rs + 16)].bitcast(F32),
                in_=self.inp["BPACK"][127:128, BPACK_MAP["c_eps"][0]:BPACK_MAP["c_eps"][0] + 1])
            c2 = eng.dma_start(
                out=self.arena_t[127:128, stat_bytes(c_negmrs + 15):stat_bytes(c_negmrs + 16)].bitcast(F32),
                in_=self.inp["BPACK"][127:128, BPACK_MAP["c_eps"][0]:BPACK_MAP["c_eps"][0] + 1])
            return [a, b, c1, c2]
        p.dma("sync", emit_shift2, reads=[f"st.r.{lt}", f"st.negmr.{lt}"],
              writes=[f"st.rs.{lt}"], n=4)

        XS = p.alloc(f"XS.{lt}", 16 * 1024 * 2)
        for t in range(NT):
            xt = self.f32(xb, 1024, t * 1024)

            @p.g("scalar", reads=[f"{xb.name}.{t}", f"st.rs.{lt}"],
                 writes=[f"XS.{lt}.{t}"],
                 ptr_reads=[f"st.r.{lt}", f"st.negmr.{lt}", f"st.rs.{lt}"])
            def _(eng, xt=xt, t=t):
                eng.activation(self.bf(XS, 512, t * 1024), xt[:, 0:512], AF.Identity,
                               scale=self.sv(f"r.{lt}", t, 1),
                               bias=self.sv(f"negmr.{lt}", t, 1))
                return eng.activation(self.bf(XS, 512, t * 1024 + 512),
                                      xt[:, 512:1024], AF.Identity,
                                      scale=self.sv(f"rs.{lt}", t, 1),
                                      bias=self.sv(f"negmrs.{lt}", t, 1))
        HF2 = p.alloc(f"HF2.{lt}", 8 * 2048 * 2)
        self.transpose_tokmajor(
            XS, 1024, HF2, 8, lt, shift_ftiles=(4, 5, 6, 7),
            affine=(lambda ft: self.bcol(f"lng.{l}", ft),
                    lambda ft: self.bcol(f"lnb.{l}", ft)))
        p.free(XS)
        self.tap(f"hf2.{lt}", HF2, "raw_bf16", rows=128, cols=8 * 2048)
        self.spill(xb, self.sp_xb, f"mlp.{lt}")
        p.free(xb)

        H2 = p.alloc(f"H2.{lt}", 22 * 2048 * 2)
        GEL = p.alloc(f"GEL.{lt}", 1024 * 2)
        w1 = self.inp["L0_ffn_w1"][l]
        slab_cols = {}

        def load_w1_half(half):
            a0, a1 = (0, 1408) if half == 0 else (1408, 2730)
            g0, g1 = (2730 + a0, 2730 + a1)
            wa = p.alloc(f"W1A{half}.{lt}", 8 * (a1 - a0) * 2)
            wg = p.alloc(f"W1G{half}.{lt}", 8 * (a1 - a0) * 2)
            p.dma("gpsimd", lambda eng, wa=wa, a0=a0, a1=a1: eng.dma_start(
                out=self.bf(wa, 8 * (a1 - a0)).rearrange("p (k m) -> p k m", k=8),
                in_=w1[:, a0:a1].rearrange("(k p) m -> p k m", p=128)),
                writes=[wa], n=1)
            p.dma("gpsimd", lambda eng, wg=wg, g0=g0, g1=g1: eng.dma_start(
                out=self.bf(wg, 8 * (g1 - g0)).rearrange("p (k m) -> p k m", k=8),
                in_=w1[:, g0:g1].rearrange("(k p) m -> p k m", p=128)),
                writes=[wg], n=1)
            slab_cols[half] = (wa, wg, a0, a1)
        load_w1_half(0)
        for mp in range(22):
            half = 0 if mp < 11 else 1
            wa, wg, a0, a1 = slab_cols[half]
            c0 = mp * 128 - a0
            mw = min(128, 2730 - mp * 128)
            wav = self.bf(wa, 8 * (a1 - a0)).rearrange("p (k m) -> p k m", k=8)
            wgv = self.bf(wg, 8 * (a1 - a0)).rearrange("p (k m) -> p k m", k=8)
            for th in range(2):
                pa = self.ps[0] if (mp + th) % 2 == 0 else self.ps[1]
                pg = self.ps[1] if (mp + th) % 2 == 0 else self.ps[0]
                patok = ["ps0a", "ps0b"] if (mp + th) % 2 == 0 else ["ps1a", "ps1b"]
                pgtok = ["ps1a", "ps1b"] if (mp + th) % 2 == 0 else ["ps0a", "ps0b"]

                @p.g("tensor", reads=[wa, wg] + [f"HF2.{lt}.{k}" for k in range(8)],
                     writes=patok + pgtok)
                def _(eng, wav=wav, wgv=wgv, c0=c0, mw=mw, th=th, pa=pa, pg=pg):
                    last = None
                    for dst, wsl in ((pa, wav), (pg, wgv)):
                        for c in range(2):
                            tc0 = th * 1024 + c * 512
                            for k in range(8):
                                last = eng.matmul(
                                    dst[0:mw, c * 512:(c + 1) * 512],
                                    wsl[:, k, c0:c0 + mw],
                                    self.bf(HF2, 512, k * 2048 + tc0),
                                    start=(k == 0), stop=(k == 7))
                    return last

                @p.g("scalar", reads=pgtok + ["bpack"], writes=[GEL])
                def _(eng, pg=pg, mp=mp, mw=mw):
                    return eng.activation(self.bf(GEL, 1024)[0:mw, :],
                                          pg[0:mw, 0:1024], AF.Gelu,
                                          bias=self.bcol(f"b1g.{l}", mp)[0:mw, :])

                @p.g("vector", reads=patok + [GEL, "bpack"],
                     writes=[f"H2.{lt}.{mp}"])
                def _(eng, pa=pa, mp=mp, mw=mw, th=th):
                    return eng.scalar_tensor_tensor(
                        self.bf(H2, 1024, mp * 2048 + th * 1024)[0:mw, :],
                        pa[0:mw, 0:1024], self.bcol(f"b1a.{l}", mp)[0:mw, :],
                        self.bf(GEL, 1024)[0:mw, :], ALU.add, ALU.mult)
            if mp == 10:
                p.free(slab_cols[0][0])
                p.free(slab_cols[0][1])
                load_w1_half(1)
        p.free(slab_cols[1][0])
        p.free(slab_cols[1][1])
        p.free(GEL)
        p.free(HF2)
        self.tap(f"h2.{lt}", H2, "raw_bf16", rows=128, cols=22 * 2048)

        xb2 = p.alloc(f"X2.{lt}", T * D * 4 // 128)
        dst = self.f32(xb2).rearrange("p (t d) -> p t d", t=NT)
        srcd = self.sp_xb.rearrange("(t p) d -> p t d", p=128)
        p.dma("sync", lambda eng: eng.dma_start(out=dst, in_=srcd),
              reads=[f"dram.mlp.{lt}"],
              writes=[xb2] + [f"{xb2.name}.{t}" for t in range(NT)], n=1)
        TMPW = p.alloc(f"TMPW.{lt}", 512 * 4)
        B2B = self.load_bbc(f"b2f.{l}")
        w2 = self.inp["L0_ffn_w2"][l]
        WQN = 4
        wcols = 1024 // WQN
        for half in range(WQN):
            m0 = half * wcols
            ws = p.alloc(f"W2H{half}.{lt}", 22 * wcols * 2)

            def emit_w2(eng, ws=ws, m0=m0):
                a = eng.dma_start(
                    out=self.bf(ws).rearrange("p (k m) -> p k m", k=22)[:, 0:21, :],
                    in_=w2[0:2688, m0:m0 + wcols].rearrange("(k p) m -> p k m", p=128))
                b = eng.dma_start(
                    out=self.bf(ws).rearrange("p (k m) -> p k m", k=22)[0:42, 21:22, :],
                    in_=w2[2688:2730, m0:m0 + wcols].rearrange("(k p) m -> p k m", p=42))
                return [a, b]
            p.dma("gpsimd", emit_w2, writes=[ws], n=2)
            wsv = self.bf(ws).rearrange("p (k m) -> p k m", k=22)
            for t in range(NT):
                pst = "ps2a" if t % 2 == 0 else "ps2b"
                psv = (self.ps[2][:, 0:wcols] if t % 2 == 0
                       else self.ps[2][:, 512:512 + wcols])

                @p.g("tensor", reads=[ws] + [f"H2.{lt}.{k}" for k in range(22)],
                     writes=[pst])
                def _(eng, t=t, psv=psv, wsv=wsv):
                    last = None
                    for k in range(22):
                        kw = min(128, 2730 - k * 128)
                        last = eng.matmul(
                            psv,
                            self.bf(H2, 128, k * 2048 + t * 128)[0:kw, :],
                            wsv[0:kw, k, :],
                            start=(k == 0), stop=(k == 21))
                    return last

                @p.g("vector", reads=[pst, B2B], writes=[TMPW])
                def _(eng, psv=psv, half=half):
                    return eng.tensor_add(self.f32(TMPW, wcols), psv,
                                          self.bbc_view(B2B, half * wcols, wcols))

                @p.g("vector", reads=[TMPW, f"{xb2.name}.{t}"],
                     writes=[f"{xb2.name}.{t}"])
                def _(eng, t=t, m0=m0):
                    xv = self.f32(xb2, wcols, t * 1024 + m0)
                    return eng.tensor_add(xv, xv, self.f32(TMPW, wcols))
            p.free(ws)
        p.free(TMPW)
        p.free(B2B)
        p.free(H2)
        self.tap(f"xmlp.{lt}", xb2, "tok_f32")
        return xb2


    def tap(self, name, buf, kind, rows=128, cols=None):
        if name not in self.debug_taps:
            return
        p = self.p
        if kind == "tok_f32":
            o = self.nc.dram_tensor(f"tap_{name}", [T, D], F32,
                                    kind="ExternalOutput").ap()
            srcv = self.f32(buf).rearrange("p (t d) -> p t d", t=NT)
            dstv = o.rearrange("(t p) d -> p t d", p=128)
            rd = [buf] + [f"{buf.name}.{t}" for t in range(NT)]
        elif kind == "tok_bf16":
            o = self.nc.dram_tensor(f"tap_{name}", [T, buf.nbytes // 2 // NT], BF16,
                                    kind="ExternalOutput").ap()
            srcv = self.bf(buf).rearrange("p (t d) -> p t d", t=NT)
            dstv = o.rearrange("(t p) d -> p t d", p=128)
            rd = [buf] + [f"{buf.name}.{t}" for t in range(NT)]
        elif kind == "raw_bf16":
            cols = cols or buf.nbytes // 2
            o = self.nc.dram_tensor(f"tap_{name}", [rows, cols], BF16,
                                    kind="ExternalOutput").ap()
            srcv = self.bf(buf, cols)[0:rows, :]
            dstv = o
            rd = [buf] + [f"{buf.name}.{t}" for t in range(24)]
        else:
            cols = cols or buf.nbytes // 4
            o = self.nc.dram_tensor(f"tap_{name}", [rows, cols], F32,
                                    kind="ExternalOutput").ap()
            srcv = self.f32(buf, cols)[0:rows, :]
            dstv = o
            rd = [buf] + [f"{buf.name}.{t}" for t in range(24)]
        p.dma("sync", lambda eng: eng.dma_start(out=dstv, in_=srcv),
              reads=rd, writes=[f"tap.{name}"], n=1)
        self.taps_done.append(name)

    def transp_weight(self, wname, oc, ic, tag, oc_group=None):
        p = self.p
        O, I = oc * 128, ic * 128
        og = oc_group or oc
        wt = p.alloc(f"WT.{tag}", ic * O * 2)
        for g0 in range(0, oc, og):
            n_oc = min(og, oc - g0)
            slab = p.alloc(f"WN.{tag}.{g0}", n_oc * I * 2)
            src = self.inp[wname][g0 * 128:(g0 + n_oc) * 128].rearrange(
                "(k p) m -> p k m", p=128)
            p.dma("gpsimd", lambda eng, slab=slab, src=src, n_oc=n_oc: eng.dma_start(
                out=self.bf(slab).rearrange("p (k m) -> p k m", k=n_oc), in_=src),
                writes=[slab], n=1)
            for icc in range(ic):
                for g in range(ceil_div(n_oc, 8)):
                    n = min(8, n_oc - g * 8)

                    @p.g("tensor", reads=[slab, "consts"], writes=["ps3t"])
                    def _(eng, icc=icc, g=g, n=n, slab=slab):
                        last = None
                        for s in range(n):
                            last = eng.transpose(
                                self.ps3t[:, s * 128:(s + 1) * 128],
                                self.bf(slab, 128, (g * 8 + s) * I + icc * 128),
                                self.cview("C_ID"))
                        return last

                    @p.g("vector", reads=["ps3t"], writes=[wt])
                    def _(eng, icc=icc, g=g, n=n, g0=g0):
                        return eng.tensor_copy(
                            self.bf(wt, n * 128, icc * O + (g0 + g * 8) * 128),
                            self.ps3t[:, 0:n * 128])
            p.free(slab)
        return wt

    def attn_sublayer(self, xa):
        p = self.p
        tag = "AT"
        self.ln_stats_plain(xa, tag)
        self.ln_finalize(tag, extra_r2=True)
        yb = p.alloc("Y.AT", T * D * 4 // 128)
        for t in range(NT):
            @p.g("scalar", reads=[f"{xa.name}.{t}"], writes=[f"Y.AT.{t}"],
                 ptr_reads=[f"st.r.{tag}", f"st.negmr.{tag}"])
            def _(eng, t=t):
                return eng.activation(self.f32(yb, 1024, t * 1024),
                                      self.f32(xa, 1024, t * 1024), AF.Identity,
                                      scale=self.sv(f"r.{tag}", t, 1),
                                      bias=self.sv(f"negmr.{tag}", t, 1))
        self.spill(yb, self.sp_xb, "res")
        p.free(yb)
        XN = p.alloc("XN.AT", 16 * 1024 * 2)
        for t in range(NT):
            @p.g("scalar", reads=[f"{xa.name}.{t}", f"st.r2.{tag}"],
                 writes=[f"XN.AT.{t}"],
                 ptr_reads=[f"st.r2.{tag}", f"st.negmr2.{tag}"])
            def _(eng, t=t):
                return eng.activation(self.bf(XN, 1024, t * 1024),
                                      self.f32(xa, 1024, t * 1024), AF.Identity,
                                      scale=self.sv(f"r2.{tag}", t, 1),
                                      bias=self.sv(f"negmr2.{tag}", t, 1))
        self.spill(xa, self.sp_xa, "x1")
        p.free(xa)
        XNF = p.alloc("XNF.AT", 8 * 2048 * 2)
        self.transpose_tokmajor(
            XN, 1024, XNF, 8, tag,
            affine=(lambda ft: self.bcol("g1", ft), lambda ft: self.bcol("b1", ft)))
        p.free(XN)
        self.tap("xnf", XNF, "raw_bf16", rows=128, cols=8 * 2048)
        WQT = self.transp_weight("L0_attn_Wq", 8, 8, "WQT")
        WKT = self.transp_weight("L0_attn_Wk", 8, 8, "WKT")
        WVT = self.transp_weight("L0_attn_Wv", 4, 8, "WVT")
        WQVT = self.transp_weight("L0_attn_Wqv", 4, 8, "WQVT")
        QFg = p.alloc("QFg", 8 * 2048 * 2)
        KFg = p.alloc("KFg", 8 * 2048 * 2)
        xnf = lambda k, c, n=512: self.bf(XNF, n, k * 2048 + c)
        xnf_tok = [f"XNF.AT.{k}" for k in range(8)]
        for m in range(8):
            for half in range(2):
                ps_q = self.ps[0] if half == 0 else self.ps[1]
                qtok = ["ps0a", "ps0b"] if half == 0 else ["ps1a", "ps1b"]

                @p.g("tensor", reads=[WQT] + xnf_tok, writes=qtok)
                def _(eng, m=m, half=half, ps_q=ps_q):
                    last = None
                    for c in range(2):
                        for k in range(8):
                            last = eng.matmul(
                                ps_q[:, c * 512:(c + 1) * 512],
                                self.bf(WQT, 128, k * 1024 + m * 128),
                                xnf(k, half * 1024 + c * 512),
                                start=(k == 0), stop=(k == 7))
                    return last

                @p.g("vector", reads=qtok + ["bpack"], writes=[f"QFg.{m}"])
                def _(eng, m=m, half=half, ps_q=ps_q):
                    return eng.tensor_scalar(
                        self.bf(QFg, 1024, m * 2048 + half * 1024),
                        ps_q[:, 0:1024], self.bcol("bq", m), 0.125,
                        ALU.add, ALU.mult)

                ps_k = self.ps[1] if half == 0 else self.ps[0]
                ktok = ["ps1a", "ps1b"] if half == 0 else ["ps0a", "ps0b"]

                @p.g("tensor", reads=[WKT] + xnf_tok, writes=ktok)
                def _(eng, m=m, half=half, ps_k=ps_k):
                    last = None
                    for c in range(2):
                        for k in range(8):
                            last = eng.matmul(
                                ps_k[:, c * 512:(c + 1) * 512],
                                self.bf(WKT, 128, k * 1024 + m * 128),
                                xnf(k, half * 1024 + c * 512),
                                start=(k == 0), stop=(k == 7))
                    return last

                @p.g("scalar", reads=ktok + ["bpack"], writes=[f"KFg.{m}"])
                def _(eng, m=m, half=half, ps_k=ps_k):
                    return eng.activation(
                        self.bf(KFg, 1024, m * 2048 + half * 1024),
                        ps_k[:, 0:1024], AF.Identity, bias=self.bcol("bk", m))
        p.free(WQT)
        p.free(WKT)
        VTg = p.alloc("VTg", 16 * 512 * 2)
        BVB = self.load_bbc("bv")
        for t in range(NT):
            pv = self.ps[2][:, 0:512] if t % 2 == 0 else self.ps[2][:, 512:1024]
            ptok = "ps2a" if t % 2 == 0 else "ps2b"

            @p.g("tensor", reads=[WVT] + xnf_tok, writes=[ptok])
            def _(eng, t=t, pv=pv):
                last = None
                for k in range(8):
                    last = eng.matmul(pv, xnf(k, t * 128, 128),
                                      self.bf(WVT, 512, k * 512),
                                      start=(k == 0), stop=(k == 7))
                return last

            @p.g("vector", reads=[ptok, BVB], writes=[f"VTg.{t}"])
            def _(eng, t=t, pv=pv):
                return eng.tensor_add(self.bf(VTg, 512, t * 512), pv,
                                      self.bbc_view(BVB))
        p.free(WVT)
        p.free(BVB)
        QVF = p.alloc("QVF", 4 * 2048 * 2)
        for m in range(4):
            for half in range(2):
                pq = self.ps[0] if half == 0 else self.ps[1]
                qtok = ["ps0a", "ps0b"] if half == 0 else ["ps1a", "ps1b"]

                @p.g("tensor", reads=[WQVT] + xnf_tok, writes=qtok)
                def _(eng, m=m, half=half, pq=pq):
                    last = None
                    for c in range(2):
                        for k in range(8):
                            last = eng.matmul(
                                pq[:, c * 512:(c + 1) * 512],
                                self.bf(WQVT, 128, k * 512 + m * 128),
                                xnf(k, half * 1024 + c * 512),
                                start=(k == 0), stop=(k == 7))
                    return last

                @p.g("vector", reads=qtok + ["bpack"], writes=[f"QVF.{m}"])
                def _(eng, m=m, half=half, pq=pq):
                    return eng.tensor_scalar(
                        self.bf(QVF, 1024, m * 2048 + half * 1024),
                        pq[:, 0:1024], self.bcol("bqv", m),
                        1.0 / math.sqrt(32.0), ALU.add, ALU.mult)
        p.free(WQVT)
        p.free(XNF)
        VWS9 = p.alloc("VWS9", 16 * 9 * 2)
        TMPV = p.alloc("TMPV", 512 * 4)
        VWS8 = p.alloc("VWS8", 8 * 4)
        WSB = self.load_bbc("wsout")

        @p.g("vector", writes=["VWS9.ones"])
        def _(eng):
            return eng.memset(
                self.bf(VWS9, 144).rearrange("p (t n) -> p t n", t=16)[:, :, 8:9],
                1.0)
        for jt in range(NT):
            @p.g("vector", reads=[f"VTg.{jt}", WSB], writes=[TMPV])
            def _(eng, jt=jt):
                return eng.tensor_mul(self.f32(TMPV, 512),
                                      self.bf(VTg, 512, jt * 512),
                                      self.bbc_view(WSB))

            @p.g("vector", reads=[TMPV], writes=["VWS8.x"])
            def _(eng, jt=jt):
                return eng.reduce_sum(
                    self.f32(VWS8, 8),
                    self.f32(TMPV, 512).rearrange("p (h c) -> p h c", h=8),
                    axis=AX.X)

            @p.g("vector", reads=["VWS8.x"], writes=[f"VWS9.{jt}"])
            def _(eng, jt=jt):
                return eng.tensor_copy(self.bf(VWS9, 8, jt * 9), self.f32(VWS8, 8))
        p.free(TMPV)
        p.free(VWS8)
        p.free(WSB)
        QVD = p.alloc("QVD", 4 * 2048 * 2)
        for qq in range(4):
            for ib in range(4):
                @p.g("tensor", reads=[f"QVF.{qq}", self.WSBLK], writes=["ps3a"])
                def _(eng, qq=qq, ib=ib):
                    return eng.matmul(self.ps3a[0:4, :], self.bf(self.WSBLK, 4),
                                      self.bf(QVF, 512, qq * 2048 + ib * 512),
                                      start=True, stop=True)

                @p.g("scalar", reads=["ps3a"], writes=[f"QVD.{qq}.{ib}"])
                def _(eng, qq=qq, ib=ib):
                    return eng.copy(
                        self.bf(QVD, 512, qq * 2048 + ib * 512)[0:4, :],
                        self.ps3a[0:4, :])
        p.free(QVF)
        self.tap("qfg", QFg, "raw_bf16", rows=128, cols=8 * 2048)
        self.tap("kfg", KFg, "raw_bf16", rows=128, cols=8 * 2048)
        self.tap("vtg", VTg, "tok_bf16")
        return self.attn_main(VTg, QFg, KFg, VWS9, QVD)

    def attn_main(self, VTg, QFg, KFg, VWS9, QVD):
        p = self.p
        XMF = p.alloc("XMF", 8 * 2048 * 2)
        PTB = [p.alloc("PTB0", 16 * 512 * 2), p.alloc("PTB1", 16 * 512 * 2)]
        USB = p.alloc("USB", 4 * 512 * 2)
        C9 = p.alloc("C9", 512 * 4)
        NROW = p.alloc("NROW", 512 * 4)
        RECNF = p.alloc("RECNF", 512 * 4)
        RECNB = p.alloc("RECNB", 512 * 2)
        RECN8 = p.alloc("RECN8", 512 * 4)
        ZA = p.alloc("ZA", 512 * 4)
        E8 = p.alloc("E8", 512 * 2)
        M8 = p.alloc("M8", 512 * 2)
        RECEF = p.alloc("RECEF", 512 * 4)
        REBF = p.alloc("REBF", 512 * 2)
        BCC = p.alloc("BCC", 2 * 512 * 2)
        TACC = p.alloc("TACC", 512 * 4)
        TC = p.alloc("TC", 512 * 4)
        ps3tf = self.ps3t[:, 0:1024].bitcast(F32)
        utoks = ["ps0a", "ps0b", "ps1a", "ps1b"]
        ups = [self.ps[0][:, 0:512], self.ps[0][:, 512:1024],
               self.ps[1][:, 0:512], self.ps[1][:, 512:1024]]
        vtg_toks = [f"VTg.{jt}" for jt in range(NT)]
        vws_toks = [f"VWS9.{jt}" for jt in range(NT)] + ["VWS9.ones"]

        iters = [(h, ib) for h in range(16) for ib in range(4)]
        nS = len(iters)

        def st_thunks(s):
            h, ib = iters[s]
            ft, pb = h // 2, 64 * (h % 2)
            par = s % 2
            out = []
            for jt in range(NT):
                pst = "ps2a" if jt % 2 == 0 else "ps2b"
                psv = (self.ps[2][:, 0:512] if jt % 2 == 0
                       else self.ps[2][:, 512:1024])

                def mm(jt=jt, pst=pst, psv=psv):
                    @p.g("tensor", reads=[f"KFg.{ft}", f"QFg.{ft}"], writes=[pst])
                    def _(eng):
                        return eng.matmul(
                            psv,
                            self.bf(KFg, 128, ft * 2048 + jt * 128)[pb:pb + 64, :],
                            self.bf(QFg, 512, ft * 2048 + ib * 512)[pb:pb + 64, :],
                            start=True, stop=True, tile_position=(pb, 0))

                def ex(jt=jt, pst=pst, psv=psv):
                    @p.g("scalar", reads=[pst], writes=[f"PTB{par}.{jt}"])
                    def _(eng):
                        return eng.activation(self.bf(PTB[par], 512, jt * 512),
                                              psv, AF.Exp)
                out.append((mm, ex))
            return out

        def pv_thunks(s):
            par = s % 2
            ptoks = [f"PTB{par}.{jt}" for jt in range(NT)]
            out = []
            for c in range(4):
                def g(c=c):
                    @p.g("tensor", reads=ptoks + vtg_toks, writes=[utoks[c]])
                    def _(eng):
                        last = None
                        for jt in range(NT):
                            last = eng.matmul(
                                ups[c],
                                self.bf(VTg, 128, jt * 512 + c * 128),
                                self.bf(PTB[par], 512, jt * 512),
                                start=(jt == 0), stop=(jt == NT - 1))
                        return last
                out.append(g)

            def gcombo():
                @p.g("tensor", reads=ptoks + vws_toks, writes=["ps3a"])
                def _(eng):
                    last = None
                    for jt in range(NT):
                        last = eng.matmul(
                            self.ps3a[0:9, :],
                            self.bf(VWS9, 9, jt * 9),
                            self.bf(PTB[par], 512, jt * 512),
                            start=(jt == 0), stop=(jt == NT - 1))
                    return last
            out.append(gcombo)
            return out

        def uevac_thunks(s):
            out = []
            for c in range(4):
                def g(c=c):
                    @p.g("vector", reads=[utoks[c]], writes=[f"USB.{c}"])
                    def _(eng):
                        return eng.tensor_copy(self.bf(USB, 512, c * 512),
                                               ups[c])
                out.append(g)
            return out

        def epia_thunks(s):
            def c9_evac():
                @p.g("scalar", reads=["ps3a"], writes=["C9"])
                def _(eng):
                    return eng.copy(self.f32(C9, 512)[0:9, :], self.ps3a[0:9, :])

            def nrow_dma():
                def emit(eng):
                    return eng.dma_start(
                        out=self.arena_t[0:1, NROW.off:NROW.off + 2048].bitcast(F32),
                        in_=self.arena_t[8:9, C9.off:C9.off + 2048].bitcast(F32))
                p.dma("gpsimd", emit, reads=["C9"], writes=["NROW"], n=1)

            def recn_f():
                @p.g("vector", reads=["NROW"], writes=["RECNF"])
                def _(eng):
                    return eng.reciprocal(self.f32(RECNF, 512)[0:1, :],
                                          self.f32(NROW, 512)[0:1, :])

            def recn_b():
                @p.g("vector", reads=["RECNF"], writes=["RECNB"])
                def _(eng):
                    return eng.tensor_copy(self.bf(RECNB, 512)[0:1, :],
                                          self.f32(RECNF, 512)[0:1, :])
            return [c9_evac, nrow_dma, recn_f, recn_b]

        def epib_thunks(s):
            h, ib = iters[s]
            ft, pb = h // 2, 64 * (h % 2)
            qq, hm = h // 4, h % 4
            out = []

            def bc_recn():
                @p.g("tensor", reads=["RECNB", "consts"], writes=["ps3t"])
                def _(eng):
                    return eng.matmul(ps3tf[0:8, :],
                                      self.cview("C_ONES1")[0:1, 0:8],
                                      self.bf(RECNB, 512)[0:1, :],
                                      start=True, stop=True)

            def rec8_evac():
                @p.g("vector", reads=["ps3t"], writes=["RECN8"])
                def _(eng):
                    return eng.tensor_copy(self.f32(RECN8, 512)[0:8, :],
                                           ps3tf[0:8, :])

            def za():
                @p.g("vector", reads=["C9", "ps3t"], writes=["ZA"])
                def _(eng):
                    return eng.tensor_mul(self.f32(ZA, 512)[0:8, :],
                                          self.f32(C9, 512)[0:8, :],
                                          ps3tf[0:8, :])

            def bc_qvd():
                @p.g("tensor", reads=[f"QVD.{qq}.{ib}", "consts"], writes=["ps3t"])
                def _(eng):
                    return eng.matmul(
                        ps3tf[0:8, :],
                        self.cview("C_IND4")[0:4, hm * 8:hm * 8 + 8],
                        self.bf(QVD, 512, qq * 2048 + ib * 512)[0:4, :],
                        start=True, stop=True)

            def z_add():
                @p.g("vector", reads=["ZA", "ps3t"], writes=["ZA"])
                def _(eng):
                    return eng.tensor_add(self.f32(ZA, 512)[0:8, :],
                                          self.f32(ZA, 512)[0:8, :],
                                          ps3tf[0:8, :])

            def e_exp():
                @p.g("scalar", reads=["ZA", "bpack"], writes=["E8"],
                     ptr_reads=["bpack"])
                def _(eng):
                    return eng.activation(self.bf(E8, 512)[0:8, :],
                                          self.f32(ZA, 512)[0:8, :], AF.Exp,
                                          bias=self.bcol("bs")[0:8, :])

            def m8():
                @p.g("vector", reads=["E8", "RECN8"], writes=["M8"])
                def _(eng):
                    return eng.tensor_mul(self.bf(M8, 512)[0:8, :],
                                          self.bf(E8, 512)[0:8, :],
                                          self.f32(RECN8, 512)[0:8, :])

            out += [bc_recn, rec8_evac, za, bc_qvd, z_add, e_exp, m8]
            for c in range(4):
                def ind8(c=c):
                    @p.g("tensor", reads=["M8", "consts"], writes=["ps3t"])
                    def _(eng):
                        return eng.matmul(
                            ps3tf[:, :],
                            self.cview("C_IND8")[0:8, c * 128:(c + 1) * 128],
                            self.bf(M8, 512)[0:8, :],
                            start=True, stop=True)

                def bcc(c=c):
                    @p.g("scalar", reads=["ps3t"], writes=[f"BCC.{c % 2}"])
                    def _(eng):
                        return eng.copy(self.bf(BCC, 512, (c % 2) * 512),
                                        ps3tf[:, :])

                def mul(c=c):
                    if c == 0:
                        @p.g("vector", reads=["USB.0", "BCC.0"], writes=["TACC"])
                        def _(eng):
                            return eng.tensor_mul(self.f32(TACC, 512),
                                                  self.bf(USB, 512, 0),
                                                  self.bf(BCC, 512, 0))
                    else:
                        @p.g("vector", reads=[f"USB.{c}", f"BCC.{c % 2}"],
                             writes=["TC"])
                        def _(eng):
                            return eng.tensor_mul(self.f32(TC, 512),
                                                  self.bf(USB, 512, c * 512),
                                                  self.bf(BCC, 512, (c % 2) * 512))

                def acc(c=c):
                    @p.g("vector", reads=["TC", "TACC"], writes=["TACC"])
                    def _(eng):
                        return eng.tensor_add(self.f32(TACC, 512),
                                              self.f32(TACC, 512),
                                              self.f32(TC, 512))
                out.append(ind8)
                out.append(bcc)
                out.append(mul)
                if c > 0:
                    out.append(acc)

            def e_sum():
                @p.g("tensor", reads=["E8", "consts"], writes=["ps3t"])
                def _(eng):
                    return eng.matmul(ps3tf[0:1, :],
                                      self.cview("C_IND8")[0:8, 512:513],
                                      self.bf(E8, 512)[0:8, :],
                                      start=True, stop=True)

            def recip_e():
                @p.g("vector", reads=["ps3t"], writes=["RECEF"])
                def _(eng):
                    return eng.reciprocal(self.f32(RECEF, 512)[0:1, :],
                                          ps3tf[0:1, :])

            def rebf():
                @p.g("vector", reads=["RECEF"], writes=["REBF"])
                def _(eng):
                    return eng.tensor_copy(self.bf(REBF, 512)[0:1, :],
                                           self.f32(RECEF, 512)[0:1, :])

            def bc64():
                @p.g("tensor", reads=["REBF", "consts"], writes=["ps3t"])
                def _(eng):
                    return eng.matmul(ps3tf[0:64, :],
                                      self.cview("C_ONES1")[0:1, 0:64],
                                      self.bf(REBF, 512)[0:1, :],
                                      start=True, stop=True)

            def fold_cp():
                @p.g("vector", reads=["TACC"], writes=["TC"])
                def _(eng):
                    return eng.tensor_copy(self.f32(TC, 512)[0:64, :],
                                           self.f32(TACC, 512)[64:128, :])

            def fold_add():
                @p.g("vector", reads=["TACC", "TC"], writes=["TACC"])
                def _(eng):
                    return eng.tensor_add(self.f32(TACC, 512)[0:64, :],
                                          self.f32(TACC, 512)[0:64, :],
                                          self.f32(TC, 512)[0:64, :])

            def final_mul():
                @p.g("vector", reads=["TACC", "ps3t"], writes=["TC"])
                def _(eng):
                    return eng.tensor_mul(self.f32(TC, 512)[0:64, :],
                                          self.f32(TACC, 512)[0:64, :],
                                          ps3tf[0:64, :])

            def final():
                dst = self.arena_t[pb:pb + 64,
                                   XMF.off + (ft * 2048 + ib * 512) * 2:
                                   XMF.off + (ft * 2048 + ib * 512 + 512) * 2]

                @p.g("vector", reads=["TC"], writes=[f"XMF.{ft}"])
                def _(eng):
                    return eng.tensor_copy(dst.bitcast(BF16),
                                           self.f32(TC, 512)[0:64, :])
            out += [e_sum, recip_e, rebf, bc64, fold_cp, fold_add,
                    final_mul, final]
            return out

        for s in range(nS + 2):
            st = st_thunks(s) if s < nS else []
            pv = pv_thunks(s - 1) if 1 <= s <= nS else []
            epib = epib_thunks(s - 2) if 2 <= s else []
            epia = epia_thunks(s - 1) if 1 <= s <= nS else []
            uev = uevac_thunks(s - 1) if 1 <= s <= nS else []
            units = []
            for i in range(max(len(st), len(pv))):
                if i < len(st):
                    units.append(st[i][0])
                    units.append(st[i][1])
                if i < len(pv):
                    units.append(pv[i])
            ei = 0
            for idx, u in enumerate(units):
                u()
                if ei < len(epib) and idx >= 1 and idx % 2 == 1:
                    epib[ei]()
                    ei += 1
            while ei < len(epib):
                epib[ei]()
                ei += 1
            for t in epia:
                t()
            for t in uev:
                t()
        for b in [PTB[0], PTB[1], USB, C9, NROW, RECNF, RECNB, RECN8, ZA, E8,
                  M8, RECEF, REBF, BCC, TACC, TC]:
            p.free(b)
        self.tap("vws9", VWS9, "raw_bf16", rows=128, cols=144)
        self.tap("qvd", QVD, "raw_bf16", rows=4, cols=8192)
        self.tap("xmf", XMF, "raw_bf16", rows=128, cols=8 * 2048)
        p.free(VTg); p.free(QFg); p.free(KFg); p.free(VWS9); p.free(QVD)
        WFT = self.transp_weight("L0_attn_Wf", 8, 8, "WFT")
        xb2 = p.alloc("RES2", T * D * 4 // 128)
        dst = self.f32(xb2).rearrange("p (t d) -> p t d", t=NT)
        p.dma("sync", lambda eng: eng.dma_start(
            out=dst, in_=self.sp_xb.rearrange("(t p) d -> p t d", p=128)),
            reads=["dram.res"],
            writes=[xb2] + [f"{xb2.name}.{t}" for t in range(NT)], n=1)
        TMPF = p.alloc("TMPF", 1024 * 4)
        BFB = self.load_bbc("bf")
        for t in range(NT):
            pw = self.ps[0] if t % 2 == 0 else self.ps[1]
            ptoks = ["ps0a", "ps0b"] if t % 2 == 0 else ["ps1a", "ps1b"]

            @p.g("tensor", reads=[WFT] + [f"XMF.{k}" for k in range(8)],
                 writes=ptoks)
            def _(eng, t=t, pw=pw):
                last = None
                for c in range(2):
                    for k in range(8):
                        last = eng.matmul(
                            pw[:, c * 512:(c + 1) * 512],
                            self.bf(XMF, 128, k * 2048 + t * 128),
                            self.bf(WFT, 512, k * 1024 + c * 512),
                            start=(k == 0), stop=(k == 7))
                return last

            @p.g("vector", reads=ptoks + [BFB], writes=[TMPF])
            def _(eng, pw=pw):
                return eng.tensor_add(self.f32(TMPF, 1024), pw[:, 0:1024],
                                      self.bbc_view(BFB))

            @p.g("vector", reads=[TMPF, f"{xb2.name}.{t}"],
                 writes=[f"{xb2.name}.{t}"])
            def _(eng, t=t):
                xv = self.f32(xb2, 1024, t * 1024)
                return eng.tensor_add(xv, xv, self.f32(TMPF, 1024))
        p.free(TMPF)
        p.free(BFB)
        p.free(XMF)
        p.free(WFT)
        self.tap("res2", xb2, "tok_f32")
        return self.attn_relu_ffn(xb2)

    def attn_relu_ffn(self, res2):
        p = self.p
        tag = "G2"
        self.ln_stats_plain(res2, tag)
        self.ln_finalize(tag)
        ON = p.alloc("ON.G2", 16 * 1024 * 2)
        for t in range(NT):
            @p.g("scalar", reads=[f"{res2.name}.{t}", f"st.r.{tag}"],
                 writes=[f"ON.G2.{t}"],
                 ptr_reads=[f"st.r.{tag}", f"st.negmr.{tag}"])
            def _(eng, t=t):
                return eng.activation(self.bf(ON, 1024, t * 1024),
                                      self.f32(res2, 1024, t * 1024), AF.Identity,
                                      scale=self.sv(f"r.{tag}", t, 1),
                                      bias=self.sv(f"negmr.{tag}", t, 1))
        ONF = p.alloc("ONF", 8 * 2048 * 2)
        self.transpose_tokmajor(
            ON, 1024, ONF, 8, tag,
            affine=(lambda ft: self.bcol("g2", ft), lambda ft: self.bcol("b2", ft)))
        p.free(ON)
        xc = self.reload_add_half(self.sp_xa, "x1", res2, "x2p", scale=1.0)
        WR1T = self.transp_weight("L0_attn_Wr1", 16, 8, "WR1T", oc_group=4)
        WR2T = self.transp_weight("L0_attn_Wr2", 8, 16, "WR2T", oc_group=2)
        TMPR = p.alloc("TMPR", 1024 * 4)
        BR2B = self.load_bbc("br2")
        MIDs = [p.alloc(f"MID.{i}", 16 * 128 * 2) for i in range(2)]
        for t in range(NT):
            MID = MIDs[t % 2]
            for mg in range(2):
                pm = self.ps[mg]
                ptoks = ["ps0a", "ps0b"] if mg == 0 else ["ps1a", "ps1b"]

                @p.g("tensor", reads=[WR1T] + [f"ONF.{k}" for k in range(8)],
                     writes=ptoks)
                def _(eng, t=t, mg=mg, pm=pm):
                    last = None
                    for ms in range(8):
                        m = mg * 8 + ms
                        for k in range(8):
                            last = eng.matmul(
                                pm[:, ms * 128:ms * 128 + 128],
                                self.bf(WR1T, 128, k * 2048 + m * 128),
                                self.bf(ONF, 128, k * 2048 + t * 128),
                                start=(k == 0 and ms % 4 == 0),
                                stop=(k == 7 and ms % 4 == 3))
                    return last

                @p.g("scalar", reads=ptoks + ["bpack"],
                     writes=[f"MID.{t % 2}.{mg}"])
                def _(eng, t=t, mg=mg, pm=pm, MID=MID):
                    last = None
                    for ms in range(8):
                        m = mg * 8 + ms
                        last = eng.activation(
                            self.bf(MID, 128, m * 128),
                            pm[:, ms * 128:ms * 128 + 128],
                            AF.Relu, bias=self.bcol("br1", m))
                    return last
            pw = self.ps[2]
            wtoks = ["ps2a", "ps2b"]

            @p.g("tensor",
                 reads=[WR2T, f"MID.{t % 2}.0", f"MID.{t % 2}.1"], writes=wtoks)
            def _(eng, t=t, pw=pw, MID=MID):
                last = None
                for c in range(2):
                    for k in range(16):
                        last = eng.matmul(
                            pw[:, c * 512:(c + 1) * 512],
                            self.bf(MID, 128, k * 128),
                            self.bf(WR2T, 512, k * 1024 + c * 512),
                            start=(k == 0), stop=(k == 15))
                return last

            @p.g("vector", reads=wtoks + [BR2B], writes=[TMPR])
            def _(eng, pw=pw):
                return eng.tensor_add(self.f32(TMPR, 1024), pw[:, 0:1024],
                                      self.bbc_view(BR2B))

            @p.g("vector", reads=[TMPR, f"{xc.name}.{t}"],
                 writes=[f"{xc.name}.{t}"])
            def _(eng, t=t):
                xv = self.f32(xc, 1024, t * 1024)
                return eng.tensor_add(xv, xv, self.f32(TMPR, 1024))
        self.tap("x2", xc, "tok_f32")
        p.free(MIDs[0])
        p.free(MIDs[1])
        p.free(TMPR)
        p.free(BR2B)
        p.free(WR1T)
        p.free(WR2T)
        p.free(ONF)
        return xc

    def emit_block(self):
        nc, p, sems = self.nc, self.p, self.sems

        def mk(engine):
            plan = p.ops[engine]

            def run(eng):
                for gi, (waits, emit, inc, n_dma) in enumerate(plan):
                    for (we, wc) in waits:
                        eng.wait_ge(sems[we], wc)
                    if emit is None:
                        continue
                    if n_dma:
                        insts = emit(eng)
                        if not isinstance(insts, (list, tuple)):
                            insts = [insts]
                        assert len(insts) == n_dma, (engine, len(insts), n_dma)
                        for ins in insts:
                            ins.then_inc(sems[engine], 16)
                        p.group_insts[insts[-1].ins.name] = (engine, gi)
                    else:
                        last = emit(eng)
                        assert last is not None, f"group on {engine} returned None"
                        last.then_inc(sems[engine], 1)
                        p.group_insts[last.ins.name] = (engine, gi)
            return run

        with nc.Block() as block:
            block.tensor(mk("tensor"))
            block.vector(mk("vector"))
            block.scalar(mk("scalar"))
            block.sync(mk("sync"))
            block.gpsimd(mk("gpsimd"))
        self.es.close()



def build_kernel(debug_taps=(), stop_after="", detect_races=True, paranoid=False):
    PARANOID[0] = paranoid
    b = Builder(debug_taps, detect_races)
    b.stop_after = stop_after
    b.build()
    b.emit_block()
    return b


def kernel(**inputs):
    x = np.ascontiguousarray(np.asarray(inputs["L0_attn_x"], dtype=np.float32))
    assert x.shape == (8, T, D)
    consts = host_constants()
    bp, bbc, wsblk = host_packs(inputs)
    b = build_kernel()
    common = {}
    for name in INPUT_SPECS:
        common[name] = np.ascontiguousarray(np.asarray(inputs[name], np.float32))
    for name, arr in consts.items():
        common[name] = np.ascontiguousarray(arr)
    common["BPACK"] = np.ascontiguousarray(bp)
    common["BBC"] = np.ascontiguousarray(bbc)
    common["BPACKBF"] = np.ascontiguousarray(wsblk)
    in_maps = [dict(common, x=np.ascontiguousarray(x[c])) for c in range(8)]
    res = run_bass_kernel_spmd(b.nc, in_maps, core_ids=list(range(8)))
    return np.stack([np.asarray(res.results[c]["out"], np.float32)
                     for c in range(8)], axis=0)
